# version 02
# baseline (speedup 1.0000x reference)
import math
import numpy as np
from contextlib import ExitStack

import concourse.bass as bass
import concourse.mybir as mybir
from concourse.bass_utils import run_bass_kernel_spmd

F32 = mybir.dt.float32
BF16 = mybir.dt.bfloat16
U32 = mybir.dt.uint32
I32 = mybir.dt.int32
AF = mybir.ActivationFunctionType
ALU = mybir.AluOpType
AX = mybir.AxisListType

NCORE = 8
DM = 1024
DC = DM // 128
TPC = 4096
SEQ = 2048
NSEQ = 2
NH = 16
HD = 64

L0_SHIFT = 60.0
L2_SHIFT = 16.0


class Sem:
    __slots__ = ("name", "h", "n")

    def __init__(self, name, h):
        self.name, self.h, self.n = name, h, 0


class _APWrap:
    __slots__ = ("_ap",)

    def __init__(self, ap):
        self._ap = ap

    def ap(self):
        return self._ap


class Prog:
    ENGS = ("sync", "tensor", "vector", "scalar", "gpsimd")

    def __init__(self, name="prog"):
        self.nc = bass.Bass("TRN2", target_bir_lowering=False, debug=False)
        self.q = {e: [] for e in self.ENGS}
        self.ctx = None
        self.name = name
        self.cm = None
        self.dm = None
        self._consts = {}
        self.prefix = ""
        self.iomap = None
        self.misc_cache = None

    def sem(self, name):
        h = self.ctx.enter_context(self.nc.semaphore(self.prefix + name))
        return Sem(name, h)

    def sbuf(self, name, shape, dtype):
        return self.ctx.enter_context(
            self.nc.sbuf_tensor(self.prefix + name, list(shape), dtype))

    def scope(self):
        import contextlib

        @contextlib.contextmanager
        def _s():
            outer = self.ctx
            with ExitStack() as inner:
                self.ctx = inner
                try:
                    yield
                finally:
                    self.ctx = outer
        return _s()

    def constf(self, val, dtype=F32):
        key = (dtype, float(val))
        if key in self._consts:
            return self._consts[key]
        t = self.sbuf("c_%d" % len(self._consts), (128, 1), dtype)
        self.op("vector", "memset", t[:, :], float(val))
        self.nc.const_aps.register(dtype, float(val), t[:, :])
        self._consts[key] = t
        return t

    _psn = 0

    def psum(self, shape, dtype=F32):
        Prog._psn += 1
        return self.ctx.enter_context(
            self.nc.psum_tensor("ps%d" % Prog._psn, list(shape), dtype))

    def dram(self, name, shape, dtype, kind=None):
        if self.iomap is not None:
            if name in self.iomap:
                return _APWrap(self.iomap[name])
            kind = None
        nm = self.prefix + name if kind is None else name
        if kind is None:
            t = self.nc.dram_tensor(nm, list(shape), dtype)
        else:
            t = self.nc.dram_tensor(nm, list(shape), dtype, kind=kind)
        if self.iomap is not None:
            self.iomap[name] = t.ap()
            return _APWrap(t.ap())
        return t

    def on(self, eng, f):
        self.q[eng].append(f)

    def _prewait(self, eng):
        cv, dv = self.cm.n, self.dm.n
        ch, dh = self.cm.h, self.dm.h
        def f(E, cv=cv, dv=dv):
            if cv:
                E.wait_ge(ch, cv)
            if dv:
                E.wait_ge(dh, dv)
        self.on(eng, f)

    def op(self, eng, opname, *a, group=False, **kw):
        if not group:
            self._prewait(eng)
        s = self.cm
        s.n += 1
        def f(E, a=a, kw=kw, opname=opname, sh=s.h):
            r = getattr(E, opname)(*a, **kw)
            r.then_inc(sh, 1)
        self.on(eng, f)

    def raw(self, eng, opname, *a, **kw):
        def f(E, a=a, kw=kw, opname=opname):
            getattr(E, opname)(*a, **kw)
        self.on(eng, f)

    def dma(self, eng, out, in_, **kw):
        self._prewait(eng)
        s = self.dm
        s.n += 16
        def f(E, o=out, i=in_, kw=kw, sh=s.h):
            E.dma_start(out=o, in_=i, **kw).then_inc(sh, 16)
        self.on(eng, f)

    def gather(self, out, in_, idx_ap, **kw):
        self._prewait("gpsimd")
        s = self.dm
        s.n += 16
        def f(E, kw=kw, sh=s.h):
            E.indirect_dma_start(out, None, in_,
                                 bass.IndirectOffsetOnAxis(ap=idx_ap, axis=0),
                                 **kw).then_inc(sh, 16)
        self.on(eng := "gpsimd", f)

    def scatter(self, out, in_, idx_ap, **kw):
        self._prewait("gpsimd")
        s = self.dm
        s.n += 16
        def f(E, kw=kw, sh=s.h):
            E.indirect_dma_start(out, bass.IndirectOffsetOnAxis(ap=idx_ap, axis=0),
                                 in_, None, **kw).then_inc(sh, 16)
        self.on("gpsimd", f)

    _ccn = 0

    def collective(self, kind, op, ins, outs):
        Prog._ccn += 1
        s = self.sem("cc%d" % Prog._ccn)
        self._prewait("gpsimd")
        def f(E, s=s, ins=ins, outs=outs, kind=kind, op=op):
            E.collective_compute(kind, op, replica_groups=[list(range(NCORE))],
                                 ins=ins, outs=outs).then_inc(s.h)
        self.on("gpsimd", f)
        for e in self.ENGS:
            def w(E, s=s):
                E.wait_ge(s.h, 1)
            self.on(e, w)

    def barrier_all(self):
        for e in self.ENGS:
            self._prewait(e)

    def build(self, emit_fn):
        with ExitStack() as ctx:
            self.ctx = ctx
            self.cm = self.sem("cm_auto")
            self.dm = self.sem("dm_auto")
            emit_fn(self)
            self._prewait("sync")
            with self.nc.Block() as blk:
                for eng in self.ENGS:
                    fns = self.q[eng]
                    def section(E, fns=fns):
                        for f in fns:
                            f(E)
                    getattr(blk, eng)(section)
        return self.nc


def cdiv(a, b):
    return (a + b - 1) // b


_CONST_CACHE = {}


def host_consts():
    if _CONST_CACHE:
        return _CONST_CACHE
    c = _CONST_CACHE
    inv_freq = 1.0 / (10000.0 ** (np.arange(0, HD, 2)[: HD // 2] / HD))
    t = np.arange(SEQ)
    fr = np.outer(t, inv_freq)
    cos = np.cos(fr).astype(np.float32)
    sin = np.sin(fr).astype(np.float32)
    c64 = np.zeros((HD, SEQ), np.float32)
    s64 = np.zeros((HD, SEQ), np.float32)
    c64[0::2] = cos.T
    c64[1::2] = cos.T
    s64[0::2] = sin.T
    s64[1::2] = sin.T
    c["ropeC"] = np.ascontiguousarray(np.concatenate([c64, c64], 0))
    c["ropeS"] = np.ascontiguousarray(np.concatenate([s64, s64], 0))
    R = np.zeros((HD, HD), np.float32)
    for i in range(HD // 2):
        R[2 * i, 2 * i + 1] = -1.0
        R[2 * i + 1, 2 * i] = 1.0
    Rb = np.zeros((128, 128), np.float32)
    Rb[:64, :64] = R.T
    Rb[64:, 64:] = R.T
    c["ropeRotT"] = np.ascontiguousarray(Rb)
    masks = np.zeros((4, 128, 512), np.float32)
    for j in range(4):
        tk = j * 128 + np.arange(128)[:, None]
        tq = np.arange(512)[None, :]
        masks[j] = (tk <= tq).astype(np.float32)
    c["cmask"] = masks
    c["eye128"] = np.eye(128, dtype=np.float32)
    return c


def emit_ln(p, xt_dram, xnbuf, misc):
    onecol = misc["ones_col_bf"]
    orow = misc["ones_row_bf"]
    xf = p.sbuf("ln_xf", (128, DC * 512), F32)
    xb = p.sbuf("ln_xb", (128, DC * 512), BF16)
    sq = p.sbuf("ln_sq", (128, DC * 512), BF16)
    ps_a = misc["psA"]
    ps_b = misc["psB"]
    ps_c = misc["psC"]
    st_mu = p.sbuf("ln_mu", (1, 512), F32)
    st_r = p.sbuf("ln_r", (1, 512), F32)
    st_v = p.sbuf("ln_v", (1, 512), F32)
    st_m2 = p.sbuf("ln_m2", (1, 512), F32)
    rb = p.sbuf("ln_rb", (128, 512), BF16)
    m2b = p.sbuf("ln_m2b", (128, 512), BF16)

    for tb in range(TPC // 512):
        for kc in range(DC):
            p.dma("sync", xf[:, kc * 512:(kc + 1) * 512],
                  xt_dram[kc * 128:(kc + 1) * 128, tb * 512:(tb + 1) * 512])
        for kc in range(DC):
            p.op("vector", "tensor_copy", xb[:, kc * 512:(kc + 1) * 512],
                 xf[:, kc * 512:(kc + 1) * 512])
        for kc in range(DC):
            p.op("scalar", "activation", sq[:, kc * 512:(kc + 1) * 512],
                 xb[:, kc * 512:(kc + 1) * 512], AF.Square)
        for kc in range(DC):
            p.op("tensor", "matmul", ps_a[0:1, :], onecol[:, :],
                 xb[:, kc * 512:(kc + 1) * 512], start=(kc == 0),
                 stop=(kc == DC - 1), group=(kc > 0))
        for kc in range(DC):
            p.op("tensor", "matmul", ps_b[0:1, :], onecol[:, :],
                 sq[:, kc * 512:(kc + 1) * 512], start=(kc == 0),
                 stop=(kc == DC - 1), group=(kc > 0))
        p.op("vector", "tensor_scalar", st_mu[:, :], ps_a[0:1, :], 1.0 / DM,
             None, op0=ALU.mult)
        p.op("vector", "tensor_scalar", st_v[:, :], ps_b[0:1, :], 1.0 / DM,
             None, op0=ALU.mult)
        p.op("vector", "tensor_tensor", st_m2[:, :], st_mu[:, :], st_mu[:, :],
             op=ALU.mult)
        p.op("vector", "tensor_tensor", st_v[:, :], st_v[:, :], st_m2[:, :],
             op=ALU.subtract)
        p.op("scalar", "activation", st_v[:, :], st_v[:, :], AF.Ln, bias=1e-5)
        p.op("scalar", "activation", st_r[:, :], st_v[:, :], AF.Exp, scale=-0.5)
        p.op("vector", "tensor_tensor", st_m2[:, :], st_mu[:, :], st_r[:, :],
             op=ALU.mult)
        p.op("tensor", "matmul", ps_c[:, :], misc["ones_b128_f"][:, :], st_r[:, :],
             start=True, stop=True)
        p.op("vector", "tensor_copy", rb[:, :], ps_c[:, :])
        p.op("tensor", "matmul", ps_c[:, :], misc["ones_b128_f"][:, :], st_m2[:, :],
             start=True, stop=True)
        p.op("vector", "tensor_copy", m2b[:, :], ps_c[:, :])
        for kc in range(DC):
            dst = xnbuf[:, kc * TPC + tb * 512: kc * TPC + (tb + 1) * 512]
            p.op("vector", "tensor_tensor", dst, xb[:, kc * 512:(kc + 1) * 512],
                 rb[:, :], op=ALU.mult)
            p.op("vector", "tensor_tensor", dst, dst, m2b[:, :],
                 op=ALU.subtract)


def emit_proj(p, w_sb, xnbuf, out_cb, banks, noc=None, ow=None):
    if ow is None:
        ow = w_sb.shape[1] // DC
    if noc is None:
        noc = ow // 128
    for oc in range(noc):
        for tb in range(TPC // 512):
            bank = banks[(oc * (TPC // 512) + tb) % len(banks)]
            for kc in range(DC):
                p.op("tensor", "matmul", bank[:, :],
                     w_sb[:, kc * ow + oc * 128: kc * ow + (oc + 1) * 128],
                     xnbuf[:, kc * TPC + tb * 512: kc * TPC + (tb + 1) * 512],
                     start=(kc == 0), stop=(kc == DC - 1), group=(kc > 0))
            out_cb(oc, tb, bank)


def load_w_bf16(p, dst_sb, src_dram, ow):
    for kc in range(DC):
        p.dma("gpsimd", dst_sb[:, kc * ow:(kc + 1) * ow],
              src_dram[kc * 128:(kc + 1) * 128, :])


def emit_misc(p):
    if p.misc_cache is not None:
        return p.misc_cache
    misc = {}
    oc_bf = p.sbuf("ones_col_bf", (128, 1), BF16)
    orow = p.sbuf("ones_row_bf", (1, 2048), BF16)
    p.op("vector", "memset", oc_bf[:, :], 1.0)
    p.op("vector", "memset", orow[:, :], 1.0)
    ob128 = p.sbuf("ones_b128_f", (1, 128), F32)
    p.op("vector", "memset", ob128[:, :], 1.0)
    misc["ones_col_bf"] = oc_bf
    misc["ones_row_bf"] = orow
    misc["ones_b128_f"] = ob128
    for v in (1e-5, -0.5, -1.0, -60.0, -16.0, 1.0, 0.0, 1e-6, -2.0, 0.5,
              1e-12, 0.125, 0.25, 8.0, -20.0, float(1.0 / 64.0), float(1.0 / DM)):
        p.constf(v)
    misc["psA"] = p.psum((128, 512))
    misc["psB"] = p.psum((128, 512))
    misc["psC"] = p.psum((128, 512))
    misc["psD"] = p.psum((128, 512))
    misc["psE"] = p.psum((128, 512))
    misc["psF"] = p.psum((128, 512))
    misc["psT"] = p.psum((128, 1024), BF16)
    p.misc_cache = misc
    return misc


def build_L0a(p=None):
    standalone = p is None
    if standalone:
        p = Prog("l0a")

    def emit(p):
        xt = p.dram("xt", (DM, TPC), F32, kind="ExternalInput").ap()
        wq = p.dram("wq", (DM, DM), F32, kind="ExternalInput").ap()
        wk = p.dram("wk", (DM, DM), F32, kind="ExternalInput").ap()
        wv = p.dram("wv", (DM, DM), F32, kind="ExternalInput").ap()
        wo = p.dram("wo", (DM, DM), F32, kind="ExternalInput").ap()
        wg = p.dram("wg", (DM, 8), F32, kind="ExternalInput").ap()
        alpha = p.dram("alpha", (1, 1), F32, kind="ExternalInput").ap()
        ropeC = p.dram("ropeC", (128, SEQ), F32, kind="ExternalInput").ap()
        ropeS = p.dram("ropeS", (128, SEQ), F32, kind="ExternalInput").ap()
        rotT = p.dram("rotT", (128, 128), F32, kind="ExternalInput").ap()
        cmask = p.dram("cmask", (4, 128, 512), F32, kind="ExternalInput").ap()

        xn_o = p.dram("xn_o", (DM, TPC), BF16, kind="ExternalOutput").ap()
        s0_o = p.dram("s0_o", (DM, TPC), BF16, kind="ExternalOutput").ap()
        lg_o = p.dram("lg_o", (8, TPC), F32, kind="ExternalOutput").ap()

        qext_s = p.dram("qext_s", (NSEQ * NH, 68, SEQ), BF16).ap()
        k_s = p.dram("k_s", (NSEQ * NH, 64, SEQ), BF16).ap()
        km2_s = p.dram("km2_s", (NSEQ * NH, 68, SEQ), BF16).ap()
        vtok_s = p.dram("vtok_s", (NSEQ * NH, SEQ, 65), BF16).ap()

        misc = emit_misc(p)
        orow = misc["ones_row_bf"]

        xnbuf = p.sbuf("xnbuf", (128, DC * TPC), BF16)
        with p.scope():
            emit_ln(p, xt, xnbuf, misc)
        for kc in range(DC):
            p.dma("scalar", xn_o[kc * 128:(kc + 1) * 128, :],
                  xnbuf[:, kc * TPC:(kc + 1) * TPC])

        w_sb = p.sbuf("w_sb", (128, DC * DM), BF16)
        wg_sb = p.sbuf("wg_sb", (128, DC * 8), BF16)
        load_w_bf16(p, wg_sb, wg, 8)

        rc = p.sbuf("ropeC_sb", (128, SEQ), F32)
        rs = p.sbuf("ropeS_sb", (128, SEQ), F32)
        rt = p.sbuf("rotT_sb", (128, 128), F32)
        cms = p.sbuf("cmask_sb", (128, 4 * 512), BF16)
        p.dma("sync", rc[:, :], ropeC[:, :])
        p.dma("sync", rs[:, :], ropeS[:, :])
        p.dma("sync", rt[:, :], rotT[:, :])
        for j in range(4):
            p.dma("gpsimd", cms[:, j * 512:(j + 1) * 512], cmask[j, :, :])

        if p.iomap is not None:
            eyedl0 = p.dram("eyedl0", (128, 128), F32, kind="ExternalInput").ap()
            eye = p.sbuf("eye_l0a", (128, 128), BF16)
            eyef0 = p.sbuf("eyef_l0a", (128, 128), F32)
            p.dma("gpsimd", eye[:, :], eyedl0[:, :])
            p.dma("sync", eyef0[:, :], eyedl0[:, :])
            misc["eyef_l0a"] = eyef0
        al = p.sbuf("alpha_sb", (1, 1), F32)
        lnsc = p.sbuf("lnsc", (1, 1), F32)
        lnsc_b = p.sbuf("lnsc_b", (128, 1), F32)
        p.dma("sync", al[:, :], alpha[:, :])
        base = math.sqrt(HD) / math.log(1.0 + HD)
        p.op("scalar", "mul", lnsc[:, :], al[:, :], math.log(base))
        p.op("tensor", "matmul", misc["psC"][:, 0:1], misc["ones_b128_f"][:, :],
             lnsc[:, :], start=True, stop=True)
        p.op("vector", "tensor_copy", lnsc_b[:, :], misc["psC"][:, 0:1])

        qf = p.sbuf("qf", (128, 512), F32)
        t1 = p.sbuf("t1", (128, 512), F32)
        qrp = p.sbuf("qrp", (128, 512), F32)
        km2t = p.sbuf("km2t", (128, 512), F32)
        sqv = p.sbuf("sqv", (128, 512), BF16)
        nstage = p.sbuf("nstage", (2, 512), F32)
        nhib = p.sbuf("nhib", (2, 512), BF16)
        nhif = p.sbuf("nhif", (2, 512), F32)
        nlo = p.sbuf("nlo", (2, 512), F32)
        onesb2 = p.sbuf("onesb2", (128, 2), BF16)
        p.op("vector", "memset", onesb2[:, :], 0.0)
        p.op("vector", "memset", onesb2[0:64, 0:1], 1.0)
        p.op("vector", "memset", onesb2[64:128, 1:2], 1.0)
        ps_rot = misc["psD"]
        ps_n = misc["psE"]
        pbanks = [misc["psA"], misc["psB"]]

        def cb_qk(which):
            def cb(oc, tb, bank):
                b = tb // 4
                tloc = (tb % 4) * 512
                p.op("vector", "tensor_copy", qf[:, :], bank[:, :])
                p.op("tensor", "matmul", ps_rot[:, :], rt[:, :], qf[:, :],
                     start=True, stop=True)
                p.op("vector", "tensor_tensor", t1[:, :], qf[:, :],
                     rc[:, tloc:tloc + 512], op=ALU.mult)
                p.op("vector", "tensor_tensor", qrp[:, :], ps_rot[:, :],
                     rs[:, tloc:tloc + 512], op=ALU.mult)
                p.op("vector", "tensor_tensor", qrp[:, :], qrp[:, :], t1[:, :],
                     op=ALU.add)
                p.op("scalar", "activation", sqv[:, :], qrp[:, :], AF.Square)
                p.op("tensor", "matmul", ps_n[0:2, :], onesb2[:, :], sqv[:, :],
                     start=True, stop=True)
                bh0 = b * 16 + 2 * oc
                bh1 = bh0 + 1
                if which == "q":
                    p.op("vector", "tensor_scalar", nstage[:, :], ps_n[0:2, :],
                         1e-5, None, op0=ALU.add)
                else:
                    p.op("vector", "tensor_copy", nstage[:, :], ps_n[0:2, :])
                p.op("vector", "tensor_copy", nhib[:, :], nstage[:, :])
                p.op("vector", "tensor_copy", nhif[:, :], nhib[:, :])
                p.op("vector", "tensor_tensor", nlo[:, :], nstage[:, :],
                     nhif[:, :], op=ALU.subtract)
                tgt = qext_s if which == "q" else km2_s
                rhi = 66 if which == "q" else 64
                for par, bh in ((0, bh0), (1, bh1)):
                    p.dma("gpsimd", tgt[bh, rhi:rhi + 1, tloc:tloc + 512],
                          nhib[par:par + 1, :])
                    p.dma("gpsimd", tgt[bh, rhi + 1:rhi + 2, tloc:tloc + 512],
                          nlo[par:par + 1, :])
                if which == "q":
                    p.dma("gpsimd", qext_s[bh0, 0:64, tloc:tloc + 512], qrp[0:64, :])
                    p.dma("gpsimd", qext_s[bh1, 0:64, tloc:tloc + 512], qrp[64:128, :])
                else:
                    p.dma("gpsimd", k_s[bh0, :, tloc:tloc + 512], qrp[0:64, :])
                    p.dma("gpsimd", k_s[bh1, :, tloc:tloc + 512], qrp[64:128, :])
                    p.op("vector", "tensor_scalar", km2t[:, :], qrp[:, :], -2.0,
                         None, op0=ALU.mult)
                    p.dma("gpsimd", km2_s[bh0, 0:64, tloc:tloc + 512], km2t[0:64, :])
                    p.dma("gpsimd", km2_s[bh1, 0:64, tloc:tloc + 512], km2t[64:128, :])
            return cb

        load_w_bf16(p, w_sb, wq, DM)
        emit_proj(p, w_sb, xnbuf, cb_qk("q"), pbanks)
        load_w_bf16(p, w_sb, wk, DM)
        emit_proj(p, w_sb, xnbuf, cb_qk("k"), pbanks)

        eye = p.sbuf("eye_sb", (128, 128), BF16)
        eyed = p.dram("eye_d", (128, 128), F32, kind="ExternalInput").ap()
        p.dma("gpsimd", eye[:, :], eyed[:, :])
        vtstage = p.sbuf("vtstage", (128, 65), BF16)
        vbf = p.sbuf("vbf", (128, 512), BF16)
        p.op("vector", "memset", vtstage[:, :], 1.0)
        ps_tr = misc["psT"]

        def cb_v(oc, tb, bank):
            b = tb // 4
            tloc = (tb % 4) * 512
            p.op("vector", "tensor_copy", vbf[:, :], bank[:, :])
            for hh in range(2):
                bh = b * 16 + 2 * oc + hh
                for j in range(4):
                    p.op("tensor", "transpose", ps_tr[:, 0:64],
                         vbf[hh * 64:(hh + 1) * 64, j * 128:(j + 1) * 128],
                         eye[hh * 64:(hh + 1) * 64, hh * 64:(hh + 1) * 64])
                    p.op("vector", "tensor_copy", vtstage[:, 0:64], ps_tr[:, 0:64])
                    p.dma("gpsimd",
                          vtok_s[bh, tloc + j * 128: tloc + (j + 1) * 128, :],
                          vtstage[:, :])
        load_w_bf16(p, w_sb, wv, DM)
        emit_proj(p, w_sb, xnbuf, cb_v, pbanks)
        load_w_bf16(p, w_sb, wo, DM)

        ones_bf_row = p.sbuf("ones_bf_row", (1, 2048), BF16)
        p.op("vector", "memset", ones_bf_row[:, :], 1.0)
        for bh in range(NSEQ * NH):
            p.dma("gpsimd", qext_s[bh, 64:65, :], ones_bf_row[:, :])
            p.dma("gpsimd", qext_s[bh, 65:66, :], ones_bf_row[:, :])
            p.dma("gpsimd", km2_s[bh, 66:67, :], ones_bf_row[:, :])
            p.dma("gpsimd", km2_s[bh, 67:68, :], ones_bf_row[:, :])

        qe_t = p.sbuf("qe_t", (68, SEQ), BF16)
        k_t = p.sbuf("k_t", (64, SEQ), BF16)
        km2_t = p.sbuf("km2_t", (68, SEQ), BF16)
        vt_t = p.sbuf("vt_t", (128, 16 * 65), BF16)
        ps_m1 = misc["psA"]
        ps_m2 = misc["psB"]
        ps_pv = misc["psC"]
        ps_rb = misc["psD"]
        sq_f = p.sbuf("sq_f", (128, 512), F32)
        lnm2 = p.sbuf("lnm2", (128, 512), F32)
        rcp = p.sbuf("rcp", (128, 512), F32)
        parg = p.sbuf("parg", (128, 512), F32)
        ptile = p.sbuf("ptile", (128, 512), BF16)
        zrow = p.sbuf("zrow", (1, 512), F32)
        zr = p.sbuf("zr", (1, 512), F32)
        zrb = p.sbuf("zrb", (128, 512), BF16)
        otile = p.sbuf("otile", (128, 512), BF16)
        obuf = p.sbuf("obuf", (128, DC * SEQ), BF16)
        sbf = p.sbuf("sbf", (128, 512), BF16)

        for b in range(NSEQ):
            for h in range(NH):
                bh = b * 16 + h
                p.dma("sync", qe_t[:, :], qext_s[bh, :, :])
                p.dma("sync", k_t[:, :], k_s[bh, :, :])
                p.dma("sync", km2_t[:, :], km2_s[bh, :, :])
                for j in range(16):
                    p.dma("scalar", vt_t[:, j * 65:(j + 1) * 65],
                          vtok_s[bh, j * 128:(j + 1) * 128, :])
                for tq in range(4):
                    ntk = (tq + 1) * 4
                    for tkj in range(ntk):
                        diag = (tkj >= tq * 4)
                        p.op("tensor", "matmul", ps_m1[:, :],
                             k_t[:, tkj * 128:(tkj + 1) * 128],
                             qe_t[0:64, tq * 512:(tq + 1) * 512],
                             start=True, stop=True)
                        p.op("tensor", "matmul", ps_m2[:, :],
                             km2_t[:, tkj * 128:(tkj + 1) * 128],
                             qe_t[:, tq * 512:(tq + 1) * 512],
                             start=True, stop=True)
                        p.op("scalar", "activation", lnm2[:, :], ps_m2[:, :],
                             AF.Ln)
                        p.op("scalar", "activation", sq_f[:, :], ps_m1[:, :],
                             AF.Square)
                        p.op("scalar", "activation", rcp[:, :], lnm2[:, :],
                             AF.Exp, scale=-1.0, bias=lnsc_b[:, 0:1])
                        p.op("vector", "tensor_tensor", parg[:, :], sq_f[:, :],
                             rcp[:, :], op=ALU.mult)
                        p.op("scalar", "activation", ptile[:, :], parg[:, :],
                             AF.Exp, bias=-L0_SHIFT)
                        if diag:
                            j = tkj - tq * 4
                            p.op("vector", "tensor_tensor", ptile[:, :],
                                 ptile[:, :], cms[:, j * 512:(j + 1) * 512],
                                 op=ALU.mult)
                        p.op("tensor", "matmul", ps_pv[0:65, :],
                             vt_t[:, tkj * 65:(tkj + 1) * 65], ptile[:, :],
                             start=(tkj == 0), stop=(tkj == ntk - 1))
                    p.op("vector", "tensor_copy", zrow[:, :], ps_pv[64:65, :])
                    p.op("vector", "reciprocal", zr[:, :], zrow[:, :])
                    p.op("tensor", "matmul", ps_rb[:, :], misc["ones_b128_f"][:, :],
                         zr[:, :], start=True, stop=True)
                    p.op("vector", "tensor_copy", zrb[:, :], ps_rb[:, :])
                    p.op("vector", "tensor_copy", otile[:, :], ps_pv[:, :])
                    p.op("vector", "tensor_tensor", otile[:, :], otile[:, :],
                         zrb[:, :], op=ALU.mult)
                    hp, par = h // 2, h % 2
                    p.op("vector", "tensor_copy",
                         obuf[par * 64:(par + 1) * 64,
                              hp * SEQ + tq * 512: hp * SEQ + (tq + 1) * 512],
                         otile[0:64, :])
            for oc in range(DC):
                for tb in range(4):
                    bank = [misc["psE"], misc["psF"]][tb % 2]
                    for kc in range(DC):
                        p.op("tensor", "matmul", bank[:, :],
                             w_sb[:, kc * DM + oc * 128: kc * DM + (oc + 1) * 128],
                             obuf[:, kc * SEQ + tb * 512: kc * SEQ + (tb + 1) * 512],
                             start=(kc == 0), stop=(kc == DC - 1), group=(kc > 0))
                    p.op("vector", "tensor_copy", sbf[:, :], bank[:, :])
                    p.dma("gpsimd",
                          s0_o[oc * 128:(oc + 1) * 128,
                               b * 2048 + tb * 512: b * 2048 + (tb + 1) * 512],
                          sbf[:, :])

        lgf = p.sbuf("lgf", (8, 512), F32)
        lgtr = p.sbuf("lgtr", (128, 8), F32)
        if p.iomap is not None:
            lgtok = p.dram("lgtok", (TPC, 8), F32).ap()
            lgfull = p.dram("lgfull", (NCORE * TPC, 8), F32).ap()
        for tb in range(8):
            for kc in range(DC):
                p.op("tensor", "matmul", misc["psA"][0:8, :],
                     wg_sb[:, kc * 8:(kc + 1) * 8],
                     xnbuf[:, kc * TPC + tb * 512: kc * TPC + (tb + 1) * 512],
                     start=(kc == 0), stop=(kc == DC - 1), group=(kc > 0))
            p.op("vector", "tensor_copy", lgf[:, :], misc["psA"][0:8, :])
            p.dma("sync", lg_o[:, tb * 512:(tb + 1) * 512], lgf[:, :])
            if p.iomap is not None:
                eyef0 = misc["eyef_l0a"]
                for j in range(4):
                    p.op("tensor", "transpose", misc["psB"][0:128, 0:8],
                         lgf[:, j * 128:(j + 1) * 128], eyef0[0:8, 0:8])
                    p.op("vector", "tensor_copy", lgtr[:, :],
                         misc["psB"][0:128, 0:8])
                    p.dma("gpsimd",
                          lgtok[tb * 512 + j * 128: tb * 512 + (j + 1) * 128, :],
                          lgtr[:, :])
        if p.iomap is not None:
            xntok = p.dram("xntok", (TPC, DM), BF16).ap()
            xtr = p.sbuf("xtr", (128, 128), BF16)
            for k in range(32):
                for kc in range(DC):
                    p.op("tensor", "transpose", misc["psT"][:, 0:128],
                         xnbuf[:, kc * TPC + k * 128: kc * TPC + (k + 1) * 128],
                         eye[:, :])
                    p.op("vector", "tensor_copy", xtr[:, :], misc["psT"][:, 0:128])
                    p.dma("gpsimd",
                          xntok[k * 128:(k + 1) * 128, kc * 128:(kc + 1) * 128],
                          xtr[:, :])
            p.collective("AllGather", ALU.bypass,
                         [lgtok.tensor.ap().opt()], [lgfull.tensor.ap().opt()])

    if standalone:
        p.build(emit)
    else:
        emit(p)
    return p


def l0a_inmaps(inputs, shards):
    c = host_consts()
    base = {
        "wq": inputs["L0_attn_Wq"], "wk": inputs["L0_attn_Wk"],
        "wv": inputs["L0_attn_Wv"], "wo": inputs["L0_attn_Wo"],
        "wg": inputs["L0_ffn_wg"],
        "alpha": np.ascontiguousarray(inputs["L0_attn_alpha"].reshape(1, 1)),
        "ropeC": c["ropeC"], "ropeS": c["ropeS"], "rotT": c["ropeRotT"],
        "cmask": c["cmask"], "eye_d": c["eye128"],
    }
    base = {k: np.ascontiguousarray(v.astype(np.float32)) for k, v in base.items()}
    maps = []
    for ci in range(NCORE):
        m = dict(base)
        m["xt"] = shards[ci]
        maps.append(m)
    return maps


def shard_x(x):
    out = []
    for ci in range(NCORE):
        xs = x[ci * NSEQ:(ci + 1) * NSEQ].reshape(NSEQ * SEQ, DM)
        out.append(np.ascontiguousarray(xs.T.astype(np.float32)))
    return out


def kernel(**inputs):
    raise NotImplementedError("assembled incrementally; see scratch tests")


CAPL0 = 1280
TRASH0 = 8 * CAPL0
CAP0 = 8192


def build_L0b1(p=None):
    standalone = p is None
    if standalone:
        p = Prog("l0b1")

    def emit(p):
        lg = p.dram("lg", (256 * 128, 8), F32, kind="ExternalInput").ap()
        iota8 = p.dram("iota8", (128, 8), F32, kind="ExternalInput").ap()
        tril = p.dram("tril", (128, 128), F32, kind="ExternalInput").ap()
        slots_o = p.dram("slots_o", (128, 512), U32, kind="ExternalOutput").ap()
        wk_o = p.dram("wk_o", (128, 512), F32, kind="ExternalOutput").ap()

        misc = emit_misc(p)
        psA, psB, psC, psD = misc["psA"], misc["psB"], misc["psC"], misc["psD"]
        onef = misc["ones_b128_f"]
        onecf = p.constf(1.0)

        io8 = p.sbuf("io8", (128, 8), F32)
        trl = p.sbuf("trl", (128, 128), F32)
        p.dma("sync", io8[:, :], iota8[:, :])
        p.dma("sync", trl[:, :], tril[:, :])

        lgt = p.sbuf("lgt", (128, 8), F32)
        v8 = p.sbuf("v8", (128, 8), F32)
        i8 = p.sbuf("i8", (128, 8), U32)
        i8f = p.sbuf("i8f", (128, 8), F32)
        oh = p.sbuf("oh", (128, 8), F32)
        ohk = p.sbuf("ohk", (128, 8), F32)
        locf = p.sbuf("locf", (128, 8), F32)
        junk8 = p.sbuf("junk8", (128, 8), F32)
        colsum = p.sbuf("colsum", (1, 8), F32)
        offrow = p.sbuf("offrow", (1, 8), F32)
        loffrow = p.sbuf("loffrow", (1, 8), F32)
        loffbuf = p.sbuf("loffbuf", (1, 64), F32)
        lsum = p.sbuf("lsum", (1, 8), F32)
        e1 = p.sbuf("e1", (128, 1), F32)
        w0c = p.sbuf("w0c", (128, 1), F32)
        w1c = p.sbuf("w1c", (128, 1), F32)
        posc = p.sbuf("posc", (128, 1), F32)
        keepc = p.sbuf("keepc", (128, 1), F32)
        keeplc = p.sbuf("keeplc", (128, 1), F32)
        lposc = p.sbuf("lposc", (128, 1), F32)
        slotf = p.sbuf("slotf", (128, 1), F32)
        tmpc = p.sbuf("tmpc", (128, 1), F32)
        slots_sb = p.sbuf("slots_sb", (128, 512), U32)
        wk_sb = p.sbuf("wk_sb2", (128, 512), F32)

        p.op("vector", "memset", offrow[:, :], 0.0)

        for kk in range(2):
            for ti in range(256):
                seg = ti // 32
                if ti % 32 == 0:
                    if kk == 0:
                        p.op("vector", "memset", loffrow[:, :], 0.0)
                    else:
                        p.op("vector", "tensor_copy", loffrow[:, :],
                             loffbuf[:, seg * 8:(seg + 1) * 8])
                p.dma("sync", lgt[:, :], lg[ti * 128:(ti + 1) * 128, :])
                p.op("vector", "max", v8[:, :], lgt[:, :])
                p.op("vector", "max_index", i8[:, :], v8[:, :], lgt[:, :])
                p.op("vector", "tensor_copy", i8f[:, :], i8[:, :])
                p.op("vector", "tensor_scalar", ohk[:, :], io8[:, :],
                     i8f[:, kk:kk + 1], None, op0=ALU.is_equal)
                p.op("tensor", "matmul", psB[0:1, 0:8], onecf[:, :], ohk[:, :],
                     start=True, stop=True)
                p.op("vector", "tensor_copy", colsum[:, :], psB[0:1, 0:8])
                p.op("tensor", "matmul", psA[0:128, 0:8], trl[:, :],
                     ohk[:, :], start=True, stop=False)
                p.op("tensor", "matmul", psA[0:128, 0:8], onef[:, :],
                     offrow[:, :], start=False, stop=True, group=True)
                p.op("vector", "tensor_copy", locf[:, :], psA[0:128, 0:8])
                p.op("vector", "scalar_tensor_tensor", junk8[:, :],
                     locf[:, :], 1.0, ohk[:, :], op0=ALU.mult,
                     op1=ALU.mult, accum_out=posc[:, 0:1])
                p.op("vector", "tensor_scalar", posc[:, :], posc[:, :],
                     -1.0, None, op0=ALU.add)
                p.op("vector", "tensor_scalar", keepc[:, :], posc[:, :],
                     float(CAP0), None, op0=ALU.is_lt)
                p.op("vector", "tensor_scalar", oh[:, :], ohk[:, :],
                     keepc[:, 0:1], None, op0=ALU.mult)
                p.op("tensor", "matmul", psC[0:128, 0:8], trl[:, :],
                     oh[:, :], start=True, stop=False)
                p.op("tensor", "matmul", psC[0:128, 0:8], onef[:, :],
                     loffrow[:, :], start=False, stop=True, group=True)
                p.op("vector", "tensor_copy", locf[:, :], psC[0:128, 0:8])
                p.op("vector", "scalar_tensor_tensor", junk8[:, :],
                     locf[:, :], 1.0, oh[:, :], op0=ALU.mult,
                     op1=ALU.mult, accum_out=lposc[:, 0:1])
                p.op("vector", "tensor_scalar", lposc[:, :], lposc[:, :],
                     -1.0, None, op0=ALU.add)
                p.op("vector", "tensor_scalar", keeplc[:, :], lposc[:, :],
                     float(CAPL0), None, op0=ALU.is_lt)
                p.op("vector", "tensor_tensor", keepc[:, :], keepc[:, :],
                     keeplc[:, :], op=ALU.mult)
                p.op("vector", "tensor_scalar", slotf[:, :],
                     i8f[:, kk:kk + 1], float(CAPL0), lposc[:, 0:1],
                     op0=ALU.mult, op1=ALU.add)
                p.op("vector", "tensor_scalar", tmpc[:, :], keepc[:, :],
                     float(-TRASH0), float(TRASH0), op0=ALU.mult, op1=ALU.add)
                p.op("vector", "tensor_tensor", slotf[:, :], slotf[:, :],
                     tmpc[:, :], op=ALU.add)
                p.op("vector", "tensor_scalar", slotf[:, :], slotf[:, :],
                     float(TRASH0), None, op0=ALU.min)
                p.op("vector", "tensor_scalar", slotf[:, :], slotf[:, :],
                     0.0, None, op0=ALU.max)
                p.op("vector", "tensor_copy",
                     slots_sb[:, kk * 256 + ti: kk * 256 + ti + 1], slotf[:, :])
                p.op("vector", "tensor_copy",
                     wk_sb[:, kk * 256 + ti: kk * 256 + ti + 1], keepc[:, :])
                p.op("tensor", "matmul", psD[0:1, 0:8], onecf[:, :],
                     oh[:, :], start=True, stop=True)
                p.op("vector", "tensor_copy", lsum[:, :], psD[0:1, 0:8])
                p.op("vector", "tensor_tensor", loffrow[:, :],
                     loffrow[:, :], lsum[:, :], op=ALU.add)
                p.op("vector", "tensor_tensor", offrow[:, :], offrow[:, :],
                     colsum[:, :], op=ALU.add)
                if ti % 32 == 31 and kk == 0:
                    p.op("vector", "tensor_copy",
                         loffbuf[:, seg * 8:(seg + 1) * 8], loffrow[:, :])

        for ti in range(256):
            p.dma("sync", lgt[:, :], lg[ti * 128:(ti + 1) * 128, :])
            p.op("vector", "max", v8[:, :], lgt[:, :])
            p.op("vector", "tensor_tensor", e1[:, 0:1], v8[:, 1:2],
                 v8[:, 0:1], op=ALU.subtract)
            p.op("scalar", "activation", e1[:, :], e1[:, :], AF.Exp)
            p.op("vector", "tensor_scalar", tmpc[:, :], e1[:, :], 1.0, None,
                 op0=ALU.add)
            p.op("vector", "reciprocal", w0c[:, :], tmpc[:, :])
            p.op("vector", "tensor_tensor", w1c[:, :], e1[:, :], w0c[:, :],
                 op=ALU.mult)
            p.op("vector", "tensor_tensor", w0c[:, :], w0c[:, :],
                 wk_sb[:, ti:ti + 1], op=ALU.mult)
            p.op("vector", "tensor_tensor", w1c[:, :], w1c[:, :],
                 wk_sb[:, 256 + ti:256 + ti + 1], op=ALU.mult)
            p.op("vector", "tensor_copy", wk_sb[:, ti:ti + 1], w0c[:, :])
            p.op("vector", "tensor_copy", wk_sb[:, 256 + ti:256 + ti + 1],
                 w1c[:, :])

        p.dma("gpsimd", slots_o[:, :], slots_sb[:, :])
        p.dma("gpsimd", wk_o[:, :], wk_sb[:, :])
        if p.iomap is not None:
            eyedb1 = p.dram("eyedb1", (128, 128), F32, kind="ExternalInput").ap()
            eyefb1 = p.sbuf("eyefb1", (128, 128), F32)
            p.dma("sync", eyefb1[:, :], eyedb1[:, :])
            slotsT = p.dram("slotsT_s", (512, 128), F32).ap()
            wkT = p.dram("wkT_s", (512, 128), F32).ap()
            slf = p.sbuf("slf", (128, 128), F32)
            for j in range(4):
                p.op("vector", "tensor_copy", slf[:, :],
                     slots_sb[:, j * 128:(j + 1) * 128])
                p.op("tensor", "transpose", psA[:, 0:128], slf[:, :],
                     eyefb1[:, :])
                p.op("vector", "tensor_copy", slf[:, :], psA[:, 0:128])
                p.dma("gpsimd", slotsT[j * 128:(j + 1) * 128, :], slf[:, :])
                p.op("tensor", "transpose", psA[:, 0:128],
                     wk_sb[:, j * 128:(j + 1) * 128], eyefb1[:, :])
                p.op("vector", "tensor_copy", slf[:, :], psA[:, 0:128])
                p.dma("gpsimd", wkT[j * 128:(j + 1) * 128, :], slf[:, :])

    if standalone:
        p.build(emit)
    else:
        emit(p)
    return p


def build_L0b2(p=None):
    standalone = p is None
    if standalone:
        p = Prog("l0b2")

    def emit(p):
        myslots = p.dram("myslots", (128, 64), U32, kind="ExternalInput").ap()
        myw = p.dram("myw", (128, 64), F32, kind="ExternalInput").ap()
        xn_tok = p.dram("xn_tok", (TPC, DM), BF16, kind="ExternalInput").ap()
        xt = p.dram("xt", (DM, TPC), F32, kind="ExternalInput").ap()
        s0 = p.dram("s0", (DM, TPC), BF16, kind="ExternalInput").ap()
        w1 = p.dram("w1", (8, DM, 4096), F32, kind="ExternalInput").ap()
        w2 = p.dram("w2", (8, 4096, DM), F32, kind="ExternalInput").ap()
        b1t = p.dram("b1t", (128, 8 * 32), F32, kind="ExternalInput").ap()
        b2t = p.dram("b2t", (128, 8 * 8), F32, kind="ExternalInput").ap()
        eyed = p.dram("eye_d2", (128, 128), F32, kind="ExternalInput").ap()

        x1_o = p.dram("x1_o", (DM, TPC), F32, kind="ExternalOutput").ap()

        disp = p.dram("disp", (TRASH0 + 1, DM), BF16).ap()
        y_s = p.dram("y_s", (TRASH0 + 1, DM), BF16).ap()

        misc = emit_misc(p)
        psA = misc["psA"]
        eye = p.sbuf("eye_sb2", (128, 128), BF16)
        p.dma("gpsimd", eye[:, :], eyed[:, :])
        slots_sb = p.sbuf("slots_sb", (128, 64), U32)
        wk_sb = p.sbuf("wk_sb2", (128, 64), F32)
        if p.iomap is None:
            p.dma("sync", slots_sb[:, :], myslots[:, :])
            p.dma("sync", wk_sb[:, :], myw[:, :])
        else:
            selidx = p.dram("selidx", (64, 1), U32, kind="ExternalInput").ap()
            slotsT = p.dram("slotsT_s", (512, 128), F32).ap()
            wkT = p.dram("wkT_s", (512, 128), F32).ap()
            eyedb2f = p.sbuf("eyedb2f", (128, 128), F32)
            p.dma("sync", eyedb2f[:, :], eyed[:, :])
            sel_sb = p.sbuf("sel_sb", (64, 1), U32)
            g64 = p.sbuf("g64", (64, 128), F32)
            gT = p.sbuf("gT", (128, 64), F32)
            p.dma("sync", sel_sb[:, :], selidx[:, :])
            p.gather(g64[:, :], slotsT[:, :], sel_sb[:, 0:1])
            p.op("tensor", "transpose", psA[0:128, 0:64], g64[:, :],
                 eyedb2f[0:64, 0:64])
            p.op("vector", "tensor_copy", gT[:, :], psA[0:128, 0:64])
            p.op("vector", "tensor_copy", slots_sb[:, :], gT[:, :])
            p.gather(g64[:, :], wkT[:, :], sel_sb[:, 0:1])
            p.op("tensor", "transpose", psA[0:128, 0:64], g64[:, :],
                 eyedb2f[0:64, 0:64])
            p.op("vector", "tensor_copy", wk_sb[:, :], psA[0:128, 0:64])

        xtok = p.sbuf("xtok", (128, DM), BF16)
        zrow_b = p.sbuf("zrow_b", (1, DM), BF16)
        p.op("vector", "memset", zrow_b[:, :], 0.0)
        p.dma("gpsimd", y_s[TRASH0:TRASH0 + 1, :], zrow_b[:, :])
        for k in range(32):
            p.dma("sync", xtok[:, :], xn_tok[k * 128:(k + 1) * 128, :])
            for kk in range(2):
                p.scatter(disp[:, :], xtok[:, :],
                          slots_sb[:, kk * 32 + k: kk * 32 + k + 1])

        NCH = cdiv(CAPL0, 512)

        def nslice(nb):
            n0 = nb * 512
            return n0, min(CAPL0, n0 + 512)

        with p.scope():
            dT = p.sbuf("dT", (128, DC * CAPL0), BF16)
            hT = p.sbuf("hT", (128, 32 * CAPL0), BF16)
            wfull = p.sbuf("wfull", (128, 32768), BF16)
            b1sb = p.sbuf("b1sb", (128, 8 * 32), F32)
            b2sb = p.sbuf("b2sb", (128, 8 * 8), F32)
            ybf = p.sbuf("ybf", (128, 512), BF16)
            ytr = p.sbuf("ytr", (128, 128), BF16)
            psT = misc["psT"]
            p.dma("sync", b1sb[:, :], b1t[:, :])
            p.dma("sync", b2sb[:, :], b2t[:, :])
            for e in range(8):
                for kc in range(DC):
                    p.dma("sync", dT[:, kc * CAPL0:(kc + 1) * CAPL0],
                          disp[e * CAPL0:(e + 1) * CAPL0,
                               kc * 128:(kc + 1) * 128], transpose=True)
                for kc in range(DC):
                    p.dma("gpsimd", wfull[:, kc * 4096:(kc + 1) * 4096],
                          w1[e, kc * 128:(kc + 1) * 128, :])
                for oc in range(32):
                    for nb in range(NCH):
                        n0, n1 = nslice(nb)
                        for kc in range(DC):
                            p.op("tensor", "matmul", psA[:, 0:n1 - n0],
                                 wfull[:, kc * 4096 + oc * 128:
                                       kc * 4096 + (oc + 1) * 128],
                                 dT[:, kc * CAPL0 + n0: kc * CAPL0 + n1],
                                 start=(kc == 0), stop=(kc == DC - 1),
                                 group=(kc > 0))
                        p.op("scalar", "activation",
                             hT[:, oc * CAPL0 + n0: oc * CAPL0 + n1],
                             psA[:, 0:n1 - n0], AF.Gelu,
                             bias=b1sb[:, e * 32 + oc: e * 32 + oc + 1])
                for hc in range(32):
                    p.dma("gpsimd", wfull[:, hc * 1024:(hc + 1) * 1024],
                          w2[e, hc * 128:(hc + 1) * 128, :])
                for oc in range(DC):
                    for nb in range(NCH):
                        n0, n1 = nslice(nb)
                        for hc in range(32):
                            p.op("tensor", "matmul", psA[:, 0:n1 - n0],
                                 wfull[:, hc * 1024 + oc * 128:
                                       hc * 1024 + (oc + 1) * 128],
                                 hT[:, hc * CAPL0 + n0: hc * CAPL0 + n1],
                                 start=(hc == 0), stop=(hc == 31),
                                 group=(hc > 0))
                        p.op("scalar", "activation", ybf[:, 0:n1 - n0],
                             psA[:, 0:n1 - n0], AF.Identity,
                             bias=b2sb[:, e * 8 + oc: e * 8 + oc + 1])
                        for j in range(cdiv(n1 - n0, 128)):
                            p.op("tensor", "transpose", psT[:, 0:128],
                                 ybf[:, j * 128:(j + 1) * 128], eye[:, :])
                            p.op("vector", "tensor_copy", ytr[:, :],
                                 psT[:, 0:128])
                            p.dma("gpsimd",
                                  y_s[e * CAPL0 + n0 + j * 128:
                                      e * CAPL0 + n0 + (j + 1) * 128,
                                      oc * 128:(oc + 1) * 128], ytr[:, :])

        with p.scope():
            y0g = p.sbuf("y0g", (128, DM), BF16)
            y1g = p.sbuf("y1g", (128, DM), BF16)
            moe = p.sbuf("moe", (128, DM), F32)
            gsil = p.sbuf("gsil", (128, DM), BF16)
            gtr = p.sbuf("gtr", (128, 128), BF16)
            xf1 = p.sbuf("xf1", (128, 128), F32)
            sf1 = p.sbuf("sf1", (128, 128), BF16)
            psT = misc["psT"]
            p.op("vector", "memset", y0g[:, :], 0.0)
            p.op("vector", "memset", y1g[:, :], 0.0)
            for k in range(32):
                p.gather(y0g[:, :], y_s[:, :], slots_sb[:, k:k + 1])
                p.gather(y1g[:, :], y_s[:, :], slots_sb[:, 32 + k:32 + k + 1])
                p.op("vector", "tensor_scalar", moe[:, :], y0g[:, :],
                     wk_sb[:, k:k + 1], None, op0=ALU.mult)
                p.op("vector", "scalar_tensor_tensor", moe[:, :], y1g[:, :],
                     wk_sb[:, 32 + k:32 + k + 1], moe[:, :], op0=ALU.mult,
                     op1=ALU.add)
                p.op("scalar", "activation", gsil[:, :], moe[:, :], AF.Silu)
                for kc in range(DC):
                    p.op("tensor", "transpose", psT[:, 0:128],
                         gsil[:, kc * 128:(kc + 1) * 128], eye[:, :])
                    p.op("vector", "tensor_copy", gtr[:, :], psT[:, 0:128])
                    p.dma("sync", xf1[:, :],
                          xt[kc * 128:(kc + 1) * 128, k * 128:(k + 1) * 128])
                    p.dma("sync", sf1[:, :],
                          s0[kc * 128:(kc + 1) * 128, k * 128:(k + 1) * 128])
                    p.op("vector", "tensor_tensor", gtr[:, :], gtr[:, :],
                         sf1[:, :], op=ALU.mult)
                    p.op("vector", "tensor_tensor", xf1[:, :], xf1[:, :],
                         gtr[:, :], op=ALU.add)
                    p.dma("gpsimd",
                          x1_o[kc * 128:(kc + 1) * 128, k * 128:(k + 1) * 128],
                          xf1[:, :])

    if standalone:
        p.build(emit)
    else:
        emit(p)
    return p


def moe_host_consts(inputs):
    b1 = inputs["L0_ffn_b1"].astype(np.float32)
    b2 = inputs["L0_ffn_b2"].astype(np.float32)
    b1t = np.ascontiguousarray(
        b1.reshape(8, 32, 128).transpose(2, 0, 1).reshape(128, 256))
    b2t = np.ascontiguousarray(
        b2.reshape(8, 8, 128).transpose(2, 0, 1).reshape(128, 64))
    iota8 = np.ascontiguousarray(np.tile(np.arange(8, dtype=np.float32), (128, 1)))
    tril = np.ascontiguousarray(np.triu(np.ones((128, 128), np.float32)))
    return b1t, b2t, iota8, tril


def build_L1a_pre(p=None):
    standalone = p is None
    if standalone:
        p = Prog("l1a_pre")

    def emit(p):
        xt = p.dram("xt", (DM, TPC), F32, kind="ExternalInput").ap()
        wqT = p.dram("wqT", (DM, DM), F32, kind="ExternalInput").ap()
        wkT = p.dram("wkT", (DM, DM), F32, kind="ExternalInput").ap()
        wvT = p.dram("wvT", (DM, DM), F32, kind="ExternalInput").ap()
        wgT = p.dram("wgT", (DM, DM), F32, kind="ExternalInput").ap()
        wbT = p.dram("wbT", (DM, 16), F32, kind="ExternalInput").ap()
        wgkT = p.dram("wgkT", (DM, 16), F32, kind="ExternalInput").ap()
        cw = p.dram("cw", (3, 128, DC * 4), F32, kind="ExternalInput").ap()
        cb = p.dram("cb", (3, 128, DC), F32, kind="ExternalInput").ap()
        bbv = p.dram("bbv", (16, 1), F32, kind="ExternalInput").ap()
        bgkv = p.dram("bgkv", (16, 1), F32, kind="ExternalInput").ap()
        dtbv = p.dram("dtbv", (16, 1), F32, kind="ExternalInput").ap()
        alogv = p.dram("alogv", (16, 1), F32, kind="ExternalInput").ap()
        ropeC = p.dram("ropeC", (128, SEQ), F32, kind="ExternalInput").ap()
        ropeS = p.dram("ropeS", (128, SEQ), F32, kind="ExternalInput").ap()
        rotT = p.dram("rotT", (128, 128), F32, kind="ExternalInput").ap()
        bc2 = p.dram("bc2", (2, 128), F32, kind="ExternalInput").ap()
        eyed = p.dram("eye_d", (128, 128), F32, kind="ExternalInput").ap()

        xn_o = p.dram("xn_o", (DM, TPC), BF16, kind="ExternalOutput").ap()
        qf_o = p.dram("qf_o", (DM, TPC), BF16, kind="ExternalOutput").ap()
        kf_o = p.dram("kf_o", (DM, TPC), BF16, kind="ExternalOutput").ap()
        vf_o = p.dram("vf_o", (DM, TPC), BF16, kind="ExternalOutput").ap()
        g_o = p.dram("g_o", (DM, TPC), BF16, kind="ExternalOutput").ap()
        qt_o = p.dram("qt_o", (TPC, DM), BF16, kind="ExternalOutput").ap()
        kt_o = p.dram("kt_o", (TPC, DM), BF16, kind="ExternalOutput").ap()
        vt_o = p.dram("vt_o", (TPC, DM), BF16, kind="ExternalOutput").ap()
        la_o = p.dram("la_o", (16, TPC), F32, kind="ExternalOutput").ap()
        be_o = p.dram("be_o", (16, TPC), F32, kind="ExternalOutput").ap()

        misc = emit_misc(p)
        psA, psB, psC, psD = misc["psA"], misc["psB"], misc["psC"], misc["psD"]
        onef = misc["ones_b128_f"]

        xnbuf = p.sbuf("xnbuf", (128, DC * TPC), BF16)
        with p.scope():
            emit_ln(p, xt, xnbuf, misc)
        for kc in range(DC):
            p.dma("scalar", xn_o[kc * 128:(kc + 1) * 128, :],
                  xnbuf[:, kc * TPC:(kc + 1) * TPC])

        w_sb = p.sbuf("w_sb", (128, DC * DM), BF16)
        rc = p.sbuf("ropeC_sb", (128, SEQ), F32)
        rs = p.sbuf("ropeS_sb", (128, SEQ), F32)
        rt = p.sbuf("rotT_sb", (128, 128), F32)
        eye = p.sbuf("eye_sb", (128, 128), BF16)
        bc2s = p.sbuf("bc2s", (2, 128), F32)
        cws = p.sbuf("cws", (128, DC * 4), F32)
        cbs = p.sbuf("cbs", (128, DC), F32)
        p.dma("sync", rc[:, :], ropeC[:, :])
        p.dma("sync", rs[:, :], ropeS[:, :])
        p.dma("sync", rt[:, :], rotT[:, :])
        p.dma("gpsimd", eye[:, :], eyed[:, :])
        p.dma("sync", bc2s[:, :], bc2[:, :])

        cpad = p.sbuf("cpad", (128, SEQ + 3), F32)
        cv1 = p.sbuf("cv1", (128, SEQ), F32)
        cv2 = p.sbuf("cv2", (128, SEQ), F32)
        qrp = p.sbuf("qrp", (128, SEQ), F32)
        t1 = p.sbuf("t1b", (128, SEQ), F32)
        sqv = p.sbuf("sqv", (128, SEQ), BF16)
        nrm = p.sbuf("nrm", (2, SEQ), F32)
        nrmb = p.sbuf("nrmb", (128, SEQ), F32)
        outb = p.sbuf("outb", (128, SEQ), BF16)
        trb = p.sbuf("trb", (128, 128), BF16)
        onesb2 = p.sbuf("onesb2", (128, 2), BF16)
        p.op("vector", "memset", onesb2[:, :], 0.0)
        p.op("vector", "memset", onesb2[0:64, 0:1], 1.0)
        p.op("vector", "memset", onesb2[64:128, 1:2], 1.0)
        psT = misc["psT"]

        def proj_conv(wT_d, ci, f_out, t_out, do_rope_norm):
            load_w_bf16(p, w_sb, wT_d, DM)
            p.dma("sync", cws[:, :], cw[ci, :, :])
            p.dma("sync", cbs[:, :], cb[ci, :, :])
            for oc in range(DC):
                for b in range(NSEQ):
                    p.op("vector", "memset", cpad[:, 0:3], 0.0)
                    for tb in range(4):
                        bank = [psA, psB][tb % 2]
                        gtb = b * 4 + tb
                        for kc in range(DC):
                            p.op("tensor", "matmul", bank[:, :],
                                 w_sb[:, kc * DM + oc * 128: kc * DM + (oc + 1) * 128],
                                 xnbuf[:, kc * TPC + gtb * 512: kc * TPC + (gtb + 1) * 512],
                                 start=(kc == 0), stop=(kc == DC - 1),
                                 group=(kc > 0))
                        p.op("vector", "tensor_copy",
                             cpad[:, 3 + tb * 512: 3 + (tb + 1) * 512],
                             bank[:, :])
                    p.op("vector", "tensor_scalar", cv1[:, :], cpad[:, 0:SEQ],
                         cws[:, oc * 4 + 0: oc * 4 + 1], None, op0=ALU.mult)
                    for j in range(1, 4):
                        src, dst = (cv1, cv2) if j % 2 == 1 else (cv2, cv1)
                        p.op("vector", "scalar_tensor_tensor", dst[:, :],
                             cpad[:, j:j + SEQ], cws[:, oc * 4 + j: oc * 4 + j + 1],
                             src[:, :], op0=ALU.mult, op1=ALU.add)
                    p.op("scalar", "activation", qrp[:, :], cv2[:, :], AF.Silu,
                         bias=cbs[:, oc:oc + 1])
                    if do_rope_norm:
                        p.op("tensor", "matmul", psC[:, 0:512], rt[:, :],
                             qrp[:, 0:512], start=True, stop=True)
                        p.op("tensor", "matmul", psD[:, 0:512], rt[:, :],
                             qrp[:, 512:1024], start=True, stop=True)
                        p.op("vector", "tensor_tensor", t1[:, 0:512],
                             psC[:, 0:512], rs[:, 0:512], op=ALU.mult)
                        p.op("vector", "tensor_tensor", t1[:, 512:1024],
                             psD[:, 0:512], rs[:, 512:1024], op=ALU.mult)
                        p.op("tensor", "matmul", psC[:, 0:512], rt[:, :],
                             qrp[:, 1024:1536], start=True, stop=True)
                        p.op("tensor", "matmul", psD[:, 0:512], rt[:, :],
                             qrp[:, 1536:2048], start=True, stop=True)
                        p.op("vector", "tensor_tensor", t1[:, 1024:1536],
                             psC[:, 0:512], rs[:, 1024:1536], op=ALU.mult)
                        p.op("vector", "tensor_tensor", t1[:, 1536:2048],
                             psD[:, 0:512], rs[:, 1536:2048], op=ALU.mult)
                        p.op("vector", "tensor_tensor", qrp[:, :], qrp[:, :],
                             rc[:, :], op=ALU.mult)
                        p.op("vector", "tensor_tensor", qrp[:, :], qrp[:, :],
                             t1[:, :], op=ALU.add)
                        p.op("scalar", "activation", sqv[:, :], qrp[:, :],
                             AF.Square)
                        for ts in range(4):
                            p.op("tensor", "matmul", psC[0:2, 0:512],
                                 onesb2[:, :], sqv[:, ts * 512:(ts + 1) * 512],
                                 start=True, stop=True)
                            p.op("vector", "tensor_copy",
                                 nrm[:, ts * 512:(ts + 1) * 512], psC[0:2, 0:512])
                        p.op("scalar", "activation", nrm[:, :], nrm[:, :],
                             AF.Ln, bias=1e-12)
                        p.op("scalar", "activation", nrm[:, :], nrm[:, :],
                             AF.Exp, scale=-0.5)
                        p.op("tensor", "matmul", psC[:, 0:512], bc2s[:, :], nrm[:, 0:512], start=True, stop=True)
                        p.op("tensor", "matmul", psD[:, 0:512], bc2s[:, :],
                             nrm[:, 512:1024], start=True, stop=True)
                        p.op("vector", "tensor_tensor", qrp[:, 0:512],
                             qrp[:, 0:512], psC[:, 0:512], op=ALU.mult)
                        p.op("vector", "tensor_tensor", qrp[:, 512:1024],
                             qrp[:, 512:1024], psD[:, 0:512], op=ALU.mult)
                        p.op("tensor", "matmul", psC[:, 0:512], bc2s[:, :],
                             nrm[:, 1024:1536], start=True, stop=True)
                        p.op("tensor", "matmul", psD[:, 0:512], bc2s[:, :],
                             nrm[:, 1536:2048], start=True, stop=True)
                        p.op("vector", "tensor_tensor", qrp[:, 1024:1536],
                             qrp[:, 1024:1536], psC[:, 0:512], op=ALU.mult)
                        p.op("vector", "tensor_tensor", qrp[:, 1536:2048],
                             qrp[:, 1536:2048], psD[:, 0:512], op=ALU.mult)
                    p.op("vector", "tensor_copy", outb[:, :], qrp[:, :])
                    p.dma("gpsimd",
                          f_out[oc * 128:(oc + 1) * 128,
                                b * SEQ:(b + 1) * SEQ], outb[:, :])
                    for j in range(16):
                        p.op("tensor", "transpose", psT[:, 0:128],
                             outb[:, j * 128:(j + 1) * 128], eye[:, :])
                        p.op("vector", "tensor_copy", trb[:, :], psT[:, 0:128])
                        p.dma("gpsimd",
                              t_out[b * SEQ + j * 128: b * SEQ + (j + 1) * 128,
                                    oc * 128:(oc + 1) * 128], trb[:, :])

        proj_conv(wqT, 0, qf_o, qt_o, True)
        proj_conv(wkT, 1, kf_o, kt_o, True)
        proj_conv(wvT, 2, vf_o, vt_o, False)

        load_w_bf16(p, w_sb, wgT, DM)
        gb = p.sbuf("gb", (128, 512), BF16)

        def cb_g(oc, tb, bank):
            p.op("vector", "tensor_copy", gb[:, :], bank[:, :])
            p.dma("gpsimd", g_o[oc * 128:(oc + 1) * 128,
                               tb * 512:(tb + 1) * 512], gb[:, :])
        emit_proj(p, w_sb, xnbuf, cb_g, [psA, psB])

        w16 = p.sbuf("w16", (128, DC * 16), BF16)
        bbs = p.sbuf("bbs", (16, 1), F32)
        bgks = p.sbuf("bgks", (16, 1), F32)
        dtbs = p.sbuf("dtbs", (16, 1), F32)
        alogs = p.sbuf("alogs", (16, 1), F32)
        negc = p.sbuf("negc", (16, 1), F32)
        zrow = p.sbuf("zrow16", (16, 512), F32)
        sp1 = p.sbuf("sp1", (16, 512), F32)
        sp2 = p.sbuf("sp2", (16, 512), F32)
        brow = p.sbuf("brow", (16, 512), F32)
        p.dma("sync", bbs[:, :], bbv[:, :])
        p.dma("sync", bgks[:, :], bgkv[:, :])
        p.dma("sync", dtbs[:, :], dtbv[:, :])
        p.dma("sync", alogs[:, :], alogv[:, :])
        p.op("scalar", "activation", negc[:, :], alogs[:, :], AF.Exp)
        p.op("vector", "tensor_scalar", negc[:, :], negc[:, :], -1.0, None,
             op0=ALU.mult)
        load_w_bf16(p, w16, wbT, 16)
        w16b = p.sbuf("w16b", (128, DC * 16), BF16)
        load_w_bf16(p, w16b, wgkT, 16)
        for tb in range(8):
            for kc in range(DC):
                p.op("tensor", "matmul", psA[0:16, :],
                     w16[:, kc * 16:(kc + 1) * 16],
                     xnbuf[:, kc * TPC + tb * 512: kc * TPC + (tb + 1) * 512],
                     start=(kc == 0), stop=(kc == DC - 1), group=(kc > 0))
            p.op("scalar", "activation", brow[:, :], psA[0:16, :], AF.Sigmoid,
                 bias=bbs[:, 0:1])
            p.dma("gpsimd", be_o[:, tb * 512:(tb + 1) * 512], brow[:, :])
            for kc in range(DC):
                p.op("tensor", "matmul", psB[0:16, :],
                     w16b[:, kc * 16:(kc + 1) * 16],
                     xnbuf[:, kc * TPC + tb * 512: kc * TPC + (tb + 1) * 512],
                     start=(kc == 0), stop=(kc == DC - 1), group=(kc > 0))
            p.op("vector", "tensor_scalar", zrow[:, :], psB[0:16, :],
                 bgks[:, 0:1], dtbs[:, 0:1], op0=ALU.add, op1=ALU.add)
            p.op("scalar", "activation", sp1[:, :], zrow[:, :], AF.Abs)
            p.op("scalar", "activation", sp1[:, :], sp1[:, :], AF.Exp,
                 scale=-1.0)
            p.op("scalar", "activation", sp1[:, :], sp1[:, :], AF.Ln, bias=1.0)
            p.op("scalar", "activation", sp2[:, :], zrow[:, :], AF.Relu)
            p.op("vector", "tensor_tensor", sp1[:, :], sp1[:, :], sp2[:, :],
                 op=ALU.add)
            p.op("vector", "tensor_scalar", sp1[:, :], sp1[:, :],
                 negc[:, 0:1], None, op0=ALU.mult)
            p.dma("gpsimd", la_o[:, tb * 512:(tb + 1) * 512], sp1[:, :])

    if standalone:
        p.build(emit)
    else:
        emit(p)
    return p


CHK = 128
NCHK = SEQ // CHK


def build_L1a_scan(p=None):
    standalone = p is None
    if standalone:
        p = Prog("l1a_scan")

    def emit(p):
        qf = p.dram("qf", (DM, SEQ), BF16, kind="ExternalInput").ap()
        kf = p.dram("kf", (DM, SEQ), BF16, kind="ExternalInput").ap()
        qt = p.dram("qt", (SEQ, DM), BF16, kind="ExternalInput").ap()
        kt = p.dram("kt", (SEQ, DM), BF16, kind="ExternalInput").ap()
        vt = p.dram("vt", (SEQ, DM), BF16, kind="ExternalInput").ap()
        la = p.dram("la", (16, SEQ), F32, kind="ExternalInput").ap()
        be = p.dram("be", (16, SEQ), F32, kind="ExternalInput").ap()
        masksu = p.dram("masksu", (128, 128), F32, kind="ExternalInput").ap()
        masksl = p.dram("masksl", (128, 128), F32, kind="ExternalInput").ap()
        eyed = p.dram("eye_d", (128, 128), F32, kind="ExternalInput").ap()

        of_o = p.dram("of_o", (DM, SEQ), BF16, kind="ExternalOutput").ap()
        rows_s = p.dram("rows_s", (7, 16, SEQ), F32).ap()

        misc = emit_misc(p)
        psA, psB, psC, psD, psE, psF = (misc["psA"], misc["psB"], misc["psC"],
                                        misc["psD"], misc["psE"], misc["psF"])
        psT = misc["psT"]
        onef = misc["ones_b128_f"]

        su = p.sbuf("su", (128, 128), F32)
        sl = p.sbuf("sl", (128, 128), F32)
        eyeb = p.sbuf("eyeb", (128, 128), BF16)
        eyef = p.sbuf("eyef", (128, 128), F32)
        p.dma("sync", su[:, :], masksu[:, :])
        p.dma("sync", sl[:, :], masksl[:, :])
        p.dma("gpsimd", eyeb[:, :], eyed[:, :])
        p.dma("sync", eyef[:, :], eyed[:, :])

        with p.scope():
            lar = p.sbuf("lar", (16, SEQ), F32)
            ber = p.sbuf("ber", (16, SEQ), F32)
            Lr = p.sbuf("Lr", (16, SEQ), F32)
            tmp = p.sbuf("tmpr", (16, SEQ), F32)
            ones16 = p.sbuf("ones16", (16, SEQ), F32)
            p.dma("sync", lar[:, :], la[:, :])
            p.dma("sync", ber[:, :], be[:, :])
            p.op("vector", "memset", ones16[:, :], 1.0)
            for c in range(NCHK):
                s = slice(c * CHK, (c + 1) * CHK)
                p.op("vector", "tensor_tensor_scan", Lr[:, s], ones16[:, s],
                     lar[:, s], 0.0, op0=ALU.mult, op1=ALU.add)
            p.dma("gpsimd", rows_s[0, :, :], Lr[:, :])
            p.op("scalar", "activation", tmp[:, :], ber[:, :], AF.Ln)
            p.op("vector", "tensor_tensor", tmp[:, :], tmp[:, :], Lr[:, :],
                 op=ALU.add)
            p.dma("gpsimd", rows_s[1, :, :], tmp[:, :])
            p.op("scalar", "activation", tmp[:, :], tmp[:, :], AF.Exp)
            p.dma("gpsimd", rows_s[6, :, :], tmp[:, :])
            p.op("vector", "tensor_scalar", tmp[:, :], Lr[:, :], -1.0, None,
                 op0=ALU.mult)
            p.dma("gpsimd", rows_s[2, :, :], tmp[:, :])
            p.op("vector", "tensor_tensor", tmp[:, :], Lr[:, :], lar[:, :],
                 op=ALU.subtract)
            p.dma("gpsimd", rows_s[3, :, :], tmp[:, :])
            p.op("scalar", "activation", tmp[:, :], tmp[:, :], AF.Exp)
            p.dma("gpsimd", rows_s[4, :, :], tmp[:, :])
            p.op("scalar", "activation", tmp[:, :], Lr[:, :], AF.Exp)
            p.dma("gpsimd", rows_s[5, :, :], tmp[:, :])

        kfsl = p.sbuf("kfsl", (128, DC * CHK), BF16)
        qfsl = p.sbuf("qfsl", (128, DC * CHK), BF16)
        ktsl = p.sbuf("ktsl", (128, DM), BF16)
        qtsl = p.sbuf("qtsl", (128, DM), BF16)
        vtsl = p.sbuf("vtsl", (128, DM), BF16)
        colsL = p.sbuf("colsL", (128, 16), F32)
        colsLB = p.sbuf("colsLB", (128, 16), F32)
        colsBA = p.sbuf("colsBA", (128, 16), F32)
        colsBE = p.sbuf("colsBE", (128, 16), F32)
        colsRT = p.sbuf("colsRT", (128, 16), F32)
        r16 = p.sbuf("r16", (16, CHK), F32)
        lastc = p.sbuf("lastc", (16, 1), F32)
        rowLb = p.sbuf("rowLb", (1, CHK), F32)
        rowNL = p.sbuf("rowNL", (1, CHK), F32)
        rowLp = p.sbuf("rowLp", (1, CHK), F32)
        rowAb = p.sbuf("rowAb", (1, CHK), F32)
        ac11 = p.sbuf("ac11", (1, 1), F32)
        accol = p.sbuf("accol", (64, 1), F32)
        Ebuild = p.sbuf("Ebuild", (128, 128), F32)
        Wm = p.sbuf("Wm", (128, 128), F32)
        Wmt = p.sbuf("Wmt", (128, 128), F32)
        Htm = p.sbuf("Htm", (128, 128), F32)
        PWa = p.sbuf("PWa", (128, 128), F32)
        PWb = p.sbuf("PWb", (128, 128), F32)
        WT = [p.sbuf("WTl%d" % i, (128, 128), F32) for i in range(6)]
        Yst = p.sbuf("Yst", (128, 128), F32)
        Yst2 = p.sbuf("Yst2", (128, 128), F32)
        Wbart = p.sbuf("Wbart", (64, 128), F32)
        Ubf = p.sbuf("Ubf", (128, 64), F32)
        AQf = p.sbuf("AQf", (64, 128), F32)
        Ktil = p.sbuf("Ktil", (128, 64), F32)
        Obf = p.sbuf("Obf", (128, 64), F32)
        Otr = p.sbuf("Otr", (64, 128), F32)
        S0f = [p.sbuf("S0f%d" % h, (64, 64), F32) for h in range(16)]
        S0b = p.sbuf("S0b", (64, 64), F32)
        stmp = p.sbuf("stmp", (64, 64), F32)

        for h in range(16):
            p.op("vector", "memset", S0f[h][:, :], 0.0)

        def col_tile(row_idx, dst, do_exp=False):
            p.dma("sync", r16[:, :], rows_s[row_idx, :, cs])
            if do_exp:
                p.op("scalar", "activation", r16[:, :], r16[:, :], AF.Exp)
            p.op("tensor", "transpose", psF[0:128, 0:16], r16[:, :],
                 eyef[0:16, 0:16])
            p.op("vector", "tensor_copy", dst[:, :], psF[0:128, 0:16])

        for c in range(NCHK):
            cs = slice(c * CHK, (c + 1) * CHK)
            for kc2 in range(DC):
                p.dma("sync", kfsl[:, kc2 * CHK:(kc2 + 1) * CHK],
                      kf[kc2 * 128:(kc2 + 1) * 128, cs])
                p.dma("sync", qfsl[:, kc2 * CHK:(kc2 + 1) * CHK],
                      qf[kc2 * 128:(kc2 + 1) * 128, cs])
            p.dma("sync", ktsl[:, :], kt[cs, :])
            p.dma("sync", qtsl[:, :], qt[cs, :])
            p.dma("sync", vtsl[:, :], vt[cs, :])
            col_tile(0, colsL)
            col_tile(1, colsLB)
            col_tile(6, colsBA)
            p.dma("sync", r16[:, :], be[:, cs])
            p.op("tensor", "transpose", psF[0:128, 0:16], r16[:, :],
                 eyef[0:16, 0:16])
            p.op("vector", "tensor_copy", colsBE[:, :], psF[0:128, 0:16])
            p.dma("sync", r16[:, :], rows_s[2, :, cs])
            p.dma("sync", lastc[:, :],
                  rows_s[0, :, c * CHK + CHK - 1: c * CHK + CHK],
                  allow_slow_non_contiguous=True)
            p.op("vector", "tensor_scalar", r16[:, :], r16[:, :],
                 lastc[:, 0:1], None, op0=ALU.add)
            p.op("scalar", "activation", r16[:, :], r16[:, :], AF.Exp)
            p.op("tensor", "transpose", psF[0:128, 0:16], r16[:, :],
                 eyef[0:16, 0:16])
            p.op("vector", "tensor_copy", colsRT[:, :], psF[0:128, 0:16])

            for h in range(16):
                kc_, rr = h // 2, (h % 2) * 64
                Kf = kfsl[rr:rr + 64, kc_ * CHK:(kc_ + 1) * CHK]
                Qf = qfsl[rr:rr + 64, kc_ * CHK:(kc_ + 1) * CHK]
                Kt_ = ktsl[:, h * 64:(h + 1) * 64]
                Qt_ = qtsl[:, h * 64:(h + 1) * 64]
                Vt_ = vtsl[:, h * 64:(h + 1) * 64]
                p.dma("sync", rowLb[:, :], rows_s[1, h:h + 1, cs])
                p.dma("sync", rowNL[:, :], rows_s[2, h:h + 1, cs])
                p.dma("sync", rowLp[:, :], rows_s[3, h:h + 1, cs])
                p.dma("sync", rowAb[:, :], rows_s[4, h:h + 1, cs])
                p.dma("sync", ac11[:, :],
                      rows_s[5, h:h + 1, c * CHK + CHK - 1: c * CHK + CHK])
                p.op("vector", "tensor_copy", S0b[:, :], S0f[h][:, :])
                p.op("tensor", "matmul", psA[:, 0:128], Kf, Kf,
                     start=True, stop=True)
                p.op("tensor", "matmul", psB[:, 0:128], Kf, Qf,
                     start=True, stop=True)
                p.op("tensor", "matmul", psC[:, 0:128], onef[:, :],
                     rowLb[:, :], start=True, stop=True)
                p.op("vector", "tensor_scalar", Ebuild[:, :], psC[:, 0:128],
                     colsL[:, h:h + 1], 0.0, op0=ALU.subtract, op1=ALU.min)
                p.op("scalar", "activation", Ebuild[:, :], Ebuild[:, :], AF.Exp)
                p.op("vector", "tensor_tensor", Ebuild[:, :], Ebuild[:, :],
                     su[:, :], op=ALU.mult)
                p.op("vector", "tensor_tensor", Wmt[:, :], Ebuild[:, :],
                     psA[:, 0:128], op=ALU.mult)
                p.op("tensor", "matmul", psC[:, 0:128], onef[:, :],
                     rowNL[:, :], start=True, stop=True)
                p.op("vector", "tensor_scalar", Ebuild[:, :], psC[:, 0:128],
                     colsLB[:, h:h + 1], 0.0, op0=ALU.add, op1=ALU.min)
                p.op("scalar", "activation", Ebuild[:, :], Ebuild[:, :], AF.Exp)
                p.op("vector", "tensor_tensor", Ebuild[:, :], Ebuild[:, :],
                     sl[:, :], op=ALU.mult)
                p.op("vector", "tensor_tensor", Wm[:, :], Ebuild[:, :],
                     psA[:, 0:128], op=ALU.mult)
                p.op("tensor", "matmul", psC[:, 0:128], onef[:, :],
                     rowLp[:, :], start=True, stop=True)
                p.op("vector", "tensor_scalar", Ebuild[:, :], psC[:, 0:128],
                     colsL[:, h:h + 1], 0.0, op0=ALU.subtract, op1=ALU.min)
                p.op("scalar", "activation", Ebuild[:, :], Ebuild[:, :], AF.Exp)
                p.op("vector", "tensor_tensor", Ebuild[:, :], Ebuild[:, :],
                     su[:, :], op=ALU.mult)
                p.op("vector", "tensor_tensor", Htm[:, :], Ebuild[:, :],
                     psB[:, 0:128], op=ALU.mult)
                srcW, srcWt = Wm, Wmt
                for lev in range(6):
                    dstW = PWa if lev % 2 == 0 else PWb
                    p.op("tensor", "matmul", psC[:, 0:128], srcWt[:, :],
                         srcW[:, :], start=True, stop=True)
                    p.op("vector", "tensor_copy", dstW[:, :], psC[:, 0:128])
                    p.op("tensor", "matmul", psC[:, 0:128], srcW[:, :],
                         srcWt[:, :], start=True, stop=True)
                    p.op("vector", "tensor_copy", WT[lev][:, :], psC[:, 0:128])
                    srcW, srcWt = dstW, WT[lev]
                p.op("vector", "tensor_scalar", Yst[:, 0:64], Vt_,
                     colsBE[:, h:h + 1], None, op0=ALU.mult)
                p.op("vector", "tensor_scalar", Yst[:, 64:128], Kt_,
                     colsBA[:, h:h + 1], None, op0=ALU.mult)
                cur, oth = Yst, Yst2
                for lev in range(5, -1, -1):
                    p.op("tensor", "matmul", psC[:, 0:128], WT[lev][:, :],
                         cur[:, :], start=True, stop=True)
                    p.op("vector", "tensor_tensor", oth[:, :], cur[:, :],
                         psC[:, 0:128], op=ALU.add)
                    cur, oth = oth, cur
                p.op("tensor", "matmul", psC[:, 0:128], Wmt[:, :], cur[:, :],
                     start=True, stop=True)
                p.op("vector", "tensor_tensor", oth[:, :], cur[:, :],
                     psC[:, 0:128], op=ALU.subtract)
                cur = oth
                p.op("tensor", "transpose", psF[0:64, 0:128], cur[:, 64:128],
                     eyef[:, :])
                p.op("vector", "tensor_copy", Wbart[:, :], psF[0:64, 0:128])
                p.op("tensor", "matmul", psD[:, 0:64], Wbart[:, :], S0b[:, :],
                     start=True, stop=True)
                p.op("vector", "tensor_tensor", Ubf[:, :], cur[:, 0:64],
                     psD[:, 0:64], op=ALU.subtract)
                p.op("tensor", "matmul", psF[0:64, 0:128], onef[:, 0:64],
                     rowAb[:, :], start=True, stop=True)
                p.op("vector", "tensor_tensor", AQf[:, :], Qf, psF[0:64, 0:128],
                     op=ALU.mult)
                p.op("tensor", "matmul", psE[:, 0:64], Htm[:, :], Ubf[:, :],
                     start=True, stop=False)
                p.op("tensor", "matmul", psE[:, 0:64], AQf[:, :], S0b[:, :],
                     start=False, stop=True, group=True, skip_group_check=True)
                p.op("vector", "tensor_copy", Obf[:, :], psE[:, 0:64])
                p.op("tensor", "transpose", psF[0:64, 0:128], Obf[:, :],
                     eyef[:, :])
                p.op("vector", "tensor_copy", Otr[:, :], psF[0:64, 0:128])
                p.dma("gpsimd", of_o[h * 64:(h + 1) * 64, cs], Otr[:, :])
                p.op("vector", "tensor_scalar", Ktil[:, :], Kt_,
                     colsRT[:, h:h + 1], None, op0=ALU.mult)
                p.op("tensor", "matmul", psD[0:64, 0:64], Ktil[:, :],
                     Ubf[:, :], start=True, stop=True)
                p.op("tensor", "matmul", psF[0:64, 0:1], onef[:, 0:64],
                     ac11[:, :], start=True, stop=True)
                p.op("vector", "tensor_copy", accol[:, :], psF[0:64, 0:1])
                p.op("vector", "tensor_scalar", stmp[:, :], S0f[h][:, :],
                     accol[:, 0:1], None, op0=ALU.mult)
                p.op("vector", "tensor_tensor", S0f[h][:, :], stmp[:, :],
                     psD[0:64, 0:64], op=ALU.add)

    if standalone:
        p.build(emit)
    else:
        emit(p)
    return p


def build_L1a_out(p=None):
    standalone = p is None
    if standalone:
        p = Prog("l1a_out")

    def emit(p):
        qf = p.dram("qf", (DM, TPC), BF16, kind="ExternalInput").ap()
        kf = p.dram("kf", (DM, TPC), BF16, kind="ExternalInput").ap()
        vf = p.dram("vf", (DM, TPC), BF16, kind="ExternalInput").ap()
        gf = p.dram("gf", (DM, TPC), BF16, kind="ExternalInput").ap()
        off = p.dram("off", (DM, TPC), BF16, kind="ExternalInput").ap()
        woT = p.dram("woT", (DM, DM), F32, kind="ExternalInput").ap()
        dp2 = p.dram("dp2", (2, DC), F32, kind="ExternalInput").ap()
        onw = p.dram("onw", (128, 1), F32, kind="ExternalInput").ap()
        bc2 = p.dram("bc2", (2, 128), F32, kind="ExternalInput").ap()

        s1_o = p.dram("s1_o", (DM, TPC), BF16, kind="ExternalOutput").ap()

        misc = emit_misc(p)
        psA, psB, psC, psD = misc["psA"], misc["psB"], misc["psC"], misc["psD"]
        onesb2 = p.sbuf("onesb2", (128, 2), BF16)
        p.op("vector", "memset", onesb2[:, :], 0.0)
        p.op("vector", "memset", onesb2[0:64, 0:1], 1.0)
        p.op("vector", "memset", onesb2[64:128, 1:2], 1.0)
        bc2s = p.sbuf("bc2s", (2, 128), F32)
        dp2s = p.sbuf("dp2s", (2, DC), F32)
        onws = p.sbuf("onws", (128, 1), F32)
        p.dma("sync", bc2s[:, :], bc2[:, :])
        p.dma("sync", dp2s[:, :], dp2[:, :])
        p.dma("sync", onws[:, :], onw[:, :])

        qt_ = p.sbuf("qt_", (128, 512), BF16)
        kt_ = p.sbuf("kt_", (128, 512), BF16)
        vt_ = p.sbuf("vt_", (128, 512), BF16)
        ot_ = p.sbuf("ot_", (128, 512), F32)
        gt_ = p.sbuf("gt_", (128, 512), BF16)
        prod = p.sbuf("prod", (128, 512), BF16)
        dqk = p.sbuf("dqk", (2, 512), F32)
        ms = p.sbuf("ms", (2, 512), F32)
        sqb = p.sbuf("sqb", (128, 512), BF16)
        gated = p.sbuf("gated", (128, DC * TPC), BF16)
        sbf = p.sbuf("sbf", (128, 512), BF16)

        for oc in range(DC):
            for tb in range(8):
                cs = slice(tb * 512, (tb + 1) * 512)
                rowsl = slice(oc * 128, (oc + 1) * 128)
                p.dma("sync", qt_[:, :], qf[rowsl, cs])
                p.dma("sync", kt_[:, :], kf[rowsl, cs])
                p.dma("sync", vt_[:, :], vf[rowsl, cs])
                p.dma("gpsimd", ot_[:, :], off[rowsl, cs])
                p.dma("sync", gt_[:, :], gf[rowsl, cs])
                p.op("vector", "tensor_tensor", prod[:, :], qt_[:, :],
                     kt_[:, :], op=ALU.mult)
                p.op("tensor", "matmul", psA[0:2, :], onesb2[:, :],
                     prod[:, :], start=True, stop=True)
                p.op("vector", "tensor_scalar", dqk[:, :], psA[0:2, :],
                     dp2s[:, oc:oc + 1], None, op0=ALU.mult)
                p.op("tensor", "matmul", psB[:, :], bc2s[:, :], dqk[:, :],
                     start=True, stop=True)
                p.op("vector", "tensor_tensor", prod[:, :], vt_[:, :],
                     psB[:, :], op=ALU.mult)
                p.op("vector", "tensor_tensor", ot_[:, :], ot_[:, :],
                     prod[:, :], op=ALU.add)
                p.op("scalar", "activation", sqb[:, :], ot_[:, :], AF.Square)
                p.op("tensor", "matmul", psA[0:2, :], onesb2[:, :],
                     sqb[:, :], start=True, stop=True)
                p.op("vector", "tensor_scalar", ms[:, :], psA[0:2, :],
                     1.0 / 64.0, None, op0=ALU.mult)
                p.op("scalar", "activation", ms[:, :], ms[:, :], AF.Ln,
                     bias=1e-6)
                p.op("scalar", "activation", ms[:, :], ms[:, :], AF.Exp,
                     scale=-0.5)
                p.op("tensor", "matmul", psB[:, :], bc2s[:, :], ms[:, :],
                     start=True, stop=True)
                p.op("vector", "tensor_tensor", ot_[:, :], ot_[:, :],
                     psB[:, :], op=ALU.mult)
                p.op("vector", "tensor_scalar", ot_[:, :], ot_[:, :],
                     onws[:, 0:1], None, op0=ALU.mult)
                p.op("scalar", "activation", ot_[:, :], ot_[:, :], AF.Silu)
                p.op("vector", "tensor_tensor",
                     gated[:, oc * TPC + tb * 512: oc * TPC + (tb + 1) * 512],
                     ot_[:, :], gt_[:, :], op=ALU.mult)

        w_sb = p.sbuf("w_sb", (128, DC * DM), BF16)
        load_w_bf16(p, w_sb, woT, DM)

        def cb(oc, tb, bank):
            p.op("vector", "tensor_copy", sbf[:, :], bank[:, :])
            p.dma("gpsimd", s1_o[oc * 128:(oc + 1) * 128,
                                 tb * 512:(tb + 1) * 512], sbf[:, :])
        emit_proj(p, w_sb, gated, cb, [psC, psD])

    if standalone:
        p.build(emit)
    else:
        emit(p)
    return p


def l1_host_consts():
    c = host_consts()
    if "l1ropeC" in c:
        return c
    D = HD
    inv_freq = 1.0 / (10000.0 ** (np.arange(0, D, 2, dtype=np.float32) / D))
    t = np.arange(SEQ, dtype=np.float32)
    freqs = np.outer(t, inv_freq)
    emb = np.concatenate([freqs, freqs], 1)
    cose = np.cos(emb).astype(np.float32)
    sine = np.sin(emb).astype(np.float32)
    cp = cose[:, ::2].T
    sp_ = sine[:, ::2].T
    c64 = np.zeros((64, SEQ), np.float32)
    s64 = np.zeros((64, SEQ), np.float32)
    c64[0::2] = cp; c64[1::2] = cp
    s64[0::2] = sp_; s64[1::2] = sp_
    c["l1ropeC"] = np.ascontiguousarray(np.concatenate([c64, c64], 0))
    c["l1ropeS"] = np.ascontiguousarray(np.concatenate([s64, s64], 0))
    bc2 = np.zeros((2, 128), np.float32)
    bc2[0, :64] = 1.0; bc2[1, 64:] = 1.0
    c["bc2"] = bc2
    tt = np.arange(128)
    c["masksu"] = np.ascontiguousarray((tt[:, None] < tt[None, :]).astype(np.float32))
    c["masksl"] = np.ascontiguousarray((tt[None, :] < tt[:, None]).astype(np.float32))
    return c


def l1a_pre_inmaps(inputs, x1_shards):
    c = l1_host_consts()
    cw = np.stack([inputs["L1_attn_cqw"][:, 0, :], inputs["L1_attn_ckw"][:, 0, :],
                   inputs["L1_attn_cvw"][:, 0, :]]).astype(np.float32)
    cwl = np.ascontiguousarray(
        cw.reshape(3, 8, 128, 4).transpose(0, 2, 1, 3).reshape(3, 128, 32))
    cbv = np.stack([inputs["L1_attn_cqb"], inputs["L1_attn_ckb"],
                    inputs["L1_attn_cvb"]]).astype(np.float32)
    cbl = np.ascontiguousarray(
        cbv.reshape(3, 8, 128).transpose(0, 2, 1))
    base = {
        "wqT": np.ascontiguousarray(inputs["L1_attn_Wq"].T.astype(np.float32)),
        "wkT": np.ascontiguousarray(inputs["L1_attn_Wk"].T.astype(np.float32)),
        "wvT": np.ascontiguousarray(inputs["L1_attn_Wv"].T.astype(np.float32)),
        "wgT": np.ascontiguousarray(inputs["L1_attn_Wg"].T.astype(np.float32)),
        "wbT": np.ascontiguousarray(inputs["L1_attn_Wb"].T.astype(np.float32)),
        "wgkT": np.ascontiguousarray(inputs["L1_attn_Wgk"].T.astype(np.float32)),
        "cw": cwl, "cb": cbl,
        "bbv": np.ascontiguousarray(inputs["L1_attn_bb"].reshape(16, 1).astype(np.float32)),
        "bgkv": np.ascontiguousarray(inputs["L1_attn_bgk"].reshape(16, 1).astype(np.float32)),
        "dtbv": np.ascontiguousarray(inputs["L1_attn_dt_bias"].reshape(16, 1).astype(np.float32)),
        "alogv": np.ascontiguousarray(inputs["L1_attn_A_log"].reshape(16, 1).astype(np.float32)),
        "ropeC": c["l1ropeC"], "ropeS": c["l1ropeS"], "rotT": c["ropeRotT"],
        "bc2": c["bc2"], "eye_d": c["eye128"],
    }
    maps = []
    for ci in range(NCORE):
        m = dict(base)
        m["xt"] = x1_shards[ci]
        maps.append(m)
    return maps


def l1a_out_inmaps(inputs, pre_res, of_all):
    c = l1_host_consts()
    dp = inputs["L1_attn_Dp"].astype(np.float32)
    dp2 = np.ascontiguousarray(dp.reshape(8, 2).T)
    onw = np.ascontiguousarray(
        np.tile(inputs["L1_attn_onorm_w"].astype(np.float32), 2).reshape(128, 1))
    base = {
        "woT": np.ascontiguousarray(inputs["L1_attn_Wo"].T.astype(np.float32)),
        "dp2": dp2, "onw": onw, "bc2": c["bc2"],
    }
    maps = []
    for ci in range(NCORE):
        m = dict(base)
        r = pre_res[ci]
        m["qf"] = r["qf_o"]
        m["kf"] = r["kf_o"]
        m["vf"] = r["vf_o"]
        m["gf"] = r["g_o"]
        m["off"] = of_all[ci]
        maps.append(m)
    return maps


def build_L1b(p=None):
    standalone = p is None
    if standalone:
        p = Prog("l1b")

    def emit(p):
        xn1 = p.dram("xn1", (DM, TPC), BF16, kind="ExternalInput").ap()
        x1 = p.dram("x1", (DM, TPC), F32, kind="ExternalInput").ap()
        s1 = p.dram("s1", (DM, TPC), BF16, kind="ExternalInput").ap()
        winT = p.dram("winT", (DM, DM), F32, kind="ExternalInput").ap()
        woutT = p.dram("woutT", (DM, DM), F32, kind="ExternalInput").ap()
        keysa = p.dram("keysa", (8, 128, 512), F32, kind="ExternalInput").ap()
        keysb = p.dram("keysb", (8, 128, 512), F32, kind="ExternalInput").ap()
        ushd = p.dram("ushd", (128, 1), F32, kind="ExternalInput").ap()
        evtab = p.dram("evtab", (262144, 128), F32, kind="ExternalInput").ap()
        selS = p.dram("selS", (32, 256), F32, kind="ExternalInput").ap()
        selID = p.dram("selID", (32, 256), F32, kind="ExternalInput").ap()
        lnwt = p.dram("lnwt", (128, DC), F32, kind="ExternalInput").ap()
        lnbt = p.dram("lnbt", (128, DC), F32, kind="ExternalInput").ap()
        eyed = p.dram("eye_d", (128, 128), F32, kind="ExternalInput").ap()

        x2_o = p.dram("x2_o", (DM, TPC), F32, kind="ExternalOutput").ap()

        mT_s = p.dram("mT_s", (DM, TPC), BF16).ap()
        act_s = p.dram("act_s", (8, TPC), F32).ap()

        misc = emit_misc(p)
        psA, psB, psC, psD = misc["psA"], misc["psB"], misc["psC"], misc["psD"]
        psT = misc["psT"]
        onef = misc["ones_b128_f"]

        eye = p.sbuf("eye_sb", (128, 128), BF16)
        p.dma("gpsimd", eye[:, :], eyed[:, :])
        eyef = p.sbuf("eyef_sb", (128, 128), F32)
        p.dma("sync", eyef[:, :], eyed[:, :])

        hT = p.sbuf("hT", (128, DC * TPC), BF16)
        with p.scope():
            xbuf = p.sbuf("xbuf", (128, DC * TPC), BF16)
            w_sb = p.sbuf("w_sb", (128, DC * DM), BF16)
            for kc in range(DC):
                p.dma("sync", xbuf[:, kc * TPC:(kc + 1) * TPC],
                      xn1[kc * 128:(kc + 1) * 128, :])
            load_w_bf16(p, w_sb, winT, DM)

            def cb_h(oc, tb, bank):
                p.op("vector", "tensor_copy",
                     hT[:, oc * TPC + tb * 512: oc * TPC + (tb + 1) * 512],
                     bank[:, :])
            emit_proj(p, w_sb, xbuf, cb_h, [psA, psB])

        with p.scope():
            ushb = p.sbuf("ushb", (128, 1), BF16)
            p.dma("gpsimd", ushb[:, :], ushd[:, :])
            arow = p.sbuf("arow", (1, 512), F32)
            for tb in range(8):
                for hh in range(8):
                    p.op("tensor", "matmul", psA[0:1, :], ushb[:, :],
                         hT[:, hh * TPC + tb * 512: hh * TPC + (tb + 1) * 512],
                         start=True, stop=True)
                    p.op("scalar", "activation", arow[0:1, :], psA[0:1, :],
                         AF.Sigmoid)
                    p.dma("gpsimd", act_s[hh, tb * 512:(tb + 1) * 512],
                          arow[0:1, :])

        ka_sb = p.sbuf("ka_sb", (128, 8 * 512), BF16)
        kb_sb = p.sbuf("kb_sb", (128, 8 * 512), BF16)
        for hh in range(8):
            p.dma("gpsimd", ka_sb[:, hh * 512:(hh + 1) * 512], keysa[hh, :, :])
            p.dma("gpsimd", kb_sb[:, hh * 512:(hh + 1) * 512], keysb[hh, :, :])
        selSs = p.sbuf("selSs", (32, 256), F32)
        selIDs = p.sbuf("selIDs", (64, 256), F32)
        p.dma("sync", selSs[:, :], selS[:, :])
        p.dma("sync", selIDs[32:64, :], selID[:, :])

        sc_t = p.sbuf("sc_t", (128, 512), F32)
        v8a = p.sbuf("v8a", (128, 8), F32)
        i8a = p.sbuf("i8a", (128, 8), U32)
        mr_t = p.sbuf("mr_t", (128, 512), F32)
        stg = p.sbuf("stg", (128, 64), F32)
        stgT = p.sbuf("stgT", (64, 128), F32)
        grid_s = p.sbuf("grid_s", (128, 256), F32)
        grid_idf = p.sbuf("grid_idf", (128, 256), F32)
        gid_i = p.sbuf("gid_i", (128, 256), I32)
        keyA = p.sbuf("keyA", (128, 256), I32)
        keyB = p.sbuf("keyB", (128, 256), I32)
        tmpi = p.sbuf("tmpi", (128, 256), I32)
        kA16 = p.sbuf("kA16", (128, 16), F32)
        kB16 = p.sbuf("kB16", (128, 16), F32)
        mr2 = p.sbuf("mr2", (128, 256), F32)
        rAi = p.sbuf("rAi", (128, 16), I32)
        rBi = p.sbuf("rBi", (128, 16), I32)
        rAf = p.sbuf("rAf", (128, 16), F32)
        rBf = p.sbuf("rBf", (128, 16), F32)
        rowf = p.sbuf("rowf", (128, 16), F32)
        rows_u = p.sbuf("rows_u", (128, 16), U32)
        negm = p.sbuf("negm", (128, 1), F32)
        w16 = p.sbuf("w16", (128, 16), F32)
        zc = p.sbuf("zc", (128, 1), F32)
        actc = p.sbuf("actc", (128, 8), F32)
        vg = [p.sbuf("vg%d" % i, (128, 128), BF16) for i in range(2)]
        acc = [p.sbuf("acc%d" % i, (128, 128), F32) for i in range(2)]
        mtr = p.sbuf("mtr", (128, 128), BF16)

        a8 = p.sbuf("a8", (8, 128), F32)
        for k in range(32):
            p.dma("sync", a8[:, :], act_s[:, k * 128:(k + 1) * 128])
            p.op("tensor", "transpose", psD[0:128, 0:8], a8[:, :],
                 eyef[0:8, 0:8])
            p.op("vector", "tensor_copy", actc[:, :], psD[0:128, 0:8])
            for hh in range(8):
                lhs_h = hT[:, hh * TPC + k * 128: hh * TPC + (k + 1) * 128]
                for (tbl, keys, vo, io_, col0) in (
                        (0, ka_sb, v8a, i8a, 0), (1, kb_sb, v8a, i8a, 16)):
                    p.op("tensor", "matmul", psA[:, :], lhs_h,
                         keys[:, hh * 512:(hh + 1) * 512], start=True, stop=True)
                    p.op("vector", "tensor_copy", sc_t[:, :], psA[:, :])
                    p.op("vector", "max", v8a[:, :], sc_t[:, :])
                    p.op("vector", "max_index", i8a[:, :], v8a[:, :], sc_t[:, :])
                    p.op("vector", "tensor_copy", stg[:, col0:col0 + 8], v8a[:, :])
                    p.op("vector", "tensor_copy", stg[:, 32 + col0:32 + col0 + 8],
                         i8a[:, :])
                    p.op("vector", "match_replace", mr_t[:, :], v8a[:, :],
                         sc_t[:, :], -1e30)
                    p.op("vector", "max", v8a[:, :], mr_t[:, :])
                    p.op("vector", "max_index", i8a[:, :], v8a[:, :], mr_t[:, :])
                    p.op("vector", "tensor_copy", stg[:, col0 + 8:col0 + 16],
                         v8a[:, :])
                    p.op("vector", "tensor_copy",
                         stg[:, 32 + col0 + 8:32 + col0 + 16], i8a[:, :])
                p.op("tensor", "transpose", psC[0:64, 0:128], stg[:, :],
                     eyef[:, :])
                p.op("vector", "tensor_copy", stgT[:, :], psC[0:64, 0:128])
                p.op("tensor", "matmul", psA[:, 0:256], stgT[0:32, :],
                     selSs[:, :], start=True, stop=True)
                p.op("vector", "tensor_copy", grid_s[:, :], psA[:, 0:256])
                p.op("tensor", "matmul", psB[:, 0:256], stgT[32:64, :],
                     selIDs[32:64, :], start=True, stop=True)
                p.op("vector", "tensor_copy", grid_idf[:, :], psB[:, 0:256])
                p.op("vector", "tensor_copy", gid_i[:, :], grid_idf[:, :])
                p.op("vector", "tensor_scalar", keyA[:, :],
                     grid_s[:, :].bitcast(I32), ~0x1FF, None,
                     op0=ALU.bitwise_and)
                p.op("vector", "tensor_scalar", tmpi[:, :], gid_i[:, :], 9,
                     None, op0=ALU.logical_shift_right)
                p.op("vector", "tensor_tensor", keyB[:, :], keyA[:, :],
                     tmpi[:, :], op=ALU.bitwise_or)
                p.op("vector", "tensor_scalar", tmpi[:, :], gid_i[:, :], 511,
                     None, op0=ALU.bitwise_and)
                p.op("vector", "tensor_tensor", keyA[:, :], keyA[:, :],
                     tmpi[:, :], op=ALU.bitwise_or)
                p.op("vector", "max", kA16[:, 0:8], keyB[:, :].bitcast(F32))
                p.op("vector", "match_replace", mr2[:, :], kA16[:, 0:8],
                     keyB[:, :].bitcast(F32), -1e30)
                p.op("vector", "max", kA16[:, 8:16], mr2[:, :])
                p.op("vector", "max", kB16[:, 0:8], keyA[:, :].bitcast(F32))
                p.op("vector", "match_replace", mr2[:, :], kB16[:, 0:8],
                     keyA[:, :].bitcast(F32), -1e30)
                p.op("vector", "max", kB16[:, 8:16], mr2[:, :])
                p.op("vector", "tensor_scalar", rAi[:, :],
                     kA16[:, :].bitcast(I32), 0x1FF, None, op0=ALU.bitwise_and)
                p.op("vector", "tensor_scalar", rBi[:, :],
                     kB16[:, :].bitcast(I32), 0x1FF, None, op0=ALU.bitwise_and)
                p.op("vector", "tensor_copy", rAf[:, :], rAi[:, :])
                p.op("vector", "tensor_copy", rBf[:, :], rBi[:, :])
                p.op("vector", "tensor_scalar", rowf[:, :], rAf[:, :], 512.0,
                     None, op0=ALU.mult)
                p.op("vector", "tensor_tensor", rowf[:, :], rowf[:, :],
                     rBf[:, :], op=ALU.add)
                p.op("vector", "tensor_copy", rows_u[:, :], rowf[:, :])
                p.op("vector", "tensor_scalar", negm[:, :], kA16[:, 0:1],
                     -1.0, None, op0=ALU.mult)
                p.op("scalar", "activation", w16[:, :], kA16[:, :], AF.Exp,
                     bias=negm[:, 0:1], accum_out=zc[:, 0:1])
                p.op("vector", "reciprocal", zc[:, :], zc[:, :])
                p.op("vector", "tensor_scalar", w16[:, :], w16[:, :],
                     zc[:, 0:1], None, op0=ALU.mult)
                p.op("vector", "tensor_scalar", w16[:, :], w16[:, :],
                     actc[:, hh:hh + 1], None, op0=ALU.mult)
                for s_ in range(16):
                    p.gather(vg[s_ % 2][:, :], evtab[:, :],
                             rows_u[:, s_:s_ + 1])
                    if s_ == 0:
                        p.op("vector", "tensor_scalar", acc[0][:, :],
                             vg[0][:, :], w16[:, 0:1], None, op0=ALU.mult)
                    else:
                        p.op("vector", "scalar_tensor_tensor",
                             acc[s_ % 2][:, :], vg[s_ % 2][:, :],
                             w16[:, s_:s_ + 1], acc[(s_ + 1) % 2][:, :],
                             op0=ALU.mult, op1=ALU.add)
                p.op("vector", "tensor_copy", mtr[:, :], acc[15 % 2][:, :])
                p.op("tensor", "transpose", psT[:, 0:128], mtr[:, :], eye[:, :])
                p.op("vector", "tensor_copy", mtr[:, :], psT[:, 0:128])
                p.dma("gpsimd", mT_s[hh * 128:(hh + 1) * 128,
                                     k * 128:(k + 1) * 128], mtr[:, :])

        with p.scope():
            w_sb = p.sbuf("w_sb2", (128, DC * DM), BF16)
            load_w_bf16(p, w_sb, woutT, DM)
            mbuf = p.sbuf("mbuf", (128, DC * 512), BF16)
            lnws = p.sbuf("lnws", (128, DC), F32)
            lnbs = p.sbuf("lnbs", (128, DC), F32)
            p.dma("sync", lnws[:, :], lnwt[:, :])
            p.dma("sync", lnbs[:, :], lnbt[:, :])
            onecol = misc["ones_col_bf"]
            sq = p.sbuf("sq2", (128, 512), BF16)
            st_mu = p.sbuf("st_mu2", (1, 512), F32)
            st_v = p.sbuf("st_v2", (1, 512), F32)
            st_r = p.sbuf("st_r2", (1, 512), F32)
            st_m2 = p.sbuf("st_m22", (1, 512), F32)
            rb = p.sbuf("rb2", (128, 512), F32)
            m2b = p.sbuf("m2b2", (128, 512), F32)
            pj = [p.sbuf("pj%d" % i, (128, DC * 512), F32) for i in range(1)]
            gb = p.sbuf("gb2", (128, 512), F32)
            xf1 = p.sbuf("xf12", (128, 512), F32)
            sf1 = p.sbuf("sf12", (128, 512), BF16)
            for tb in range(8):
                for kc in range(DC):
                    p.dma("sync", mbuf[:, kc * 512:(kc + 1) * 512],
                          mT_s[kc * 128:(kc + 1) * 128, tb * 512:(tb + 1) * 512])
                for oc in range(DC):
                    bank = [psA, psB][oc % 2]
                    for kc in range(DC):
                        p.op("tensor", "matmul", bank[:, :],
                             w_sb[:, kc * DM + oc * 128: kc * DM + (oc + 1) * 128],
                             mbuf[:, kc * 512:(kc + 1) * 512],
                             start=(kc == 0), stop=(kc == DC - 1), group=(kc > 0))
                    p.op("vector", "tensor_copy",
                         pj[0][:, oc * 512:(oc + 1) * 512], bank[:, :])
                onecf = p.constf(1.0)
                for oc in range(DC):
                    p.op("tensor", "matmul", psC[0:1, :], onecf[:, :],
                         pj[0][:, oc * 512:(oc + 1) * 512],
                         start=(oc == 0), stop=(oc == DC - 1), group=(oc > 0))
                for oc in range(DC):
                    p.op("scalar", "activation", sq[:, :],
                         pj[0][:, oc * 512:(oc + 1) * 512], AF.Square)
                    p.op("tensor", "matmul", psD[0:1, :], onecol[:, :], sq[:, :],
                         start=(oc == 0), stop=(oc == DC - 1), group=(oc > 0))
                p.op("vector", "tensor_scalar", st_mu[:, :], psC[0:1, :],
                     1.0 / DM, None, op0=ALU.mult)
                p.op("vector", "tensor_scalar", st_v[:, :], psD[0:1, :],
                     1.0 / DM, None, op0=ALU.mult)
                p.op("vector", "tensor_tensor", st_m2[:, :], st_mu[:, :],
                     st_mu[:, :], op=ALU.mult)
                p.op("vector", "tensor_tensor", st_v[:, :], st_v[:, :],
                     st_m2[:, :], op=ALU.subtract)
                p.op("scalar", "activation", st_v[:, :], st_v[:, :], AF.Ln,
                     bias=1e-5)
                p.op("scalar", "activation", st_r[:, :], st_v[:, :], AF.Exp,
                     scale=-0.5)
                p.op("vector", "tensor_tensor", st_m2[:, :], st_mu[:, :],
                     st_r[:, :], op=ALU.mult)
                p.op("tensor", "matmul", psC[:, :], misc["ones_b128_f"][:, :],
                     st_r[:, :], start=True, stop=True)
                p.op("vector", "tensor_copy", rb[:, :], psC[:, :])
                p.op("tensor", "matmul", psC[:, :], misc["ones_b128_f"][:, :],
                     st_m2[:, :], start=True, stop=True)
                p.op("vector", "tensor_copy", m2b[:, :], psC[:, :])
                for oc in range(DC):
                    sl_ = slice(oc * 512, (oc + 1) * 512)
                    p.op("vector", "tensor_tensor", gb[:, :], pj[0][:, sl_],
                         rb[:, :], op=ALU.mult)
                    p.op("vector", "tensor_tensor", gb[:, :], gb[:, :],
                         m2b[:, :], op=ALU.subtract)
                    p.op("vector", "tensor_scalar", gb[:, :], gb[:, :],
                         lnws[:, oc:oc + 1], lnbs[:, oc:oc + 1],
                         op0=ALU.mult, op1=ALU.add)
                    p.op("scalar", "activation", gb[:, :], gb[:, :], AF.Silu)
                    p.dma("sync", xf1[:, :],
                          x1[oc * 128:(oc + 1) * 128, tb * 512:(tb + 1) * 512])
                    p.dma("gpsimd", sf1[:, :],
                          s1[oc * 128:(oc + 1) * 128, tb * 512:(tb + 1) * 512])
                    p.op("vector", "tensor_tensor", gb[:, :], gb[:, :],
                         sf1[:, :], op=ALU.mult)
                    p.op("vector", "tensor_tensor", xf1[:, :], xf1[:, :],
                         gb[:, :], op=ALU.add)
                    p.dma("gpsimd",
                          x2_o[oc * 128:(oc + 1) * 128, tb * 512:(tb + 1) * 512],
                          xf1[:, :])

    if standalone:
        p.build(emit)
    else:
        emit(p)
    return p


def l1b_inmaps(inputs, xn1_all, x1_shards, s1_all):
    c = host_consts()
    selS = np.zeros((32, 256), np.float32)
    selID = np.zeros((32, 256), np.float32)
    for cc in range(256):
        i, j = cc >> 4, cc & 15
        selS[i, cc] = 1.0
        selS[16 + j, cc] = 1.0
        selID[i, cc] = 512.0
        selID[16 + j, cc] = 1.0
    lnw = inputs["L1_ffn_ln_w"].astype(np.float32)
    lnb = inputs["L1_ffn_ln_b"].astype(np.float32)
    base = {
        "winT": np.ascontiguousarray(inputs["L1_ffn_W_in"].T.astype(np.float32)),
        "woutT": np.ascontiguousarray(inputs["L1_ffn_W_out"].T.astype(np.float32)),
        "keysa": np.ascontiguousarray(
            inputs["L1_ffn_keys_a"].transpose(0, 2, 1).astype(np.float32)),
        "keysb": np.ascontiguousarray(
            inputs["L1_ffn_keys_b"].transpose(0, 2, 1).astype(np.float32)),
        "ushd": np.ascontiguousarray(
            inputs["L1_ffn_u_shared"].reshape(128, 1).astype(np.float32)),
        "evtab": np.ascontiguousarray(inputs["L1_ffn_expert_v"].astype(np.float32)),
        "selS": selS, "selID": selID,
        "lnwt": np.ascontiguousarray(lnw.reshape(8, 128).T),
        "lnbt": np.ascontiguousarray(lnb.reshape(8, 128).T),
        "eye_d": c["eye128"],
    }
    maps = []
    for ci in range(NCORE):
        m = dict(base)
        m["xn1"] = xn1_all[ci]
        m["x1"] = x1_shards[ci]
        m["s1"] = s1_all[ci]
        maps.append(m)
    return maps


def build_L2a(p=None):
    standalone = p is None
    if standalone:
        p = Prog("l2a")

    def emit(p):
        xt = p.dram("xt", (DM, TPC), F32, kind="ExternalInput").ap()
        wqT = p.dram("wqT", (DM, DM), F32, kind="ExternalInput").ap()
        wkT = p.dram("wkT", (DM, DM), F32, kind="ExternalInput").ap()
        wvT = p.dram("wvT", (DM, DM), F32, kind="ExternalInput").ap()
        wgT = p.dram("wgT", (DM, 8), F32, kind="ExternalInput").ap()
        eyed = p.dram("eye_d", (128, 128), F32, kind="ExternalInput").ap()

        xn_o = p.dram("xn_o", (DM, TPC), BF16, kind="ExternalOutput").ap()
        s2_o = p.dram("s2_o", (DM, TPC), BF16, kind="ExternalOutput").ap()
        sc_o = p.dram("sc_o", (8, TPC), F32, kind="ExternalOutput").ap()

        q_s = p.dram("q_s", (NSEQ * NH, 64, SEQ), BF16).ap()
        k_s = p.dram("k_s2", (NSEQ * NH, 64, SEQ), BF16).ap()
        vtok_s = p.dram("vtok_s2", (NSEQ * NH, SEQ, 65), BF16).ap()

        misc = emit_misc(p)
        psA, psB, psC, psD = misc["psA"], misc["psB"], misc["psC"], misc["psD"]
        psT = misc["psT"]
        orow = misc["ones_row_bf"]

        xnbuf = p.sbuf("xnbuf", (128, DC * TPC), BF16)
        with p.scope():
            emit_ln(p, xt, xnbuf, misc)
        for kc in range(DC):
            p.dma("scalar", xn_o[kc * 128:(kc + 1) * 128, :],
                  xnbuf[:, kc * TPC:(kc + 1) * TPC])

        w_sb = p.sbuf("w_sb", (128, DC * DM), BF16)
        eye = p.sbuf("eye_sb", (128, 128), BF16)
        p.dma("gpsimd", eye[:, :], eyed[:, :])
        qb = p.sbuf("qb", (128, 512), BF16)

        def cb_qk(dst):
            def cb(oc, tb, bank):
                b = tb // 4
                tloc = (tb % 4) * 512
                bh0 = b * 16 + 2 * oc
                p.op("vector", "tensor_copy", qb[:, :], bank[:, :])
                p.dma("gpsimd", dst[bh0, :, tloc:tloc + 512], qb[0:64, :])
                p.dma("gpsimd", dst[bh0 + 1, :, tloc:tloc + 512], qb[64:128, :])
            return cb
        load_w_bf16(p, w_sb, wqT, DM)
        emit_proj(p, w_sb, xnbuf, cb_qk(q_s), [psA, psB])
        load_w_bf16(p, w_sb, wkT, DM)
        emit_proj(p, w_sb, xnbuf, cb_qk(k_s), [psA, psB])

        vtstage = p.sbuf("vtstage", (128, 65), BF16)
        p.op("vector", "memset", vtstage[:, :], 1.0)
        vbf = p.sbuf("vbf", (128, 512), BF16)

        def cb_v(oc, tb, bank):
            b = tb // 4
            tloc = (tb % 4) * 512
            p.op("vector", "tensor_copy", vbf[:, :], bank[:, :])
            for hh in range(2):
                bh = b * 16 + 2 * oc + hh
                for j in range(4):
                    p.op("tensor", "transpose", psT[:, 0:64],
                         vbf[hh * 64:(hh + 1) * 64, j * 128:(j + 1) * 128],
                         eye[hh * 64:(hh + 1) * 64, hh * 64:(hh + 1) * 64])
                    p.op("vector", "tensor_copy", vtstage[:, 0:64], psT[:, 0:64])
                    p.dma("gpsimd",
                          vtok_s[bh, tloc + j * 128: tloc + (j + 1) * 128, :],
                          vtstage[:, :])
        load_w_bf16(p, w_sb, wvT, DM)
        emit_proj(p, w_sb, xnbuf, cb_v, [psA, psB])

        wg_sb = p.sbuf("wg_sb", (128, DC * 8), BF16)
        load_w_bf16(p, wg_sb, wgT, 8)
        lgf = p.sbuf("lgf", (8, 512), F32)
        for tb in range(8):
            for kc in range(DC):
                p.op("tensor", "matmul", psA[0:8, :],
                     wg_sb[:, kc * 8:(kc + 1) * 8],
                     xnbuf[:, kc * TPC + tb * 512: kc * TPC + (tb + 1) * 512],
                     start=(kc == 0), stop=(kc == DC - 1), group=(kc > 0))
            p.op("vector", "tensor_copy", lgf[:, :], psA[0:8, :])
            p.dma("sync", sc_o[:, tb * 512:(tb + 1) * 512], lgf[:, :])

        if p.iomap is not None:
            xntok2 = p.dram("xntok2", (TPC, DM), BF16).ap()
            xtr2 = p.sbuf("xtr2", (128, 128), BF16)
            for k in range(32):
                for kc in range(DC):
                    p.op("tensor", "transpose", psT[:, 0:128],
                         xnbuf[:, kc * TPC + k * 128: kc * TPC + (k + 1) * 128],
                         eye[:, :])
                    p.op("vector", "tensor_copy", xtr2[:, :], psT[:, 0:128])
                    p.dma("gpsimd",
                          xntok2[k * 128:(k + 1) * 128, kc * 128:(kc + 1) * 128],
                          xtr2[:, :])
            scg = p.dram("scg", (NCORE * 8, TPC), F32).ap()
            p.collective("AllGather", ALU.bypass,
                         [sc_o.tensor.ap().opt()], [scg.tensor.ap().opt()])
        q_t = p.sbuf("q_t", (64, SEQ), BF16)
        k_t = p.sbuf("k_t", (64, SEQ), BF16)
        vt_t = p.sbuf("vt_t", (128, 16 * 65), BF16)
        ptile = p.sbuf("ptile", (128, 512), BF16)
        zrow = p.sbuf("zrow", (1, 512), F32)
        zr = p.sbuf("zr", (1, 512), F32)
        zrb = p.sbuf("zrb", (128, 512), BF16)
        otile = p.sbuf("otile", (128, 512), BF16)
        for b in range(NSEQ):
            for h in range(NH):
                bh = b * 16 + h
                p.dma("sync", q_t[:, :], q_s[bh, :, :])
                p.dma("sync", k_t[:, :], k_s[bh, :, :])
                for j in range(16):
                    p.dma("scalar", vt_t[:, j * 65:(j + 1) * 65],
                          vtok_s[bh, j * 128:(j + 1) * 128, :])
                for tq in range(4):
                    for tkj in range(16):
                        p.op("tensor", "matmul", psA[:, :],
                             k_t[:, tkj * 128:(tkj + 1) * 128],
                             q_t[:, tq * 512:(tq + 1) * 512],
                             start=True, stop=True)
                        p.op("scalar", "activation", ptile[:, :], psA[:, :],
                             AF.Exp, scale=0.125, bias=-L2_SHIFT)
                        p.op("tensor", "matmul", psC[0:65, :],
                             vt_t[:, tkj * 65:(tkj + 1) * 65], ptile[:, :],
                             start=(tkj == 0), stop=(tkj == 15))
                    p.op("vector", "tensor_copy", zrow[:, :], psC[64:65, :])
                    p.op("vector", "reciprocal", zr[:, :], zrow[:, :])
                    p.op("tensor", "matmul", psD[:, :],
                         misc["ones_b128_f"][:, :], zr[:, :],
                         start=True, stop=True)
                    p.op("vector", "tensor_copy", zrb[:, :], psD[:, :])
                    p.op("vector", "tensor_copy", otile[:, :], psC[:, :])
                    p.op("vector", "tensor_tensor", otile[:, :], otile[:, :],
                         zrb[:, :], op=ALU.mult)
                    p.dma("gpsimd",
                          s2_o[h * 64:(h + 1) * 64,
                               b * 2048 + tq * 512: b * 2048 + (tq + 1) * 512],
                          otile[0:64, :])

    if standalone:
        p.build(emit)
    else:
        emit(p)
    return p


CAPL2 = 768
TRASH2 = 8 * CAPL2
CAP2 = 4096


def build_L2b(p=None):
    standalone = p is None
    if standalone:
        p = Prog("l2b")

    def emit(p):
        scf = p.dram("scf", (8, 256 * 128), F32, kind="ExternalInput").ap()
        scmy = p.dram("scmy", (8, TPC), F32, kind="ExternalInput").ap()
        xn_tok = p.dram("xn_tok", (TPC, DM), BF16, kind="ExternalInput").ap()
        xt = p.dram("xt", (DM, TPC), F32, kind="ExternalInput").ap()
        s2 = p.dram("s2", (DM, TPC), BF16, kind="ExternalInput").ap()
        w1T = p.dram("w1T", (8, DM, 2048), F32, kind="ExternalInput").ap()
        w3T = p.dram("w3T", (8, DM, 2048), F32, kind="ExternalInput").ap()
        w2T = p.dram("w2T", (8, 2048, DM), F32, kind="ExternalInput").ap()
        grp8 = p.dram("grp8", (128, 8), F32, kind="ExternalInput").ap()
        bc8 = p.dram("bc8", (8, 128), F32, kind="ExternalInput").ap()
        ecap = p.dram("ecap", (8, 1), F32, kind="ExternalInput").ap()
        eyed = p.dram("eye_d", (128, 128), F32, kind="ExternalInput").ap()

        x3_o = p.dram("x3_o", (DM, TPC), F32, kind="ExternalOutput").ap()

        disp = p.dram("disp2", (TRASH2 + 1, DM), BF16).ap()
        y_s = p.dram("y_s2", (TRASH2 + 1, DM), BF16).ap()

        misc = emit_misc(p)
        psA, psB, psC, psD = misc["psA"], misc["psB"], misc["psC"], misc["psD"]
        psT = misc["psT"]
        eye = p.sbuf("eye_sb", (128, 128), BF16)
        eyef = p.sbuf("eyef_sb", (128, 128), F32)
        p.dma("gpsimd", eye[:, :], eyed[:, :])
        p.dma("sync", eyef[:, :], eyed[:, :])

        slots_u = p.sbuf("slots_u", (128, 8 * 32), U32)
        wtok = p.sbuf("wtok", (128, 8 * 32), F32)
        xtok = p.sbuf("xtok", (128, DM), BF16)
        zrow_b = p.sbuf("zrow_b", (1, DM), BF16)
        rscope = p.scope()
        rscope.__enter__()
        SC = p.sbuf("SC", (128, 2048), F32)
        if p.iomap is None:
            p.dma("sync", SC[:, :], scf.rearrange("e (c t) -> (e c) t", c=16))
        else:
            for e_ in range(8):
                for r_ in range(NCORE):
                    p.dma("sync", SC[e_ * 16 + 2 * r_: e_ * 16 + 2 * r_ + 2, :],
                          scf[r_ * 8 + e_, :].rearrange("(u t) -> u t", u=2))
        grp = p.sbuf("grp", (128, 8), F32)
        bcg = p.sbuf("bcg", (8, 128), F32)
        p.dma("sync", grp[:, :], grp8[:, :])
        p.dma("sync", bcg[:, :], bc8[:, :])
        lo = p.sbuf("lo", (128, 1), F32)
        hi = p.sbuf("hi", (128, 1), F32)
        mid = p.sbuf("mid", (128, 1), F32)
        cntp = p.sbuf("cntp", (128, 1), F32)
        cnt8 = p.sbuf("cnt8", (8, 1), F32)
        dec8 = p.sbuf("dec8", (8, 1), F32)
        decb = p.sbuf("decb", (128, 1), F32)
        tmpb = p.sbuf("tmpb", (128, 1), F32)
        junk = p.sbuf("junk2", (128, 2048), F32)
        p.op("vector", "memset", lo[:, :], -60.0)
        p.op("vector", "memset", hi[:, :], 60.0)
        for it in range(46):
            p.op("vector", "tensor_tensor", mid[:, :], lo[:, :], hi[:, :],
                 op=ALU.add)
            p.op("vector", "tensor_scalar", mid[:, :], mid[:, :], 0.5, None,
                 op0=ALU.mult)
            p.op("vector", "tensor_scalar", junk[:, :], SC[:, :],
                 mid[:, 0:1], None, op0=ALU.is_ge, op1=ALU.add,
                 accum_out=cntp[:, 0:1])
            p.op("tensor", "matmul", psA[0:8, 0:1], grp[:, :], cntp[:, :],
                 start=True, stop=True)
            p.op("vector", "tensor_scalar", cnt8[:, :], psA[0:8, 0:1],
                 float(CAP2), None, op0=ALU.is_ge)
            p.op("tensor", "matmul", psB[:, 0:1], bcg[:, :], cnt8[:, :],
                 start=True, stop=True)
            p.op("vector", "tensor_copy", decb[:, :], psB[:, 0:1])
            p.op("vector", "tensor_tensor", tmpb[:, :], mid[:, :], lo[:, :],
                 op=ALU.subtract)
            p.op("vector", "scalar_tensor_tensor", tmpb[:, :], tmpb[:, :],
                 1.0, decb[:, :], op0=ALU.mult, op1=ALU.mult)
            p.op("vector", "tensor_tensor", lo[:, :], lo[:, :], tmpb[:, :],
                 op=ALU.add)
            p.op("vector", "tensor_tensor", tmpb[:, :], hi[:, :], mid[:, :],
                 op=ALU.subtract)
            p.op("vector", "scalar_tensor_tensor", tmpb[:, :], tmpb[:, :],
                 1.0, decb[:, :], op0=ALU.mult, op1=ALU.mult)
            p.op("vector", "tensor_tensor", hi[:, :], mid[:, :], tmpb[:, :],
                 op=ALU.add)
        mrow = p.sbuf("mrow", (128, 1), F32)
        p.op("vector", "tensor_scalar", junk[:, :], SC[:, :], 1.0,
             -1e30, op0=ALU.mult, op1=ALU.max, accum_out=mrow[:, 0:1])
        sh = p.sbuf("sh", (128, 1), F32)
        cur = mrow
        for off in (8, 4, 2, 1):
            mask = [(i ^ off) % 32 for i in range(32)]
            p.op("vector", "stream_shuffle", sh[:, :], cur[:, :], mask)
            p.op("vector", "tensor_tensor", mrow[:, :], cur[:, :], sh[:, :],
                 op=ALU.max)
            cur = mrow
        ex = p.sbuf("ex", (128, 2048), F32)
        msk = p.sbuf("msk", (128, 2048), F32)
        zp = p.sbuf("zp", (128, 1), F32)
        negm = p.sbuf("negm2", (128, 1), F32)
        p.op("vector", "tensor_scalar", negm[:, :], mrow[:, :], -1.0, None,
             op0=ALU.mult)
        p.op("scalar", "activation", ex[:, :], SC[:, :], AF.Exp,
             bias=negm[:, 0:1])
        p.op("vector", "tensor_scalar", msk[:, :], SC[:, :], lo[:, 0:1],
             None, op0=ALU.is_ge)
        p.op("vector", "scalar_tensor_tensor", junk[:, :], ex[:, :], 1.0,
             msk[:, :], op0=ALU.mult, op1=ALU.mult, accum_out=zp[:, 0:1])
        z8 = p.sbuf("z8", (8, 1), F32)
        p.op("tensor", "matmul", psA[0:8, 0:1], grp[:, :], zp[:, :],
             start=True, stop=True)
        p.op("vector", "tensor_copy", z8[:, :], psA[0:8, 0:1])
        th_s = p.dram("th_s", (128, 2), F32).ap()
        thm = p.sbuf("thm", (128, 2), F32)
        p.op("vector", "tensor_copy", thm[:, 0:1], lo[:, :])
        p.op("vector", "tensor_copy", thm[:, 1:2], mrow[:, :])
        p.dma("gpsimd", th_s[:, :], thm[:, :])
        th8 = p.sbuf("th8", (8, 2), F32)
        p.dma("sync", th8[:, :], th_s[:, :].rearrange("(e c) x -> e (c x)", c=16)[:, 0:2])

        SCm = p.sbuf("SCm", (8, TPC), F32)
        p.dma("sync", SCm[:, :], scmy[:, :])
        wrow = p.sbuf("wrow", (8, TPC), F32)
        mskm = p.sbuf("mskm", (8, TPC), F32)
        lposr = p.sbuf("lposr", (8, TPC), F32)
        slotr = p.sbuf("slotr", (8, TPC), F32)
        ones8 = p.sbuf("ones8", (8, TPC), F32)
        zi8 = p.sbuf("zi8", (8, 1), F32)
        nm8 = p.sbuf("nm8", (8, 1), F32)
        ecaps = p.sbuf("ecaps", (8, 1), F32)
        p.dma("sync", ecaps[:, :], ecap[:, :])
        p.op("vector", "memset", ones8[:, :], 1.0)
        p.op("vector", "reciprocal", zi8[:, :], z8[:, :])
        p.op("vector", "tensor_scalar", nm8[:, :], th8[:, 1:2], -1.0, None,
             op0=ALU.mult)
        p.op("scalar", "activation", wrow[:, :], SCm[:, :], AF.Exp,
             bias=nm8[:, 0:1])
        p.op("vector", "tensor_scalar", mskm[:, :], SCm[:, :], th8[:, 0:1],
             None, op0=ALU.is_ge)
        p.op("vector", "tensor_scalar", wrow[:, :], wrow[:, :], zi8[:, 0:1],
             None, op0=ALU.mult)
        p.op("vector", "tensor_tensor", wrow[:, :], wrow[:, :], mskm[:, :],
             op=ALU.mult)
        p.op("vector", "tensor_tensor_scan", lposr[:, :], ones8[:, :],
             mskm[:, :], 0.0, op0=ALU.mult, op1=ALU.add)
        p.op("vector", "tensor_scalar", slotr[:, :], lposr[:, :],
             float(CAPL2), None, op0=ALU.is_le)
        p.op("vector", "tensor_tensor", mskm[:, :], mskm[:, :], slotr[:, :],
             op=ALU.mult)
        p.op("vector", "tensor_scalar", slotr[:, :], lposr[:, :],
             ecaps[:, 0:1], None, op0=ALU.add)
        p.op("vector", "scalar_tensor_tensor", slotr[:, :], slotr[:, :],
             float(-TRASH2), mskm[:, :], op0=ALU.add, op1=ALU.mult)
        p.op("vector", "tensor_scalar", slotr[:, :], slotr[:, :],
             float(TRASH2), float(TRASH2), op0=ALU.add, op1=ALU.min)
        sw = p.sbuf("sw", (16, 128), F32)
        swT = p.sbuf("swT", (128, 16), F32)
        p.op("vector", "memset", zrow_b[:, :], 0.0)
        p.dma("gpsimd", y_s[TRASH2:TRASH2 + 1, :], zrow_b[:, :])
        for k in range(32):
            cs = slice(k * 128, (k + 1) * 128)
            p.op("tensor", "transpose", psD[0:128, 0:8], slotr[:, cs],
                 eyef[0:8, 0:8])
            p.op("vector", "tensor_copy", swT[:, 0:8], psD[0:128, 0:8])
            p.op("tensor", "transpose", psD[0:128, 0:8], wrow[:, cs],
                 eyef[0:8, 0:8])
            p.op("vector", "tensor_copy", swT[:, 8:16], psD[0:128, 0:8])
            for e in range(8):
                p.op("vector", "tensor_copy",
                     slots_u[:, e * 32 + k: e * 32 + k + 1], swT[:, e:e + 1])
                p.op("vector", "tensor_copy",
                     wtok[:, e * 32 + k: e * 32 + k + 1], swT[:, 8 + e:9 + e])
            p.dma("sync", xtok[:, :], xn_tok[cs, :])
            for e in range(8):
                p.scatter(disp[:, :], xtok[:, :],
                          slots_u[:, e * 32 + k: e * 32 + k + 1])

        rscope.__exit__(None, None, None)
        with p.scope():
            dT = p.sbuf("dT", (128, DC * CAPL2), BF16)
            h1 = p.sbuf("h1", (128, 16 * CAPL2), BF16)
            wfull = p.sbuf("wfull", (128, DC * 2048), BF16)
            ybf = p.sbuf("ybf", (128, 512), BF16)
            ytr = p.sbuf("ytr", (128, 128), BF16)
            NB2 = cdiv(CAPL2, 512)

            def nsl(nb):
                n0 = nb * 512
                return n0, min(CAPL2, n0 + 512)

            for e in range(8):
                for kc in range(DC):
                    p.dma("sync", dT[:, kc * CAPL2:(kc + 1) * CAPL2],
                          disp[e * CAPL2:(e + 1) * CAPL2,
                               kc * 128:(kc + 1) * 128], transpose=True)
                for kc in range(DC):
                    p.dma("gpsimd", wfull[:, kc * 2048:(kc + 1) * 2048],
                          w1T[e, kc * 128:(kc + 1) * 128, :])
                for oc in range(16):
                    for nb in range(NB2):
                        n0, n1 = nsl(nb)
                        for kc in range(DC):
                            p.op("tensor", "matmul", psA[:, 0:n1 - n0],
                                 wfull[:, kc * 2048 + oc * 128:
                                       kc * 2048 + (oc + 1) * 128],
                                 dT[:, kc * CAPL2 + n0: kc * CAPL2 + n1],
                                 start=(kc == 0), stop=(kc == DC - 1),
                                 group=(kc > 0))
                        p.op("scalar", "activation",
                             h1[:, oc * CAPL2 + n0: oc * CAPL2 + n1],
                             psA[:, 0:n1 - n0], AF.Silu)
                for kc in range(DC):
                    p.dma("gpsimd", wfull[:, kc * 2048:(kc + 1) * 2048],
                          w3T[e, kc * 128:(kc + 1) * 128, :])
                for oc in range(16):
                    for nb in range(NB2):
                        n0, n1 = nsl(nb)
                        for kc in range(DC):
                            p.op("tensor", "matmul", psA[:, 0:n1 - n0],
                                 wfull[:, kc * 2048 + oc * 128:
                                       kc * 2048 + (oc + 1) * 128],
                                 dT[:, kc * CAPL2 + n0: kc * CAPL2 + n1],
                                 start=(kc == 0), stop=(kc == DC - 1),
                                 group=(kc > 0))
                        p.op("vector", "tensor_tensor",
                             h1[:, oc * CAPL2 + n0: oc * CAPL2 + n1],
                             h1[:, oc * CAPL2 + n0: oc * CAPL2 + n1],
                             psA[:, 0:n1 - n0], op=ALU.mult)
                for hc in range(16):
                    p.dma("gpsimd", wfull[:, hc * 1024:(hc + 1) * 1024],
                          w2T[e, hc * 128:(hc + 1) * 128, :])
                for oc in range(DC):
                    for nb in range(NB2):
                        n0, n1 = nsl(nb)
                        for hc in range(16):
                            p.op("tensor", "matmul", psA[:, 0:n1 - n0],
                                 wfull[:, hc * 1024 + oc * 128:
                                       hc * 1024 + (oc + 1) * 128],
                                 h1[:, hc * CAPL2 + n0: hc * CAPL2 + n1],
                                 start=(hc == 0), stop=(hc == 15),
                                 group=(hc > 0))
                        p.op("vector", "tensor_copy", ybf[:, 0:n1 - n0],
                             psA[:, 0:n1 - n0])
                        for j in range(cdiv(n1 - n0, 128)):
                            p.op("tensor", "transpose", psT[:, 0:128],
                                 ybf[:, j * 128:(j + 1) * 128], eye[:, :])
                            p.op("vector", "tensor_copy", ytr[:, :],
                                 psT[:, 0:128])
                            p.dma("gpsimd",
                                  y_s[e * CAPL2 + n0 + j * 128:
                                      e * CAPL2 + n0 + (j + 1) * 128,
                                      oc * 128:(oc + 1) * 128], ytr[:, :])

        with p.scope():
            yg = [p.sbuf("yg%d" % i, (128, DM), BF16) for i in range(2)]
            acc = [p.sbuf("gacc%d" % i, (128, DM), F32) for i in range(2)]
            gsil = p.sbuf("gsil", (128, DM), BF16)
            gtr = p.sbuf("gtr", (128, 128), BF16)
            xf1 = p.sbuf("xf1", (128, 128), F32)
            sf1 = p.sbuf("sf1", (128, 128), BF16)
            p.op("vector", "memset", yg[0][:, :], 0.0)
            p.op("vector", "memset", yg[1][:, :], 0.0)
            for k in range(32):
                for e in range(8):
                    p.gather(yg[e % 2][:, :], y_s[:, :],
                             slots_u[:, e * 32 + k: e * 32 + k + 1])
                    if e == 0:
                        p.op("vector", "tensor_scalar", acc[0][:, :],
                             yg[0][:, :], wtok[:, k:k + 1], None, op0=ALU.mult)
                    else:
                        p.op("vector", "scalar_tensor_tensor",
                             acc[e % 2][:, :], yg[e % 2][:, :],
                             wtok[:, e * 32 + k: e * 32 + k + 1],
                             acc[(e + 1) % 2][:, :], op0=ALU.mult, op1=ALU.add)
                p.op("scalar", "activation", gsil[:, :], acc[7 % 2][:, :],
                     AF.Silu)
                for kc in range(DC):
                    p.op("tensor", "transpose", psT[:, 0:128],
                         gsil[:, kc * 128:(kc + 1) * 128], eye[:, :])
                    p.op("vector", "tensor_copy", gtr[:, :], psT[:, 0:128])
                    p.dma("sync", xf1[:, :],
                          xt[kc * 128:(kc + 1) * 128, k * 128:(k + 1) * 128])
                    p.dma("gpsimd", sf1[:, :],
                          s2[kc * 128:(kc + 1) * 128, k * 128:(k + 1) * 128])
                    p.op("vector", "tensor_tensor", gtr[:, :], gtr[:, :],
                         sf1[:, :], op=ALU.mult)
                    p.op("vector", "tensor_tensor", xf1[:, :], xf1[:, :],
                         gtr[:, :], op=ALU.add)
                    p.dma("gpsimd",
                          x3_o[kc * 128:(kc + 1) * 128, k * 128:(k + 1) * 128],
                          xf1[:, :])

    if standalone:
        p.build(emit)
    else:
        emit(p)
    return p


def l2_inmaps(inputs, x2_shards):
    c = host_consts()
    base = {
        "wqT": np.ascontiguousarray(inputs["L2_attn_Wq"].T.astype(np.float32)),
        "wkT": np.ascontiguousarray(inputs["L2_attn_Wk"].T.astype(np.float32)),
        "wvT": np.ascontiguousarray(inputs["L2_attn_Wv"].T.astype(np.float32)),
        "wgT": np.ascontiguousarray(inputs["L2_ffn_Wg"].T.astype(np.float32)),
        "eye_d": c["eye128"],
    }
    maps = []
    for ci in range(NCORE):
        m = dict(base)
        m["xt"] = x2_shards[ci]
        maps.append(m)
    return maps


def l2b_inmaps(inputs, scores_full, res2a, x2_shards):
    c = host_consts()
    grp8 = np.zeros((128, 8), np.float32)
    for pp in range(128):
        grp8[pp, pp // 16] = 1.0
    bc8 = np.ascontiguousarray(grp8.T)
    ecap = np.ascontiguousarray(
        (np.arange(8, dtype=np.float32) * CAPL2 - 1.0).reshape(8, 1))
    base = {
        "scf": np.ascontiguousarray(scores_full.astype(np.float32)),
        "w1T": np.ascontiguousarray(
            inputs["L2_ffn_W1"].transpose(0, 2, 1).astype(np.float32)),
        "w3T": np.ascontiguousarray(
            inputs["L2_ffn_W3"].transpose(0, 2, 1).astype(np.float32)),
        "w2T": np.ascontiguousarray(
            inputs["L2_ffn_W2"].transpose(0, 2, 1).astype(np.float32)),
        "grp8": grp8, "bc8": bc8, "ecap": ecap, "eye_d": c["eye128"],
    }
    maps = []
    for ci in range(NCORE):
        m = dict(base)
        m["scmy"] = np.ascontiguousarray(
            scores_full[:, ci * TPC:(ci + 1) * TPC].astype(np.float32))
        m["xn_tok"] = np.ascontiguousarray(np.asarray(res2a[ci]["xn_o"]).T)
        m["xt"] = x2_shards[ci]
        m["s2"] = res2a[ci]["s2_o"]
        maps.append(m)
    return maps


RND_MAGIC = 1.5 * (2.0 ** 23)


def build_L3(p=None):
    standalone = p is None
    if standalone:
        p = Prog("l3")

    def emit(p):
        xt = p.dram("xt", (DM, TPC), F32, kind="ExternalInput").ap()
        wqkvT = p.dram("wqkvT", (DM, 3 * DM), F32, kind="ExternalInput").ap()
        woT = p.dram("woT", (DM, DM), F32, kind="ExternalInput").ap()
        wff = p.dram("wff", (DM, DM), F32, kind="ExternalInput").ap()
        bff = p.dram("bff", (128, DC), F32, kind="ExternalInput").ap()
        bc2d = p.dram("bc2d", (2, 128), F32, kind="ExternalInput").ap()
        eyed = p.dram("eye_d", (128, 128), F32, kind="ExternalInput").ap()

        x4_o = p.dram("x4_o", (DM, TPC), F32, kind="ExternalOutput").ap()

        qf_s = p.dram("qf_s", (DM, TPC), BF16).ap()
        ktok_s = p.dram("ktok_s", (TPC, DM), BF16).ap()
        vtok_s = p.dram("vtok_s3", (TPC, DM), BF16).ap()
        kf_s = p.dram("kf_s3", (DM, TPC), BF16).ap()
        s3_s = p.dram("s3_s", (DM, TPC), BF16).ap()
        wq_s = p.dram("wq_s", (DM, DM), BF16).ap()

        misc = emit_misc(p)
        psA, psB, psC, psD = misc["psA"], misc["psB"], misc["psC"], misc["psD"]
        psT = misc["psT"]
        onesb2 = p.sbuf("onesb2", (128, 2), BF16)
        p.op("vector", "memset", onesb2[:, :], 0.0)
        p.op("vector", "memset", onesb2[0:64, 0:1], 1.0)
        p.op("vector", "memset", onesb2[64:128, 1:2], 1.0)
        eye = p.sbuf("eye_sb", (128, 128), BF16)
        p.dma("gpsimd", eye[:, :], eyed[:, :])
        bc2s = p.sbuf("bc2s", (2, 128), F32)
        p.dma("sync", bc2s[:, :], bc2d[:, :])
        for v in (1e-30, 1.0 / 7.5, RND_MAGIC, -RND_MAGIC):
            p.constf(v)

        xnbuf = p.sbuf("xnbuf", (128, DC * TPC), BF16)
        with p.scope():
            emit_ln(p, xt, xnbuf, misc)

        with p.scope():
            w_sb = p.sbuf("w_sb", (128, DC * DM), BF16)
            qb = p.sbuf("qb3", (128, 512), BF16)
            eqb = p.sbuf("eqb", (128, 512), BF16)
            nrm = p.sbuf("nrm3", (2, 512), F32)
            vtstage = p.sbuf("vtstage3", (128, 128), BF16)
            for part in range(3):
                for kc in range(DC):
                    p.dma("gpsimd", w_sb[:, kc * DM:(kc + 1) * DM],
                          wqkvT[kc * 128:(kc + 1) * 128,
                                part * DM:(part + 1) * DM])

                def cb3(oc, tb, bank, part=part):
                    if part == 0:
                        p.op("scalar", "activation", eqb[:, :], bank[:, :],
                             AF.Exp)
                        p.op("tensor", "matmul", psC[0:2, 0:512], onesb2[:, :],
                             eqb[:, :], start=True, stop=True)
                        p.op("vector", "tensor_copy", nrm[:, :], psC[0:2, 0:512])
                        p.op("vector", "reciprocal", nrm[:, :], nrm[:, :])
                        p.op("vector", "tensor_scalar", nrm[:, :], nrm[:, :],
                             0.125, None, op0=ALU.mult)
                        p.op("tensor", "matmul", psD[:, 0:512], bc2s[:, :],
                             nrm[:, :], start=True, stop=True)
                        p.op("vector", "tensor_tensor", qb[:, :], eqb[:, :],
                             psD[:, 0:512], op=ALU.mult)
                        p.dma("gpsimd", qf_s[oc * 128:(oc + 1) * 128,
                                             tb * 512:(tb + 1) * 512], qb[:, :])
                    else:
                        if part == 1:
                            p.op("scalar", "activation", qb[:, :], bank[:, :],
                                 AF.Exp)
                            p.dma("gpsimd", kf_s[oc * 128:(oc + 1) * 128,
                                                 tb * 512:(tb + 1) * 512],
                                  qb[:, :])
                        else:
                            p.op("vector", "tensor_copy", qb[:, :], bank[:, :])
                        dst = ktok_s if part == 1 else vtok_s
                        for j in range(4):
                            for hh in range(2):
                                p.op("tensor", "transpose", psT[:, 0:64],
                                     qb[hh * 64:(hh + 1) * 64,
                                        j * 128:(j + 1) * 128],
                                     eye[hh * 64:(hh + 1) * 64,
                                         hh * 64:(hh + 1) * 64])
                                p.op("vector", "tensor_copy",
                                     vtstage[:, hh * 64:(hh + 1) * 64],
                                     psT[:, 0:64])
                            p.dma("gpsimd",
                                  dst[tb * 512 + j * 128: tb * 512 + (j + 1) * 128,
                                      oc * 128:(oc + 1) * 128], vtstage[:, :])
                emit_proj(p, w_sb, xnbuf, cb3, [psA, psB])

        BKT = 64
        NBK = SEQ // BKT
        with p.scope():
            qfh = p.sbuf("qfh", (64, SEQ), BF16)
            kfh = p.sbuf("kfh", (64, SEQ), BF16)
            kth = p.sbuf("kth", (128, 16 * 64), BF16)
            vth = p.sbuf("vth", (128, 16 * 64), BF16)
            bks = p.sbuf("bks", (64, NBK), F32)
            bkc = p.sbuf("bkc", (64, NBK), F32)
            onesr = p.sbuf("onesr", (64, NBK), F32)
            denr = p.sbuf("denr", (1, SEQ), F32)
            dtmp = p.sbuf("dtmp", (64, 64), F32)
            ctxa = p.sbuf("ctxa", (64, 64), F32)
            ctxb = p.sbuf("ctxb", (64, 64), BF16)
            qsc = p.sbuf("qsc", (64, SEQ), BF16)
            obuf3 = p.sbuf("obuf3", (128, DC * SEQ), BF16)
            sbf3 = p.sbuf("sbf3", (128, 512), BF16)
            onecf = p.constf(1.0)
            p.op("vector", "memset", onesr[:, :], 1.0)
            w_sb2 = p.sbuf("w_sb2", (128, DC * DM), BF16)
            load_w_bf16(p, w_sb2, woT, DM)
            for b in range(NSEQ):
                for h in range(NH):
                    kc_, rr = h // 2, (h % 2) * 64
                    p.dma("sync", qfh[:, :],
                          qf_s[h * 64:(h + 1) * 64, b * SEQ:(b + 1) * SEQ])
                    p.dma("sync", kfh[:, :],
                          kf_s[h * 64:(h + 1) * 64, b * SEQ:(b + 1) * SEQ])
                    for j in range(16):
                        p.dma("sync", kth[:, j * 64:(j + 1) * 64],
                              ktok_s[b * SEQ + j * 128: b * SEQ + (j + 1) * 128,
                                     h * 64:(h + 1) * 64])
                        p.dma("sync", vth[:, j * 64:(j + 1) * 64],
                              vtok_s[b * SEQ + j * 128: b * SEQ + (j + 1) * 128,
                                     h * 64:(h + 1) * 64])
                    p.op("vector", "tensor_reduce", bks[:, :],
                         kfh[:, :].rearrange("p (n s) -> p n s", s=BKT),
                         axis=AX.X, op=ALU.add)
                    p.op("vector", "tensor_tensor_scan", bkc[:, :],
                         onesr[:, :], bks[:, :], 0.0, op0=ALU.mult, op1=ALU.add)
                    p.op("vector", "memset", denr[:, 0:BKT], 1.0)
                    for n in range(1, NBK):
                        p.op("vector", "tensor_scalar", dtmp[:, :],
                             qfh[:, n * BKT:(n + 1) * BKT], bkc[:, n - 1:n],
                             None, op0=ALU.mult)
                        p.op("tensor", "matmul", psA[0:1, 0:64],
                             onecf[0:64, :], dtmp[:, :], start=True, stop=True)
                        p.op("vector", "tensor_copy",
                             denr[:, n * BKT:(n + 1) * BKT], psA[0:1, 0:64])
                    p.op("vector", "tensor_scalar", denr[:, :], denr[:, :],
                         1e-6, None, op0=ALU.max)
                    p.op("scalar", "activation", denr[:, :], denr[:, :], AF.Ln)
                    p.op("scalar", "activation", denr[:, :], denr[:, :],
                         AF.Exp, scale=-1.0)
                    for ts in range(4):
                        p.op("tensor", "matmul", psD[0:64, 0:512],
                             misc["ones_b128_f"][:, 0:64],
                             denr[:, ts * 512:(ts + 1) * 512],
                             start=True, stop=True)
                        p.op("vector", "tensor_tensor",
                             qsc[:, ts * 512:(ts + 1) * 512],
                             qfh[:, ts * 512:(ts + 1) * 512], psD[0:64, 0:512],
                             op=ALU.mult)
                    p.op("vector", "memset", ctxa[:, :], 0.0)
                    for n in range(NBK):
                        p.op("vector", "tensor_copy", ctxb[:, :], ctxa[:, :])
                        p.op("tensor", "matmul", psB[0:64, 0:64], ctxb[:, :],
                             qsc[:, n * BKT:(n + 1) * BKT],
                             start=True, stop=True)
                        p.op("vector", "tensor_copy",
                             obuf3[rr:rr + 64,
                                   kc_ * SEQ + n * BKT: kc_ * SEQ + (n + 1) * BKT],
                             psB[0:64, 0:64])
                        jj, roff = n // 2, (n % 2) * 64
                        p.op("tensor", "matmul", psC[0:64, 0:64],
                             kth[roff:roff + 64, jj * 64:(jj + 1) * 64],
                             vth[roff:roff + 64, jj * 64:(jj + 1) * 64],
                             start=True, stop=True)
                        p.op("vector", "tensor_tensor", ctxa[:, :], ctxa[:, :],
                             psC[0:64, 0:64], op=ALU.add)
                for oc in range(DC):
                    for tb in range(4):
                        bank = [psA, psB][tb % 2]
                        for kc in range(DC):
                            p.op("tensor", "matmul", bank[:, :],
                                 w_sb2[:, kc * DM + oc * 128: kc * DM + (oc + 1) * 128],
                                 obuf3[:, kc * SEQ + tb * 512: kc * SEQ + (tb + 1) * 512],
                                 start=(kc == 0), stop=(kc == DC - 1),
                                 group=(kc > 0))
                        p.op("vector", "tensor_copy", sbf3[:, :], bank[:, :])
                        p.dma("gpsimd",
                              s3_s[oc * 128:(oc + 1) * 128,
                                   b * SEQ + tb * 512: b * SEQ + (tb + 1) * 512],
                              sbf3[:, :])

        def emit_quant(src_ap_fn, dst_fn, ntiles, width, partition_blocks):
            wt = p.sbuf("qn_wt", (128, width), F32)
            ab = p.sbuf("qn_ab", (128, width), F32)
            sc = p.sbuf("qn_sc", (128, width // 32), F32)
            scb = p.sbuf("qn_scb", (128, width), F32)
            yq = p.sbuf("qn_yq", (128, width), F32)
            ei = p.sbuf("qn_ei", (128, width), I32)
            e2 = p.sbuf("qn_e2", (128, width), I32)
            stp = p.sbuf("qn_stp", (128, width), F32)
            sti = p.sbuf("qn_sti", (128, width), F32)
            ob = p.sbuf("qn_ob", (128, width), BF16)
            for ti in range(ntiles):
                p.dma("sync", wt[:, :], src_ap_fn(ti))
                if partition_blocks:
                    p.op("scalar", "activation", ab[:, :], wt[:, :], AF.Abs)
                    for off in (16, 8, 4, 2, 1):
                        mask = [(i ^ off) % 32 for i in range(32)]
                        p.op("vector", "stream_shuffle", scb[:, :], ab[:, :],
                             mask)
                        p.op("vector", "tensor_tensor", ab[:, :], ab[:, :],
                             scb[:, :], op=ALU.max)
                    p.op("vector", "tensor_scalar", scb[:, :], ab[:, :],
                         1.0 / 7.5, 1e-30, op0=ALU.mult, op1=ALU.max)
                else:
                    p.op("vector", "tensor_reduce", sc[:, :],
                         wt[:, :].rearrange("p (b i) -> p b i", i=32),
                         axis=AX.X, op=ALU.max, apply_absolute_value=True)
                    p.op("vector", "tensor_scalar", sc[:, :], sc[:, :],
                         1.0 / 7.5, 1e-30, op0=ALU.mult, op1=ALU.max)
                    p.op("vector", "tensor_copy",
                         scb[:, :].rearrange("p (b i) -> p b i", i=32),
                         sc[:, :].rearrange("p (b o) -> p b o", o=1)
                         .broadcast_to((128, width // 32, 32)))
                p.op("scalar", "activation", sti[:, :], scb[:, :], AF.Ln)
                p.op("scalar", "activation", sti[:, :], sti[:, :], AF.Exp,
                     scale=-1.0)
                p.op("vector", "tensor_tensor", yq[:, :], wt[:, :], sti[:, :],
                     op=ALU.mult)
                p.op("vector", "tensor_scalar", ei[:, :],
                     yq[:, :].bitcast(I32), 0x7FFFFFFF, None,
                     op0=ALU.bitwise_and)
                p.op("vector", "tensor_scalar", ei[:, :], ei[:, :], 23, None,
                     op0=ALU.logical_shift_right)
                p.op("vector", "tensor_scalar", ei[:, :], ei[:, :], -3.0,
                     124.0, op0=ALU.add, op1=ALU.max)
                p.op("vector", "tensor_scalar", e2[:, :], ei[:, :], -1.0,
                     254.0, op0=ALU.mult, op1=ALU.add)
                p.op("vector", "tensor_scalar", ei[:, :], ei[:, :], 23, None,
                     op0=ALU.logical_shift_left)
                p.op("vector", "tensor_scalar", e2[:, :], e2[:, :], 23, None,
                     op0=ALU.logical_shift_left)
                p.op("vector", "tensor_copy", stp[:, :], ei[:, :].bitcast(F32))
                p.op("vector", "tensor_copy", sti[:, :], e2[:, :].bitcast(F32))
                p.op("vector", "tensor_tensor", yq[:, :], yq[:, :], sti[:, :],
                     op=ALU.mult)
                p.op("vector", "tensor_scalar", yq[:, :], yq[:, :],
                     RND_MAGIC, -RND_MAGIC, op0=ALU.add, op1=ALU.add)
                p.op("vector", "tensor_tensor", yq[:, :], yq[:, :], stp[:, :],
                     op=ALU.mult)
                p.op("vector", "tensor_tensor", yq[:, :], yq[:, :], scb[:, :],
                     op=ALU.mult)
                p.op("vector", "tensor_copy", ob[:, :], yq[:, :])
                dst_fn(ti, ob)

        with p.scope():
            def wsrc(ti):
                return wff[ti * 128:(ti + 1) * 128, :]

            def wdst(ti, ob):
                p.dma("gpsimd", wq_s[ti * 128:(ti + 1) * 128, :], ob[:, :])
            emit_quant(wsrc, wdst, DC, DM, False)

        xqbuf = p.sbuf("xqbuf", (128, DC * TPC), BF16)
        with p.scope():
            def xsrc(ti):
                kc, tb = ti // 8, ti % 8
                return None

            xw = p.sbuf("xw_t", (128, 512), F32)
            qn_tiles = []
            def xsrc2(ti):
                return None
            wt32 = p.sbuf("qn2_wt", (128, 512), F32)
            ab = p.sbuf("qn2_ab", (128, 512), F32)
            scb = p.sbuf("qn2_scb", (128, 512), F32)
            yq = p.sbuf("qn2_yq", (128, 512), F32)
            ei = p.sbuf("qn2_ei", (128, 512), I32)
            e2 = p.sbuf("qn2_e2", (128, 512), I32)
            stp = p.sbuf("qn2_stp", (128, 512), F32)
            sti = p.sbuf("qn2_sti", (128, 512), F32)
            for kc in range(DC):
                for tb in range(8):
                    sl_ = slice(kc * TPC + tb * 512, kc * TPC + (tb + 1) * 512)
                    p.op("vector", "tensor_copy", wt32[:, :], xnbuf[:, sl_])
                    p.op("scalar", "activation", ab[:, :], wt32[:, :], AF.Abs)
                    for off in (16, 8, 4, 2, 1):
                        mask = [(i ^ off) % 32 for i in range(32)]
                        p.op("vector", "stream_shuffle", scb[:, :], ab[:, :],
                             mask)
                        p.op("vector", "tensor_tensor", ab[:, :], ab[:, :],
                             scb[:, :], op=ALU.max)
                    p.op("vector", "tensor_scalar", scb[:, :], ab[:, :],
                         1.0 / 7.5, 1e-30, op0=ALU.mult, op1=ALU.max)
                    p.op("scalar", "activation", sti[:, :], scb[:, :], AF.Ln)
                    p.op("scalar", "activation", sti[:, :], sti[:, :], AF.Exp,
                         scale=-1.0)
                    p.op("vector", "tensor_tensor", yq[:, :], wt32[:, :],
                         sti[:, :], op=ALU.mult)
                    p.op("vector", "tensor_scalar", ei[:, :],
                         yq[:, :].bitcast(I32), 0x7FFFFFFF, None,
                         op0=ALU.bitwise_and)
                    p.op("vector", "tensor_scalar", ei[:, :], ei[:, :], 23,
                         None, op0=ALU.logical_shift_right)
                    p.op("vector", "tensor_scalar", ei[:, :], ei[:, :], -3.0,
                         124.0, op0=ALU.add, op1=ALU.max)
                    p.op("vector", "tensor_scalar", e2[:, :], ei[:, :], -1.0,
                         254.0, op0=ALU.mult, op1=ALU.add)
                    p.op("vector", "tensor_scalar", ei[:, :], ei[:, :], 23,
                         None, op0=ALU.logical_shift_left)
                    p.op("vector", "tensor_scalar", e2[:, :], e2[:, :], 23,
                         None, op0=ALU.logical_shift_left)
                    p.op("vector", "tensor_copy", stp[:, :],
                         ei[:, :].bitcast(F32))
                    p.op("vector", "tensor_copy", sti[:, :],
                         e2[:, :].bitcast(F32))
                    p.op("vector", "tensor_tensor", yq[:, :], yq[:, :],
                         sti[:, :], op=ALU.mult)
                    p.op("vector", "tensor_scalar", yq[:, :], yq[:, :],
                         RND_MAGIC, -RND_MAGIC, op0=ALU.add, op1=ALU.add)
                    p.op("vector", "tensor_tensor", yq[:, :], yq[:, :],
                         stp[:, :], op=ALU.mult)
                    p.op("vector", "tensor_tensor", yq[:, :], yq[:, :],
                         scb[:, :], op=ALU.mult)
                    p.op("vector", "tensor_copy", xqbuf[:, sl_], yq[:, :])

        with p.scope():
            w_sb3 = p.sbuf("w_sb3", (128, DC * DM), BF16)
            for kc in range(DC):
                p.dma("sync", w_sb3[:, kc * DM:(kc + 1) * DM],
                      wq_s[:, kc * 128:(kc + 1) * 128], transpose=True)
            bffs = p.sbuf("bffs", (128, DC), F32)
            p.dma("sync", bffs[:, :], bff[:, :])
            gsil = p.sbuf("gsil3", (128, 512), BF16)
            xf4 = p.sbuf("xf4", (128, 512), F32)
            sf4 = p.sbuf("sf4", (128, 512), BF16)

            def cb4(oc, tb, bank):
                p.op("scalar", "activation", gsil[:, :], bank[:, :], AF.Silu,
                     bias=bffs[:, oc:oc + 1])
                p.dma("sync", xf4[:, :],
                      xt[oc * 128:(oc + 1) * 128, tb * 512:(tb + 1) * 512])
                p.dma("gpsimd", sf4[:, :],
                      s3_s[oc * 128:(oc + 1) * 128, tb * 512:(tb + 1) * 512])
                p.op("vector", "tensor_tensor", gsil[:, :], gsil[:, :],
                     sf4[:, :], op=ALU.mult)
                p.op("vector", "tensor_tensor", xf4[:, :], xf4[:, :],
                     gsil[:, :], op=ALU.add)
                p.dma("gpsimd",
                      x4_o[oc * 128:(oc + 1) * 128, tb * 512:(tb + 1) * 512],
                      xf4[:, :])
            emit_proj(p, w_sb3, xqbuf, cb4, [psA, psB])

    if standalone:
        p.build(emit)
    else:
        emit(p)
    return p


def l3_inmaps(inputs, x3_shards):
    c = l1_host_consts()
    base = {
        "wqkvT": np.ascontiguousarray(inputs["L3_attn_W_qkv"].T.astype(np.float32)),
        "woT": np.ascontiguousarray(inputs["L3_attn_W_out"].T.astype(np.float32)),
        "wff": np.ascontiguousarray(inputs["L3_ffn_weight"].astype(np.float32)),
        "bff": np.ascontiguousarray(
            inputs["L3_ffn_bias"].astype(np.float32).reshape(8, 128).T),
        "bc2": None, "bc2d": c["bc2"], "eye_d": c["eye128"],
    }
    del base["bc2"]
    maps = []
    for ci in range(NCORE):
        m = dict(base)
        m["xt"] = x3_shards[ci]
        maps.append(m)
    return maps


_PROGS = {}


def _prog(name, builder):
    if name not in _PROGS:
        _PROGS[name] = builder()
    return _PROGS[name]


FUSED = True


def kernel(**inputs):
    if FUSED:
        return kernel_fused(**inputs)
    return kernel_multilaunch(**inputs)


def kernel_multilaunch(**inputs):
    import ml_dtypes
    cores = list(range(NCORE))
    x = np.asarray(inputs["L0_attn_x"], np.float32)
    shards = shard_x(x)

    pa = _prog("l0a", build_L0a)
    r0a = run_bass_kernel_spmd(pa.nc, l0a_inmaps(inputs, shards), core_ids=cores)
    logits_full = np.concatenate(
        [np.asarray(r0a.results[ci]["lg_o"]).T for ci in range(NCORE)], axis=0)

    b1t, b2t, iota8, tril = moe_host_consts(inputs)
    pb1 = _prog("l0b1", build_L0b1)
    m1 = {"lg": np.ascontiguousarray(logits_full), "iota8": iota8, "tril": tril}
    rb1 = run_bass_kernel_spmd(pb1.nc, [m1] * NCORE, core_ids=cores)
    slots = np.asarray(rb1.results[0]["slots_o"])
    wk = np.asarray(rb1.results[0]["wk_o"])

    c = host_consts()
    pb2 = _prog("l0b2", build_L0b2)
    maps = []
    for ci in range(NCORE):
        cols = [kk * 256 + ci * 32 + k for kk in range(2) for k in range(32)]
        maps.append({
            "xn_tok": np.ascontiguousarray(np.asarray(r0a.results[ci]["xn_o"]).T),
            "xt": shards[ci],
            "s0": np.asarray(r0a.results[ci]["s0_o"]),
            "w1": np.ascontiguousarray(inputs["L0_ffn_w1"].astype(np.float32)),
            "w2": np.ascontiguousarray(inputs["L0_ffn_w2"].astype(np.float32)),
            "b1t": b1t, "b2t": b2t, "eye_d2": c["eye128"],
            "myslots": np.ascontiguousarray(slots[:, cols]),
            "myw": np.ascontiguousarray(wk[:, cols]),
        })
    rb2 = run_bass_kernel_spmd(pb2.nc, maps, core_ids=cores)
    x1_shards = [np.ascontiguousarray(np.asarray(rb2.results[ci]["x1_o"]))
                 for ci in range(NCORE)]

    ppre = _prog("l1a_pre", build_L1a_pre)
    rpre = run_bass_kernel_spmd(ppre.nc, l1a_pre_inmaps(inputs, x1_shards),
                                core_ids=cores)
    cst = l1_host_consts()
    pscan = _prog("l1a_scan", build_L1a_scan)
    of_all = [np.zeros((DM, TPC), ml_dtypes.bfloat16) for _ in range(NCORE)]
    for b in range(NSEQ):
        smaps = []
        for ci in range(NCORE):
            r = rpre.results[ci]
            sl = slice(b * SEQ, (b + 1) * SEQ)
            smaps.append({
                "qf": np.ascontiguousarray(np.asarray(r["qf_o"])[:, sl]),
                "kf": np.ascontiguousarray(np.asarray(r["kf_o"])[:, sl]),
                "qt": np.ascontiguousarray(np.asarray(r["qt_o"])[sl, :]),
                "kt": np.ascontiguousarray(np.asarray(r["kt_o"])[sl, :]),
                "vt": np.ascontiguousarray(np.asarray(r["vt_o"])[sl, :]),
                "la": np.ascontiguousarray(np.asarray(r["la_o"])[:, sl]),
                "be": np.ascontiguousarray(np.asarray(r["be_o"])[:, sl]),
                "masksu": cst["masksu"], "masksl": cst["masksl"],
                "eye_d": cst["eye128"],
            })
        rs = run_bass_kernel_spmd(pscan.nc, smaps, core_ids=cores)
        for ci in range(NCORE):
            of_all[ci][:, b * SEQ:(b + 1) * SEQ] = np.asarray(
                rs.results[ci]["of_o"])
    pout = _prog("l1a_out", build_L1a_out)
    rout = run_bass_kernel_spmd(
        pout.nc, l1a_out_inmaps(inputs, [rpre.results[i] for i in range(NCORE)],
                                of_all), core_ids=cores)

    plb = _prog("l1b", build_L1b)
    xn1_all = [np.asarray(rpre.results[ci]["xn_o"]) for ci in range(NCORE)]
    s1_all = [np.asarray(rout.results[ci]["s1_o"]) for ci in range(NCORE)]
    rlb = run_bass_kernel_spmd(plb.nc, l1b_inmaps(inputs, xn1_all, x1_shards,
                                                  s1_all), core_ids=cores)
    x2_shards = [np.ascontiguousarray(np.asarray(rlb.results[ci]["x2_o"]))
                 for ci in range(NCORE)]

    p2a = _prog("l2a", build_L2a)
    r2a = run_bass_kernel_spmd(p2a.nc, l2_inmaps(inputs, x2_shards),
                               core_ids=cores)
    scores_full = np.concatenate(
        [np.asarray(r2a.results[ci]["sc_o"]) for ci in range(NCORE)], axis=1)
    p2b = _prog("l2b", build_L2b)
    r2b = run_bass_kernel_spmd(
        p2b.nc, l2b_inmaps(inputs, scores_full,
                           [r2a.results[i] for i in range(NCORE)], x2_shards),
        core_ids=cores)
    x3_shards = [np.ascontiguousarray(np.asarray(r2b.results[ci]["x3_o"]))
                 for ci in range(NCORE)]

    p3 = _prog("l3", build_L3)
    r3 = run_bass_kernel_spmd(p3.nc, l3_inmaps(inputs, x3_shards),
                              core_ids=cores)

    out = np.empty((16, SEQ, DM), np.float32)
    for ci in range(NCORE):
        x4 = np.asarray(r3.results[ci]["x4_o"])
        out[ci * NSEQ:(ci + 1) * NSEQ] = x4.T.reshape(NSEQ, SEQ, DM)
    return out


def _np2dt(a):
    import ml_dtypes
    if a.dtype == np.float32:
        return F32
    if a.dtype == ml_dtypes.bfloat16:
        return BF16
    if a.dtype == np.uint32:
        return U32
    raise ValueError(str(a.dtype))


FUSED_BLOCKS = [
    ("l0a", "build_L0a"), ("l0b1", "build_L0b1"), ("l0b2", "build_L0b2"),
    ("l1a_pre", "build_L1a_pre"), ("scan0", "build_L1a_scan"),
    ("scan1", "build_L1a_scan"), ("l1a_out", "build_L1a_out"),
    ("l1b", "build_L1b"), ("l2a", "build_L2a"), ("l2b", "build_L2b"),
    ("l3", "build_L3"),
]


def build_fused(shapes):
    p = Prog("fused")

    def emit(p):
        nc = p.nc
        misc = emit_misc(p)
        x4out = nc.dram_tensor("x4_o", [DM, TPC], F32, kind="ExternalOutput")

        def run(prefix, fn, wired):
            p.prefix = prefix + "_"
            io = {}
            for name, (shape, dt) in shapes[prefix].items():
                t = nc.dram_tensor(prefix + "_" + name, list(shape), dt,
                                   kind="ExternalInput")
                io[name] = t.ap()
            io.update(wired)
            p.iomap = io
            with p.scope():
                fn(p)
            p.prefix = ""
            p.iomap = None
            return io

        io0 = run("l0a", build_L0a, {})
        io1 = run("l0b1", build_L0b1, {"lg": io0["lgfull"]})
        io2 = run("l0b2", build_L0b2,
                  {"xn_tok": io0["xntok"], "s0": io0["s0_o"],
                   "slotsT_s": io1["slotsT_s"], "wkT_s": io1["wkT_s"]})
        iop = run("l1a_pre", build_L1a_pre, {"xt": io2["x1_o"]})
        ofint = nc.dram_tensor("of_int", [DM, TPC], BF16)
        for b in range(NSEQ):
            sl = slice(b * SEQ, (b + 1) * SEQ)
            run("scan%d" % b, build_L1a_scan,
                {"qf": iop["qf_o"][:, sl], "kf": iop["kf_o"][:, sl],
                 "qt": iop["qt_o"][sl, :], "kt": iop["kt_o"][sl, :],
                 "vt": iop["vt_o"][sl, :], "la": iop["la_o"][:, sl],
                 "be": iop["be_o"][:, sl], "of_o": ofint.ap()[:, sl]})
        ioout = run("l1a_out", build_L1a_out,
                    {"qf": iop["qf_o"], "kf": iop["kf_o"], "vf": iop["vf_o"],
                     "gf": iop["g_o"], "off": ofint.ap()})
        iolb = run("l1b", build_L1b,
                   {"xn1": iop["xn_o"], "x1": io2["x1_o"],
                    "s1": ioout["s1_o"]})
        io2a = run("l2a", build_L2a, {"xt": iolb["x2_o"]})
        io2b = run("l2b", build_L2b,
                   {"scf": io2a["scg"], "scmy": io2a["sc_o"],
                    "xn_tok": io2a["xntok2"], "xt": iolb["x2_o"],
                    "s2": io2a["s2_o"]})
        run("l3", build_L3, {"xt": io2b["x3_o"], "x4_o": x4out.ap()})

    p.build(emit)
    return p


def fused_inmaps(inputs):
    import ml_dtypes
    c = host_consts()
    cl = l1_host_consts()
    x = np.asarray(inputs["L0_attn_x"], np.float32)
    shards = shard_x(x)
    b1t, b2t, iota8, tril = moe_host_consts(inputs)

    per_core = [dict() for _ in range(NCORE)]
    shapes = {}

    def add(prefix, name, arrs):
        if not isinstance(arrs, list):
            arrs = [arrs] * NCORE
        shapes.setdefault(prefix, {})[name] = (arrs[0].shape, _np2dt(arrs[0]))
        for ci in range(NCORE):
            per_core[ci][prefix + "_" + name] = arrs[ci]

    m = l0a_inmaps(inputs, shards)
    for k in m[0]:
        add("l0a", k, [m[ci][k] for ci in range(NCORE)])
    add("l0a", "eyedl0", c["eye128"])
    add("l0b1", "iota8", iota8)
    add("l0b1", "tril", tril)
    add("l0b1", "eyedb1", c["eye128"])
    add("l0b2", "xt", shards)
    add("l0b2", "w1", np.ascontiguousarray(inputs["L0_ffn_w1"].astype(np.float32)))
    add("l0b2", "w2", np.ascontiguousarray(inputs["L0_ffn_w2"].astype(np.float32)))
    add("l0b2", "b1t", b1t)
    add("l0b2", "b2t", b2t)
    add("l0b2", "eye_d2", c["eye128"])
    sel = []
    for ci in range(NCORE):
        v = np.array([kk * 256 + ci * 32 + k for kk in range(2)
                      for k in range(32)], np.uint32).reshape(64, 1)
        sel.append(np.ascontiguousarray(v))
    add("l0b2", "selidx", sel)
    mp = l1a_pre_inmaps(inputs, shards)
    for k in mp[0]:
        if k == "xt":
            continue
        add("l1a_pre", k, [mp[ci][k] for ci in range(NCORE)])
    for b in range(NSEQ):
        pref = "scan%d" % b
        add(pref, "masksu", cl["masksu"])
        add(pref, "masksl", cl["masksl"])
        add(pref, "eye_d", cl["eye128"])
    dp = inputs["L1_attn_Dp"].astype(np.float32)
    add("l1a_out", "woT", np.ascontiguousarray(inputs["L1_attn_Wo"].T.astype(np.float32)))
    add("l1a_out", "dp2", np.ascontiguousarray(dp.reshape(8, 2).T))
    add("l1a_out", "onw", np.ascontiguousarray(
        np.tile(inputs["L1_attn_onorm_w"].astype(np.float32), 2).reshape(128, 1)))
    add("l1a_out", "bc2", cl["bc2"])
    mlb = l1b_inmaps(inputs, [None] * NCORE, [None] * NCORE, [None] * NCORE)
    for k in mlb[0]:
        if k in ("xn1", "x1", "s1"):
            continue
        add("l1b", k, [mlb[ci][k] for ci in range(NCORE)])
    m2 = l2_inmaps(inputs, shards)
    for k in m2[0]:
        if k == "xt":
            continue
        add("l2a", k, [m2[ci][k] for ci in range(NCORE)])
    grp8 = np.zeros((128, 8), np.float32)
    for pp in range(128):
        grp8[pp, pp // 16] = 1.0
    add("l2b", "w1T", np.ascontiguousarray(
        inputs["L2_ffn_W1"].transpose(0, 2, 1).astype(np.float32)))
    add("l2b", "w3T", np.ascontiguousarray(
        inputs["L2_ffn_W3"].transpose(0, 2, 1).astype(np.float32)))
    add("l2b", "w2T", np.ascontiguousarray(
        inputs["L2_ffn_W2"].transpose(0, 2, 1).astype(np.float32)))
    add("l2b", "grp8", grp8)
    add("l2b", "bc8", np.ascontiguousarray(grp8.T))
    add("l2b", "ecap", np.ascontiguousarray(
        (np.arange(8, dtype=np.float32) * CAPL2 - 1.0).reshape(8, 1)))
    add("l2b", "eye_d", c["eye128"])
    m3 = l3_inmaps(inputs, shards)
    for k in m3[0]:
        if k == "xt":
            continue
        add("l3", k, [m3[ci][k] for ci in range(NCORE)])
    return per_core, shapes


def kernel_fused(**inputs):
    per_core, shapes = fused_inmaps(inputs)
    p = build_fused(shapes)
    res = run_bass_kernel_spmd(p.nc, per_core, core_ids=list(range(NCORE)))
    out = np.empty((16, SEQ, DM), np.float32)
    for ci in range(NCORE):
        x4 = np.asarray(res.results[ci]["x4_o"])
        out[ci * NSEQ:(ci + 1) * NSEQ] = x4.T.reshape(NSEQ, SEQ, DM)
    return out


# version 03 (final)
# speedup vs baseline: 1.3679x; 1.3679x over previous
import math
import numpy as np
from contextlib import ExitStack

import concourse.bass as bass
import concourse.mybir as mybir
from concourse.bass_utils import run_bass_kernel_spmd

F32 = mybir.dt.float32
BF16 = mybir.dt.bfloat16
U32 = mybir.dt.uint32
I32 = mybir.dt.int32
AF = mybir.ActivationFunctionType
ALU = mybir.AluOpType
AX = mybir.AxisListType

NCORE = 8
DM = 1024
DC = DM // 128
TPC = 4096
SEQ = 2048
NSEQ = 2
NH = 16
HD = 64

L0_SHIFT = 60.0
L2_SHIFT = 16.0


class Sem:
    __slots__ = ("name", "h", "n")

    def __init__(self, name, h):
        self.name, self.h, self.n = name, h, 0


class _APWrap:
    __slots__ = ("_ap",)

    def __init__(self, ap):
        self._ap = ap

    def ap(self):
        return self._ap


class Prog:
    ENGS = ("sync", "tensor", "vector", "scalar", "gpsimd")

    def __init__(self, name="prog"):
        self.nc = bass.Bass("TRN2", target_bir_lowering=False, debug=False)
        self.q = {e: [] for e in self.ENGS}
        self.ctx = None
        self.name = name
        self.cm = None
        self.dm = None
        self._consts = {}
        self.prefix = ""
        self.iomap = None
        self.misc_cache = None
        self.tracked = False
        self.esem = {}
        self.last_write = {}
        self.readers = {}
        self.waited = {}

    def sem(self, name):
        h = self.ctx.enter_context(self.nc.semaphore(self.prefix + name))
        return Sem(name, h)

    def sbuf(self, name, shape, dtype):
        return self.ctx.enter_context(
            self.nc.sbuf_tensor(self.prefix + name, list(shape), dtype))

    def scope(self):
        import contextlib

        @contextlib.contextmanager
        def _s():
            outer = self.ctx
            with ExitStack() as inner:
                self.ctx = inner
                try:
                    yield
                finally:
                    self.ctx = outer
                    if self.tracked:
                        self.soft_barrier()
        return _s()

    def constf(self, val, dtype=F32):
        key = (dtype, float(val))
        if key in self._consts:
            return self._consts[key]
        t = self.sbuf("c_%d" % len(self._consts), (128, 1), dtype)
        self.op("vector", "memset", t[:, :], float(val))
        self.nc.const_aps.register(dtype, float(val), t[:, :])
        self._consts[key] = t
        return t

    _psn = 0

    def psum(self, shape, dtype=F32):
        Prog._psn += 1
        return self.ctx.enter_context(
            self.nc.psum_tensor("ps%d" % Prog._psn, list(shape), dtype))

    def dram(self, name, shape, dtype, kind=None):
        if self.iomap is not None:
            if name in self.iomap:
                return _APWrap(self.iomap[name])
            kind = None
        nm = self.prefix + name if kind is None else name
        if kind is None:
            t = self.nc.dram_tensor(nm, list(shape), dtype)
        else:
            t = self.nc.dram_tensor(nm, list(shape), dtype, kind=kind)
        if self.iomap is not None:
            self.iomap[name] = t.ap()
            return _APWrap(t.ap())
        return t

    def on(self, eng, f):
        self.q[eng].append(f)

    def _prewait(self, eng):
        cv, dv = self.cm.n, self.dm.n
        ch, dh = self.cm.h, self.dm.h
        def f(E, cv=cv, dv=dv):
            if cv:
                E.wait_ge(ch, cv)
            if dv:
                E.wait_ge(dh, dv)
        self.on(eng, f)

    def _emit_waits(self, eng, deps):
        w = self.waited.setdefault(eng, {})
        need = {}
        for sn, v in deps:
            if v <= 0 or w.get(sn, 0) >= v:
                continue
            if need.get(sn, 0) < v:
                need[sn] = v
        for sn, v in need.items():
            s = self.esem[sn]
            self.on(eng, lambda E, h=s.h, v=v: E.wait_ge(h, v))
            w[sn] = v

    def _track(self, outs, ins, bump):
        sn, nv = bump
        for t in ins:
            rd = self.readers.setdefault(t, {})
            if rd.get(sn, 0) < nv:
                rd[sn] = nv
        for t in outs:
            self.last_write[t] = (sn, nv)
            self.readers[t] = {}

    def _deps_for(self, outs, ins):
        deps = []
        for t in ins:
            lw = self.last_write.get(t)
            if lw:
                deps.append(lw)
        for t in outs:
            lw = self.last_write.get(t)
            if lw:
                deps.append(lw)
            for sn, v in self.readers.get(t, {}).items():
                deps.append((sn, v))
        return deps

    def global_barrier_deps(self):
        return [(sn, s.n) for sn, s in self.esem.items() if s.n > 0]

    def soft_barrier(self):
        deps = self.global_barrier_deps()
        for e in self.ENGS:
            self._emit_waits(e, deps)

    paranoid = False

    def op(self, eng, opname, *a, group=False, **kw):
        if self.tracked:
            outs = set()
            ins = set()
            if a and hasattr(a[0], "tensor"):
                outs.add(a[0].tensor.name)
            for x in a[1:]:
                if hasattr(x, "tensor"):
                    ins.add(x.tensor.name)
            for k, x in kw.items():
                if hasattr(x, "tensor"):
                    (outs if k == "accum_out" else ins).add(x.tensor.name)
            if self.paranoid and not group:
                deps = self.global_barrier_deps()
            else:
                deps = self._deps_for(outs, ins)
            if group:
                deps = [d for d in deps if d[0] != eng]
            self._emit_waits(eng, deps)
            s = self.esem[eng]
            s.n += 1
            self._track(outs, ins, (eng, s.n))
            def f(E, a=a, kw=kw, opname=opname, sh=s.h):
                r = getattr(E, opname)(*a, **kw)
                r.then_inc(sh, 1)
            self.on(eng, f)
            acc = kw.get("accum_out")
            if acc is not None:
                if eng == "vector":
                    self.op(eng, "tensor_scalar", acc, acc, 1.0, None,
                            op0=ALU.mult)
                else:
                    self.op(eng, "mul", acc, acc, 1.0)
            return
        if not group:
            self._prewait(eng)
        s = self.cm
        s.n += 1
        def f(E, a=a, kw=kw, opname=opname, sh=s.h):
            r = getattr(E, opname)(*a, **kw)
            r.then_inc(sh, 1)
        self.on(eng, f)

    def raw(self, eng, opname, *a, **kw):
        def f(E, a=a, kw=kw, opname=opname):
            getattr(E, opname)(*a, **kw)
        self.on(eng, f)

    def dma(self, eng, out, in_, **kw):
        if self.tracked:
            qn = "q_" + eng
            outs = {out.tensor.name}
            ins = {in_.tensor.name}
            self._emit_waits(eng, self.global_barrier_deps()
                             if self.paranoid else self._deps_for(outs, ins))
            s = self.esem[qn]
            s.n += 16
            self._track(outs, ins, (qn, s.n))
            def f(E, o=out, i=in_, kw=kw, sh=s.h):
                E.dma_start(out=o, in_=i, **kw).then_inc(sh, 16)
            self.on(eng, f)
            return
        self._prewait(eng)
        s = self.dm
        s.n += 16
        def f(E, o=out, i=in_, kw=kw, sh=s.h):
            E.dma_start(out=o, in_=i, **kw).then_inc(sh, 16)
        self.on(eng, f)

    def gather(self, out, in_, idx_ap, **kw):
        if self.tracked:
            qn = "q_gpsimd"
            outs = {out.tensor.name}
            ins = {in_.tensor.name, idx_ap.tensor.name}
            self._emit_waits("gpsimd", self.global_barrier_deps()
                             if self.paranoid else self._deps_for(outs, ins))
            s = self.esem[qn]
            s.n += 16
            self._track(outs, ins, (qn, s.n))
            def f(E, kw=kw, sh=s.h):
                E.indirect_dma_start(out, None, in_,
                                     bass.IndirectOffsetOnAxis(ap=idx_ap,
                                                               axis=0),
                                     **kw).then_inc(sh, 16)
            self.on("gpsimd", f)
            return
        self._prewait("gpsimd")
        s = self.dm
        s.n += 16
        def f(E, kw=kw, sh=s.h):
            E.indirect_dma_start(out, None, in_,
                                 bass.IndirectOffsetOnAxis(ap=idx_ap, axis=0),
                                 **kw).then_inc(sh, 16)
        self.on(eng := "gpsimd", f)

    def scatter(self, out, in_, idx_ap, **kw):
        if self.tracked:
            qn = "q_gpsimd"
            outs = {out.tensor.name}
            ins = {in_.tensor.name, idx_ap.tensor.name}
            self._emit_waits("gpsimd", self.global_barrier_deps()
                             if self.paranoid else self._deps_for(outs, ins))
            s = self.esem[qn]
            s.n += 16
            self._track(outs, ins, (qn, s.n))
            def f(E, kw=kw, sh=s.h):
                E.indirect_dma_start(out,
                                     bass.IndirectOffsetOnAxis(ap=idx_ap,
                                                               axis=0),
                                     in_, None, **kw).then_inc(sh, 16)
            self.on("gpsimd", f)
            return
        self._prewait("gpsimd")
        s = self.dm
        s.n += 16
        def f(E, kw=kw, sh=s.h):
            E.indirect_dma_start(out, bass.IndirectOffsetOnAxis(ap=idx_ap, axis=0),
                                 in_, None, **kw).then_inc(sh, 16)
        self.on("gpsimd", f)

    _ccn = 0

    def collective(self, kind, op, ins, outs):
        Prog._ccn += 1
        s = self.sem("cc%d" % Prog._ccn)
        if self.tracked:
            self._emit_waits("gpsimd", self.global_barrier_deps())
            def fc(E, s=s, ins=ins, outs=outs, kind=kind, op=op):
                E.collective_compute(kind, op,
                                     replica_groups=[list(range(NCORE))],
                                     ins=ins, outs=outs).then_inc(s.h)
            self.on("gpsimd", fc)
            for e in self.ENGS:
                self.on(e, lambda E, s=s: E.wait_ge(s.h, 1))
            snap = {sn: sm.n for sn, sm in self.esem.items()}
            for e in self.ENGS:
                self.waited.setdefault(e, {}).update(snap)
            return
        self._prewait("gpsimd")
        def f(E, s=s, ins=ins, outs=outs, kind=kind, op=op):
            E.collective_compute(kind, op, replica_groups=[list(range(NCORE))],
                                 ins=ins, outs=outs).then_inc(s.h)
        self.on("gpsimd", f)
        for e in self.ENGS:
            def w(E, s=s):
                E.wait_ge(s.h, 1)
            self.on(e, w)

    def barrier_all(self):
        for e in self.ENGS:
            self._prewait(e)

    def build(self, emit_fn, tracked=False):
        self.tracked = tracked
        with ExitStack() as ctx:
            self.ctx = ctx
            self.cm = self.sem("cm_auto")
            self.dm = self.sem("dm_auto")
            if tracked:
                for e in self.ENGS:
                    self.esem[e] = self.sem("es_" + e)
                for e in ("sync", "scalar", "gpsimd", "vector"):
                    self.esem["q_" + e] = self.sem("qs_" + e)
            emit_fn(self)
            if tracked:
                self._emit_waits("sync", self.global_barrier_deps())
            self._prewait("sync")
            with self.nc.Block() as blk:
                for eng in self.ENGS:
                    fns = self.q[eng]
                    def section(E, fns=fns):
                        for f in fns:
                            f(E)
                    getattr(blk, eng)(section)
        return self.nc


def cdiv(a, b):
    return (a + b - 1) // b


_CONST_CACHE = {}


def host_consts():
    if _CONST_CACHE:
        return _CONST_CACHE
    c = _CONST_CACHE
    inv_freq = 1.0 / (10000.0 ** (np.arange(0, HD, 2)[: HD // 2] / HD))
    t = np.arange(SEQ)
    fr = np.outer(t, inv_freq)
    cos = np.cos(fr).astype(np.float32)
    sin = np.sin(fr).astype(np.float32)
    c64 = np.zeros((HD, SEQ), np.float32)
    s64 = np.zeros((HD, SEQ), np.float32)
    c64[0::2] = cos.T
    c64[1::2] = cos.T
    s64[0::2] = sin.T
    s64[1::2] = sin.T
    c["ropeC"] = np.ascontiguousarray(np.concatenate([c64, c64], 0))
    c["ropeS"] = np.ascontiguousarray(np.concatenate([s64, s64], 0))
    R = np.zeros((HD, HD), np.float32)
    for i in range(HD // 2):
        R[2 * i, 2 * i + 1] = -1.0
        R[2 * i + 1, 2 * i] = 1.0
    Rb = np.zeros((128, 128), np.float32)
    Rb[:64, :64] = R.T
    Rb[64:, 64:] = R.T
    c["ropeRotT"] = np.ascontiguousarray(Rb)
    masks = np.zeros((4, 128, 512), np.float32)
    for j in range(4):
        tk = j * 128 + np.arange(128)[:, None]
        tq = np.arange(512)[None, :]
        masks[j] = (tk <= tq).astype(np.float32)
    c["cmask"] = masks
    c["eye128"] = np.eye(128, dtype=np.float32)
    return c


def emit_ln(p, xt_dram, xnbuf, misc):
    onecol = misc["ones_col_bf"]
    orow = misc["ones_row_bf"]
    xf = p.sbuf("ln_xf", (128, DC * 512), F32)
    xb = p.sbuf("ln_xb", (128, DC * 512), BF16)
    sq = p.sbuf("ln_sq", (128, DC * 512), BF16)
    ps_a = misc["psA"]
    ps_b = misc["psB"]
    ps_c = misc["psC"]
    st_mu = p.sbuf("ln_mu", (1, 512), F32)
    st_r = p.sbuf("ln_r", (1, 512), F32)
    st_v = p.sbuf("ln_v", (1, 512), F32)
    st_m2 = p.sbuf("ln_m2", (1, 512), F32)
    rb = p.sbuf("ln_rb", (128, 512), BF16)
    m2b = p.sbuf("ln_m2b", (128, 512), BF16)

    for tb in range(TPC // 512):
        for kc in range(DC):
            p.dma("sync", xf[:, kc * 512:(kc + 1) * 512],
                  xt_dram[kc * 128:(kc + 1) * 128, tb * 512:(tb + 1) * 512])
        for kc in range(DC):
            p.op("vector", "tensor_copy", xb[:, kc * 512:(kc + 1) * 512],
                 xf[:, kc * 512:(kc + 1) * 512])
        for kc in range(DC):
            p.op("scalar", "activation", sq[:, kc * 512:(kc + 1) * 512],
                 xb[:, kc * 512:(kc + 1) * 512], AF.Square)
        for kc in range(DC):
            p.op("tensor", "matmul", ps_a[0:1, :], onecol[:, :],
                 xb[:, kc * 512:(kc + 1) * 512], start=(kc == 0),
                 stop=(kc == DC - 1), group=(kc > 0))
        for kc in range(DC):
            p.op("tensor", "matmul", ps_b[0:1, :], onecol[:, :],
                 sq[:, kc * 512:(kc + 1) * 512], start=(kc == 0),
                 stop=(kc == DC - 1), group=(kc > 0))
        p.op("vector", "tensor_scalar", st_mu[:, :], ps_a[0:1, :], 1.0 / DM,
             None, op0=ALU.mult)
        p.op("vector", "tensor_scalar", st_v[:, :], ps_b[0:1, :], 1.0 / DM,
             None, op0=ALU.mult)
        p.op("vector", "tensor_tensor", st_m2[:, :], st_mu[:, :], st_mu[:, :],
             op=ALU.mult)
        p.op("vector", "tensor_tensor", st_v[:, :], st_v[:, :], st_m2[:, :],
             op=ALU.subtract)
        p.op("scalar", "activation", st_v[:, :], st_v[:, :], AF.Ln, bias=1e-5)
        p.op("scalar", "activation", st_r[:, :], st_v[:, :], AF.Exp, scale=-0.5)
        p.op("vector", "tensor_tensor", st_m2[:, :], st_mu[:, :], st_r[:, :],
             op=ALU.mult)
        p.op("tensor", "matmul", ps_c[:, :], misc["ones_b128_f"][:, :], st_r[:, :],
             start=True, stop=True)
        p.op("vector", "tensor_copy", rb[:, :], ps_c[:, :])
        p.op("tensor", "matmul", ps_c[:, :], misc["ones_b128_f"][:, :], st_m2[:, :],
             start=True, stop=True)
        p.op("vector", "tensor_copy", m2b[:, :], ps_c[:, :])
        for kc in range(DC):
            dst = xnbuf[:, kc * TPC + tb * 512: kc * TPC + (tb + 1) * 512]
            p.op("vector", "tensor_tensor", dst, xb[:, kc * 512:(kc + 1) * 512],
                 rb[:, :], op=ALU.mult)
            p.op("vector", "tensor_tensor", dst, dst, m2b[:, :],
                 op=ALU.subtract)


def emit_proj(p, w_sb, xnbuf, out_cb, banks, noc=None, ow=None):
    if ow is None:
        ow = w_sb.shape[1] // DC
    if noc is None:
        noc = ow // 128
    for oc in range(noc):
        for tb in range(TPC // 512):
            bank = banks[(oc * (TPC // 512) + tb) % len(banks)]
            for kc in range(DC):
                p.op("tensor", "matmul", bank[:, :],
                     w_sb[:, kc * ow + oc * 128: kc * ow + (oc + 1) * 128],
                     xnbuf[:, kc * TPC + tb * 512: kc * TPC + (tb + 1) * 512],
                     start=(kc == 0), stop=(kc == DC - 1), group=(kc > 0))
            out_cb(oc, tb, bank)


def load_w_bf16(p, dst_sb, src_dram, ow):
    for kc in range(DC):
        p.dma("gpsimd", dst_sb[:, kc * ow:(kc + 1) * ow],
              src_dram[kc * 128:(kc + 1) * 128, :])


def emit_misc(p):
    if p.misc_cache is not None:
        return p.misc_cache
    misc = {}
    oc_bf = p.sbuf("ones_col_bf", (128, 1), BF16)
    orow = p.sbuf("ones_row_bf", (1, 2048), BF16)
    p.op("vector", "memset", oc_bf[:, :], 1.0)
    p.op("vector", "memset", orow[:, :], 1.0)
    ob128 = p.sbuf("ones_b128_f", (1, 128), F32)
    p.op("vector", "memset", ob128[:, :], 1.0)
    misc["ones_col_bf"] = oc_bf
    misc["ones_row_bf"] = orow
    misc["ones_b128_f"] = ob128
    for v in (1e-5, -0.5, -1.0, -60.0, -16.0, 1.0, 0.0, 1e-6, -2.0, 0.5,
              1e-12, 0.125, 0.25, 8.0, -20.0, float(1.0 / 64.0), float(1.0 / DM)):
        p.constf(v)
    misc["psA"] = p.psum((128, 512))
    misc["psB"] = p.psum((128, 512))
    misc["psC"] = p.psum((128, 512))
    misc["psD"] = p.psum((128, 512))
    misc["psE"] = p.psum((128, 512))
    misc["psF"] = p.psum((128, 512))
    misc["psT"] = p.psum((128, 1024), BF16)
    p.misc_cache = misc
    return misc


def build_L0a(p=None):
    standalone = p is None
    if standalone:
        p = Prog("l0a")

    def emit(p):
        xt = p.dram("xt", (DM, TPC), F32, kind="ExternalInput").ap()
        wq = p.dram("wq", (DM, DM), F32, kind="ExternalInput").ap()
        wk = p.dram("wk", (DM, DM), F32, kind="ExternalInput").ap()
        wv = p.dram("wv", (DM, DM), F32, kind="ExternalInput").ap()
        wo = p.dram("wo", (DM, DM), F32, kind="ExternalInput").ap()
        wg = p.dram("wg", (DM, 8), F32, kind="ExternalInput").ap()
        alpha = p.dram("alpha", (1, 1), F32, kind="ExternalInput").ap()
        ropeC = p.dram("ropeC", (128, SEQ), F32, kind="ExternalInput").ap()
        ropeS = p.dram("ropeS", (128, SEQ), F32, kind="ExternalInput").ap()
        rotT = p.dram("rotT", (128, 128), F32, kind="ExternalInput").ap()
        cmask = p.dram("cmask", (4, 128, 512), F32, kind="ExternalInput").ap()

        xn_o = p.dram("xn_o", (DM, TPC), BF16, kind="ExternalOutput").ap()
        s0_o = p.dram("s0_o", (DM, TPC), BF16, kind="ExternalOutput").ap()
        lg_o = p.dram("lg_o", (8, TPC), F32, kind="ExternalOutput").ap()

        qext_s = p.dram("qext_s", (NSEQ * NH, 68, SEQ), BF16).ap()
        k_s = p.dram("k_s", (NSEQ * NH, 64, SEQ), BF16).ap()
        km2_s = p.dram("km2_s", (NSEQ * NH, 68, SEQ), BF16).ap()
        vtok_s = p.dram("vtok_s", (NSEQ * NH, SEQ, 65), BF16).ap()

        misc = emit_misc(p)
        orow = misc["ones_row_bf"]

        xnbuf = p.sbuf("xnbuf", (128, DC * TPC), BF16)
        with p.scope():
            emit_ln(p, xt, xnbuf, misc)
        for kc in range(DC):
            p.dma("scalar", xn_o[kc * 128:(kc + 1) * 128, :],
                  xnbuf[:, kc * TPC:(kc + 1) * TPC])

        w_sb = p.sbuf("w_sb", (128, DC * DM), BF16)
        wg_sb = p.sbuf("wg_sb", (128, DC * 8), BF16)
        load_w_bf16(p, wg_sb, wg, 8)

        rc = p.sbuf("ropeC_sb", (128, SEQ), F32)
        rs = p.sbuf("ropeS_sb", (128, SEQ), F32)
        rt = p.sbuf("rotT_sb", (128, 128), F32)
        cms = p.sbuf("cmask_sb", (128, 4 * 512), BF16)
        p.dma("sync", rc[:, :], ropeC[:, :])
        p.dma("sync", rs[:, :], ropeS[:, :])
        p.dma("sync", rt[:, :], rotT[:, :])
        for j in range(4):
            p.dma("gpsimd", cms[:, j * 512:(j + 1) * 512], cmask[j, :, :])

        if p.iomap is not None:
            eyedl0 = p.dram("eyedl0", (128, 128), F32, kind="ExternalInput").ap()
            eye = p.sbuf("eye_l0a", (128, 128), BF16)
            eyef0 = p.sbuf("eyef_l0a", (128, 128), F32)
            p.dma("gpsimd", eye[:, :], eyedl0[:, :])
            p.dma("sync", eyef0[:, :], eyedl0[:, :])
            misc["eyef_l0a"] = eyef0
        al = p.sbuf("alpha_sb", (1, 1), F32)
        lnsc = p.sbuf("lnsc", (1, 1), F32)
        lnsc_b = p.sbuf("lnsc_b", (128, 1), F32)
        p.dma("sync", al[:, :], alpha[:, :])
        base = math.sqrt(HD) / math.log(1.0 + HD)
        p.op("scalar", "mul", lnsc[:, :], al[:, :], math.log(base))
        p.op("tensor", "matmul", misc["psC"][:, 0:1], misc["ones_b128_f"][:, :],
             lnsc[:, :], start=True, stop=True)
        p.op("vector", "tensor_copy", lnsc_b[:, :], misc["psC"][:, 0:1])

        qf = p.sbuf("qf", (128, 512), F32)
        t1 = p.sbuf("t1", (128, 512), F32)
        qrp = p.sbuf("qrp", (128, 512), F32)
        km2t = p.sbuf("km2t", (128, 512), F32)
        sqv = p.sbuf("sqv", (128, 512), BF16)
        nstage = p.sbuf("nstage", (2, 512), F32)
        nhib = p.sbuf("nhib", (2, 512), BF16)
        nhif = p.sbuf("nhif", (2, 512), F32)
        nlo = p.sbuf("nlo", (2, 512), F32)
        onesb2 = p.sbuf("onesb2", (128, 2), BF16)
        p.op("vector", "memset", onesb2[:, :], 0.0)
        p.op("vector", "memset", onesb2[0:64, 0:1], 1.0)
        p.op("vector", "memset", onesb2[64:128, 1:2], 1.0)
        ps_rot = misc["psD"]
        ps_n = misc["psE"]
        pbanks = [misc["psA"], misc["psB"]]

        def cb_qk(which):
            def cb(oc, tb, bank):
                b = tb // 4
                tloc = (tb % 4) * 512
                p.op("vector", "tensor_copy", qf[:, :], bank[:, :])
                p.op("tensor", "matmul", ps_rot[:, :], rt[:, :], qf[:, :],
                     start=True, stop=True)
                p.op("vector", "tensor_tensor", t1[:, :], qf[:, :],
                     rc[:, tloc:tloc + 512], op=ALU.mult)
                p.op("vector", "tensor_tensor", qrp[:, :], ps_rot[:, :],
                     rs[:, tloc:tloc + 512], op=ALU.mult)
                p.op("vector", "tensor_tensor", qrp[:, :], qrp[:, :], t1[:, :],
                     op=ALU.add)
                p.op("scalar", "activation", sqv[:, :], qrp[:, :], AF.Square)
                p.op("tensor", "matmul", ps_n[0:2, :], onesb2[:, :], sqv[:, :],
                     start=True, stop=True)
                bh0 = b * 16 + 2 * oc
                bh1 = bh0 + 1
                if which == "q":
                    p.op("vector", "tensor_scalar", nstage[:, :], ps_n[0:2, :],
                         1e-5, None, op0=ALU.add)
                else:
                    p.op("vector", "tensor_copy", nstage[:, :], ps_n[0:2, :])
                p.op("vector", "tensor_copy", nhib[:, :], nstage[:, :])
                p.op("vector", "tensor_copy", nhif[:, :], nhib[:, :])
                p.op("vector", "tensor_tensor", nlo[:, :], nstage[:, :],
                     nhif[:, :], op=ALU.subtract)
                tgt = qext_s if which == "q" else km2_s
                rhi = 66 if which == "q" else 64
                for par, bh in ((0, bh0), (1, bh1)):
                    p.dma("gpsimd", tgt[bh, rhi:rhi + 1, tloc:tloc + 512],
                          nhib[par:par + 1, :])
                    p.dma("gpsimd", tgt[bh, rhi + 1:rhi + 2, tloc:tloc + 512],
                          nlo[par:par + 1, :])
                if which == "q":
                    p.dma("gpsimd", qext_s[bh0, 0:64, tloc:tloc + 512], qrp[0:64, :])
                    p.dma("gpsimd", qext_s[bh1, 0:64, tloc:tloc + 512], qrp[64:128, :])
                else:
                    p.dma("gpsimd", k_s[bh0, :, tloc:tloc + 512], qrp[0:64, :])
                    p.dma("gpsimd", k_s[bh1, :, tloc:tloc + 512], qrp[64:128, :])
                    p.op("vector", "tensor_scalar", km2t[:, :], qrp[:, :], -2.0,
                         None, op0=ALU.mult)
                    p.dma("gpsimd", km2_s[bh0, 0:64, tloc:tloc + 512], km2t[0:64, :])
                    p.dma("gpsimd", km2_s[bh1, 0:64, tloc:tloc + 512], km2t[64:128, :])
            return cb

        load_w_bf16(p, w_sb, wq, DM)
        emit_proj(p, w_sb, xnbuf, cb_qk("q"), pbanks)
        load_w_bf16(p, w_sb, wk, DM)
        emit_proj(p, w_sb, xnbuf, cb_qk("k"), pbanks)

        eye = p.sbuf("eye_sb", (128, 128), BF16)
        eyed = p.dram("eye_d", (128, 128), F32, kind="ExternalInput").ap()
        p.dma("gpsimd", eye[:, :], eyed[:, :])
        vtstage = p.sbuf("vtstage", (128, 65), BF16)
        vbf = p.sbuf("vbf", (128, 512), BF16)
        p.op("vector", "memset", vtstage[:, :], 1.0)
        ps_tr = misc["psT"]

        def cb_v(oc, tb, bank):
            b = tb // 4
            tloc = (tb % 4) * 512
            p.op("vector", "tensor_copy", vbf[:, :], bank[:, :])
            for hh in range(2):
                bh = b * 16 + 2 * oc + hh
                for j in range(4):
                    p.op("tensor", "transpose", ps_tr[:, 0:64],
                         vbf[hh * 64:(hh + 1) * 64, j * 128:(j + 1) * 128],
                         eye[hh * 64:(hh + 1) * 64, hh * 64:(hh + 1) * 64])
                    p.op("vector", "tensor_copy", vtstage[:, 0:64], ps_tr[:, 0:64])
                    p.dma("gpsimd",
                          vtok_s[bh, tloc + j * 128: tloc + (j + 1) * 128, :],
                          vtstage[:, :])
        load_w_bf16(p, w_sb, wv, DM)
        emit_proj(p, w_sb, xnbuf, cb_v, pbanks)
        load_w_bf16(p, w_sb, wo, DM)

        ones_bf_row = p.sbuf("ones_bf_row", (1, 2048), BF16)
        p.op("vector", "memset", ones_bf_row[:, :], 1.0)
        for bh in range(NSEQ * NH):
            p.dma("gpsimd", qext_s[bh, 64:65, :], ones_bf_row[:, :])
            p.dma("gpsimd", qext_s[bh, 65:66, :], ones_bf_row[:, :])
            p.dma("gpsimd", km2_s[bh, 66:67, :], ones_bf_row[:, :])
            p.dma("gpsimd", km2_s[bh, 67:68, :], ones_bf_row[:, :])

        qe_t = p.sbuf("qe_t", (68, SEQ), BF16)
        k_t = p.sbuf("k_t", (64, SEQ), BF16)
        km2_t = p.sbuf("km2_t", (68, SEQ), BF16)
        vt_t = p.sbuf("vt_t", (128, 16 * 65), BF16)
        ps_m1 = misc["psA"]
        ps_m2 = misc["psB"]
        ps_pv = misc["psC"]
        ps_rb = misc["psD"]
        sq_f = p.sbuf("sq_f", (128, 512), F32)
        lnm2 = p.sbuf("lnm2", (128, 512), F32)
        rcp = p.sbuf("rcp", (128, 512), F32)
        parg = p.sbuf("parg", (128, 512), F32)
        ptile = p.sbuf("ptile", (128, 512), BF16)
        zrow = p.sbuf("zrow", (1, 512), F32)
        zr = p.sbuf("zr", (1, 512), F32)
        zrb = p.sbuf("zrb", (128, 512), BF16)
        otile = p.sbuf("otile", (128, 512), BF16)
        obuf = p.sbuf("obuf", (128, DC * SEQ), BF16)
        sbf = p.sbuf("sbf", (128, 512), BF16)

        for b in range(NSEQ):
            for h in range(NH):
                bh = b * 16 + h
                p.dma("sync", qe_t[:, :], qext_s[bh, :, :])
                p.dma("sync", k_t[:, :], k_s[bh, :, :])
                p.dma("sync", km2_t[:, :], km2_s[bh, :, :])
                for j in range(16):
                    p.dma("scalar", vt_t[:, j * 65:(j + 1) * 65],
                          vtok_s[bh, j * 128:(j + 1) * 128, :])
                for tq in range(4):
                    ntk = (tq + 1) * 4
                    for tkj in range(ntk):
                        diag = (tkj >= tq * 4)
                        p.op("tensor", "matmul", ps_m1[:, :],
                             k_t[:, tkj * 128:(tkj + 1) * 128],
                             qe_t[0:64, tq * 512:(tq + 1) * 512],
                             start=True, stop=True)
                        p.op("tensor", "matmul", ps_m2[:, :],
                             km2_t[:, tkj * 128:(tkj + 1) * 128],
                             qe_t[:, tq * 512:(tq + 1) * 512],
                             start=True, stop=True)
                        p.op("scalar", "activation", lnm2[:, :], ps_m2[:, :],
                             AF.Ln)
                        p.op("scalar", "activation", sq_f[:, :], ps_m1[:, :],
                             AF.Square)
                        p.op("scalar", "activation", rcp[:, :], lnm2[:, :],
                             AF.Exp, scale=-1.0, bias=lnsc_b[:, 0:1])
                        p.op("vector", "tensor_tensor", parg[:, :], sq_f[:, :],
                             rcp[:, :], op=ALU.mult)
                        p.op("scalar", "activation", ptile[:, :], parg[:, :],
                             AF.Exp, bias=-L0_SHIFT)
                        if diag:
                            j = tkj - tq * 4
                            p.op("vector", "tensor_tensor", ptile[:, :],
                                 ptile[:, :], cms[:, j * 512:(j + 1) * 512],
                                 op=ALU.mult)
                        p.op("tensor", "matmul", ps_pv[0:65, :],
                             vt_t[:, tkj * 65:(tkj + 1) * 65], ptile[:, :],
                             start=(tkj == 0), stop=(tkj == ntk - 1))
                    p.op("vector", "tensor_copy", zrow[:, :], ps_pv[64:65, :])
                    p.op("vector", "reciprocal", zr[:, :], zrow[:, :])
                    p.op("tensor", "matmul", ps_rb[:, :], misc["ones_b128_f"][:, :],
                         zr[:, :], start=True, stop=True)
                    p.op("vector", "tensor_copy", zrb[:, :], ps_rb[:, :])
                    p.op("vector", "tensor_copy", otile[:, :], ps_pv[:, :])
                    p.op("vector", "tensor_tensor", otile[:, :], otile[:, :],
                         zrb[:, :], op=ALU.mult)
                    hp, par = h // 2, h % 2
                    p.op("vector", "tensor_copy",
                         obuf[par * 64:(par + 1) * 64,
                              hp * SEQ + tq * 512: hp * SEQ + (tq + 1) * 512],
                         otile[0:64, :])
            for oc in range(DC):
                for tb in range(4):
                    bank = [misc["psE"], misc["psF"]][tb % 2]
                    for kc in range(DC):
                        p.op("tensor", "matmul", bank[:, :],
                             w_sb[:, kc * DM + oc * 128: kc * DM + (oc + 1) * 128],
                             obuf[:, kc * SEQ + tb * 512: kc * SEQ + (tb + 1) * 512],
                             start=(kc == 0), stop=(kc == DC - 1), group=(kc > 0))
                    p.op("vector", "tensor_copy", sbf[:, :], bank[:, :])
                    p.dma("gpsimd",
                          s0_o[oc * 128:(oc + 1) * 128,
                               b * 2048 + tb * 512: b * 2048 + (tb + 1) * 512],
                          sbf[:, :])

        lgf = p.sbuf("lgf", (8, 512), F32)
        lgtr = p.sbuf("lgtr", (128, 8), F32)
        if p.iomap is not None:
            lgtok = p.dram("lgtok", (TPC, 8), F32).ap()
            lgfull = p.dram("lgfull", (NCORE * TPC, 8), F32).ap()
        for tb in range(8):
            for kc in range(DC):
                p.op("tensor", "matmul", misc["psA"][0:8, :],
                     wg_sb[:, kc * 8:(kc + 1) * 8],
                     xnbuf[:, kc * TPC + tb * 512: kc * TPC + (tb + 1) * 512],
                     start=(kc == 0), stop=(kc == DC - 1), group=(kc > 0))
            p.op("vector", "tensor_copy", lgf[:, :], misc["psA"][0:8, :])
            p.dma("sync", lg_o[:, tb * 512:(tb + 1) * 512], lgf[:, :])
            if p.iomap is not None:
                eyef0 = misc["eyef_l0a"]
                for j in range(4):
                    p.op("tensor", "transpose", misc["psB"][0:128, 0:8],
                         lgf[:, j * 128:(j + 1) * 128], eyef0[0:8, 0:8])
                    p.op("vector", "tensor_copy", lgtr[:, :],
                         misc["psB"][0:128, 0:8])
                    p.dma("gpsimd",
                          lgtok[tb * 512 + j * 128: tb * 512 + (j + 1) * 128, :],
                          lgtr[:, :])
        if p.iomap is not None:
            xntok = p.dram("xntok", (TPC, DM), BF16).ap()
            xtr = p.sbuf("xtr", (128, 128), BF16)
            for k in range(32):
                for kc in range(DC):
                    p.op("tensor", "transpose", misc["psT"][:, 0:128],
                         xnbuf[:, kc * TPC + k * 128: kc * TPC + (k + 1) * 128],
                         eye[:, :])
                    p.op("vector", "tensor_copy", xtr[:, :], misc["psT"][:, 0:128])
                    p.dma("gpsimd",
                          xntok[k * 128:(k + 1) * 128, kc * 128:(kc + 1) * 128],
                          xtr[:, :])
            p.collective("AllGather", ALU.bypass,
                         [lgtok.tensor.ap().opt()], [lgfull.tensor.ap().opt()])

    if standalone:
        p.build(emit)
    else:
        emit(p)
    return p


def l0a_inmaps(inputs, shards):
    c = host_consts()
    base = {
        "wq": inputs["L0_attn_Wq"], "wk": inputs["L0_attn_Wk"],
        "wv": inputs["L0_attn_Wv"], "wo": inputs["L0_attn_Wo"],
        "wg": inputs["L0_ffn_wg"],
        "alpha": np.ascontiguousarray(inputs["L0_attn_alpha"].reshape(1, 1)),
        "ropeC": c["ropeC"], "ropeS": c["ropeS"], "rotT": c["ropeRotT"],
        "cmask": c["cmask"], "eye_d": c["eye128"],
    }
    base = {k: np.ascontiguousarray(v.astype(np.float32)) for k, v in base.items()}
    maps = []
    for ci in range(NCORE):
        m = dict(base)
        m["xt"] = shards[ci]
        maps.append(m)
    return maps


def shard_x(x):
    out = []
    for ci in range(NCORE):
        xs = x[ci * NSEQ:(ci + 1) * NSEQ].reshape(NSEQ * SEQ, DM)
        out.append(np.ascontiguousarray(xs.T.astype(np.float32)))
    return out


def kernel(**inputs):
    raise NotImplementedError("assembled incrementally; see scratch tests")


CAPL0 = 1280
TRASH0 = 8 * CAPL0
CAP0 = 8192


def build_L0b1(p=None):
    standalone = p is None
    if standalone:
        p = Prog("l0b1")

    def emit(p):
        lg = p.dram("lg", (256 * 128, 8), F32, kind="ExternalInput").ap()
        iota8 = p.dram("iota8", (128, 8), F32, kind="ExternalInput").ap()
        tril = p.dram("tril", (128, 128), F32, kind="ExternalInput").ap()
        slots_o = p.dram("slots_o", (128, 512), U32, kind="ExternalOutput").ap()
        wk_o = p.dram("wk_o", (128, 512), F32, kind="ExternalOutput").ap()

        misc = emit_misc(p)
        psA, psB, psC, psD = misc["psA"], misc["psB"], misc["psC"], misc["psD"]
        onef = misc["ones_b128_f"]
        onecf = p.constf(1.0)

        io8 = p.sbuf("io8", (128, 8), F32)
        trl = p.sbuf("trl", (128, 128), F32)
        p.dma("sync", io8[:, :], iota8[:, :])
        p.dma("sync", trl[:, :], tril[:, :])

        lgt = p.sbuf("lgt", (128, 8), F32)
        v8 = p.sbuf("v8", (128, 8), F32)
        i8 = p.sbuf("i8", (128, 8), U32)
        i8f = p.sbuf("i8f", (128, 8), F32)
        oh = p.sbuf("oh", (128, 8), F32)
        ohk = p.sbuf("ohk", (128, 8), F32)
        locf = p.sbuf("locf", (128, 8), F32)
        junk8 = p.sbuf("junk8", (128, 8), F32)
        colsum = p.sbuf("colsum", (1, 8), F32)
        offrow = p.sbuf("offrow", (1, 8), F32)
        loffrow = p.sbuf("loffrow", (1, 8), F32)
        loffbuf = p.sbuf("loffbuf", (1, 64), F32)
        lsum = p.sbuf("lsum", (1, 8), F32)
        e1 = p.sbuf("e1", (128, 1), F32)
        w0c = p.sbuf("w0c", (128, 1), F32)
        w1c = p.sbuf("w1c", (128, 1), F32)
        posc = p.sbuf("posc", (128, 1), F32)
        keepc = p.sbuf("keepc", (128, 1), F32)
        keeplc = p.sbuf("keeplc", (128, 1), F32)
        lposc = p.sbuf("lposc", (128, 1), F32)
        slotf = p.sbuf("slotf", (128, 1), F32)
        tmpc = p.sbuf("tmpc", (128, 1), F32)
        slots_sb = p.sbuf("slots_sb", (128, 512), U32)
        wk_sb = p.sbuf("wk_sb2", (128, 512), F32)

        p.op("vector", "memset", offrow[:, :], 0.0)

        for kk in range(2):
            for ti in range(256):
                seg = ti // 32
                if ti % 32 == 0:
                    if kk == 0:
                        p.op("vector", "memset", loffrow[:, :], 0.0)
                    else:
                        p.op("vector", "tensor_copy", loffrow[:, :],
                             loffbuf[:, seg * 8:(seg + 1) * 8])
                p.dma("sync", lgt[:, :], lg[ti * 128:(ti + 1) * 128, :])
                p.op("vector", "max", v8[:, :], lgt[:, :])
                p.op("vector", "max_index", i8[:, :], v8[:, :], lgt[:, :])
                p.op("vector", "tensor_copy", i8f[:, :], i8[:, :])
                p.op("vector", "tensor_scalar", ohk[:, :], io8[:, :],
                     i8f[:, kk:kk + 1], None, op0=ALU.is_equal)
                p.op("tensor", "matmul", psB[0:1, 0:8], onecf[:, :], ohk[:, :],
                     start=True, stop=True)
                p.op("vector", "tensor_copy", colsum[:, :], psB[0:1, 0:8])
                p.op("tensor", "matmul", psA[0:128, 0:8], trl[:, :],
                     ohk[:, :], start=True, stop=False)
                p.op("tensor", "matmul", psA[0:128, 0:8], onef[:, :],
                     offrow[:, :], start=False, stop=True, group=True)
                p.op("vector", "tensor_copy", locf[:, :], psA[0:128, 0:8])
                p.op("vector", "scalar_tensor_tensor", junk8[:, :],
                     locf[:, :], 1.0, ohk[:, :], op0=ALU.mult,
                     op1=ALU.mult, accum_out=posc[:, 0:1])
                p.op("vector", "tensor_scalar", posc[:, :], posc[:, :],
                     -1.0, None, op0=ALU.add)
                p.op("vector", "tensor_scalar", keepc[:, :], posc[:, :],
                     float(CAP0), None, op0=ALU.is_lt)
                p.op("vector", "tensor_scalar", oh[:, :], ohk[:, :],
                     keepc[:, 0:1], None, op0=ALU.mult)
                p.op("tensor", "matmul", psC[0:128, 0:8], trl[:, :],
                     oh[:, :], start=True, stop=False)
                p.op("tensor", "matmul", psC[0:128, 0:8], onef[:, :],
                     loffrow[:, :], start=False, stop=True, group=True)
                p.op("vector", "tensor_copy", locf[:, :], psC[0:128, 0:8])
                p.op("vector", "scalar_tensor_tensor", junk8[:, :],
                     locf[:, :], 1.0, oh[:, :], op0=ALU.mult,
                     op1=ALU.mult, accum_out=lposc[:, 0:1])
                p.op("vector", "tensor_scalar", lposc[:, :], lposc[:, :],
                     -1.0, None, op0=ALU.add)
                p.op("vector", "tensor_scalar", keeplc[:, :], lposc[:, :],
                     float(CAPL0), None, op0=ALU.is_lt)
                p.op("vector", "tensor_tensor", keepc[:, :], keepc[:, :],
                     keeplc[:, :], op=ALU.mult)
                p.op("vector", "tensor_scalar", slotf[:, :],
                     i8f[:, kk:kk + 1], float(CAPL0), lposc[:, 0:1],
                     op0=ALU.mult, op1=ALU.add)
                p.op("vector", "tensor_scalar", tmpc[:, :], keepc[:, :],
                     float(-TRASH0), float(TRASH0), op0=ALU.mult, op1=ALU.add)
                p.op("vector", "tensor_tensor", slotf[:, :], slotf[:, :],
                     tmpc[:, :], op=ALU.add)
                p.op("vector", "tensor_scalar", slotf[:, :], slotf[:, :],
                     float(TRASH0), None, op0=ALU.min)
                p.op("vector", "tensor_scalar", slotf[:, :], slotf[:, :],
                     0.0, None, op0=ALU.max)
                p.op("vector", "tensor_copy",
                     slots_sb[:, kk * 256 + ti: kk * 256 + ti + 1], slotf[:, :])
                p.op("vector", "tensor_copy",
                     wk_sb[:, kk * 256 + ti: kk * 256 + ti + 1], keepc[:, :])
                p.op("tensor", "matmul", psD[0:1, 0:8], onecf[:, :],
                     oh[:, :], start=True, stop=True)
                p.op("vector", "tensor_copy", lsum[:, :], psD[0:1, 0:8])
                p.op("vector", "tensor_tensor", loffrow[:, :],
                     loffrow[:, :], lsum[:, :], op=ALU.add)
                p.op("vector", "tensor_tensor", offrow[:, :], offrow[:, :],
                     colsum[:, :], op=ALU.add)
                if ti % 32 == 31 and kk == 0:
                    p.op("vector", "tensor_copy",
                         loffbuf[:, seg * 8:(seg + 1) * 8], loffrow[:, :])

        for ti in range(256):
            p.dma("sync", lgt[:, :], lg[ti * 128:(ti + 1) * 128, :])
            p.op("vector", "max", v8[:, :], lgt[:, :])
            p.op("vector", "tensor_tensor", e1[:, 0:1], v8[:, 1:2],
                 v8[:, 0:1], op=ALU.subtract)
            p.op("scalar", "activation", e1[:, :], e1[:, :], AF.Exp)
            p.op("vector", "tensor_scalar", tmpc[:, :], e1[:, :], 1.0, None,
                 op0=ALU.add)
            p.op("vector", "reciprocal", w0c[:, :], tmpc[:, :])
            p.op("vector", "tensor_tensor", w1c[:, :], e1[:, :], w0c[:, :],
                 op=ALU.mult)
            p.op("vector", "tensor_tensor", w0c[:, :], w0c[:, :],
                 wk_sb[:, ti:ti + 1], op=ALU.mult)
            p.op("vector", "tensor_tensor", w1c[:, :], w1c[:, :],
                 wk_sb[:, 256 + ti:256 + ti + 1], op=ALU.mult)
            p.op("vector", "tensor_copy", wk_sb[:, ti:ti + 1], w0c[:, :])
            p.op("vector", "tensor_copy", wk_sb[:, 256 + ti:256 + ti + 1],
                 w1c[:, :])

        p.dma("gpsimd", slots_o[:, :], slots_sb[:, :])
        p.dma("gpsimd", wk_o[:, :], wk_sb[:, :])
        if p.iomap is not None:
            eyedb1 = p.dram("eyedb1", (128, 128), F32, kind="ExternalInput").ap()
            eyefb1 = p.sbuf("eyefb1", (128, 128), F32)
            p.dma("sync", eyefb1[:, :], eyedb1[:, :])
            slotsT = p.dram("slotsT_s", (512, 128), F32).ap()
            wkT = p.dram("wkT_s", (512, 128), F32).ap()
            slf = p.sbuf("slf", (128, 128), F32)
            for j in range(4):
                p.op("vector", "tensor_copy", slf[:, :],
                     slots_sb[:, j * 128:(j + 1) * 128])
                p.op("tensor", "transpose", psA[:, 0:128], slf[:, :],
                     eyefb1[:, :])
                p.op("vector", "tensor_copy", slf[:, :], psA[:, 0:128])
                p.dma("gpsimd", slotsT[j * 128:(j + 1) * 128, :], slf[:, :])
                p.op("tensor", "transpose", psA[:, 0:128],
                     wk_sb[:, j * 128:(j + 1) * 128], eyefb1[:, :])
                p.op("vector", "tensor_copy", slf[:, :], psA[:, 0:128])
                p.dma("gpsimd", wkT[j * 128:(j + 1) * 128, :], slf[:, :])

    if standalone:
        p.build(emit)
    else:
        emit(p)
    return p


def build_L0b2(p=None):
    standalone = p is None
    if standalone:
        p = Prog("l0b2")

    def emit(p):
        myslots = p.dram("myslots", (128, 64), U32, kind="ExternalInput").ap()
        myw = p.dram("myw", (128, 64), F32, kind="ExternalInput").ap()
        xn_tok = p.dram("xn_tok", (TPC, DM), BF16, kind="ExternalInput").ap()
        xt = p.dram("xt", (DM, TPC), F32, kind="ExternalInput").ap()
        s0 = p.dram("s0", (DM, TPC), BF16, kind="ExternalInput").ap()
        w1 = p.dram("w1", (8, DM, 4096), F32, kind="ExternalInput").ap()
        w2 = p.dram("w2", (8, 4096, DM), F32, kind="ExternalInput").ap()
        b1t = p.dram("b1t", (128, 8 * 32), F32, kind="ExternalInput").ap()
        b2t = p.dram("b2t", (128, 8 * 8), F32, kind="ExternalInput").ap()
        eyed = p.dram("eye_d2", (128, 128), F32, kind="ExternalInput").ap()

        x1_o = p.dram("x1_o", (DM, TPC), F32, kind="ExternalOutput").ap()

        disp = p.dram("disp", (TRASH0 + 1, DM), BF16).ap()
        y_s = p.dram("y_s", (TRASH0 + 1, DM), BF16).ap()

        misc = emit_misc(p)
        psA = misc["psA"]
        eye = p.sbuf("eye_sb2", (128, 128), BF16)
        p.dma("gpsimd", eye[:, :], eyed[:, :])
        slots_sb = p.sbuf("slots_sb", (128, 64), U32)
        wk_sb = p.sbuf("wk_sb2", (128, 64), F32)
        if p.iomap is None:
            p.dma("sync", slots_sb[:, :], myslots[:, :])
            p.dma("sync", wk_sb[:, :], myw[:, :])
        else:
            selidx = p.dram("selidx", (64, 1), U32, kind="ExternalInput").ap()
            slotsT = p.dram("slotsT_s", (512, 128), F32).ap()
            wkT = p.dram("wkT_s", (512, 128), F32).ap()
            eyedb2f = p.sbuf("eyedb2f", (128, 128), F32)
            p.dma("sync", eyedb2f[:, :], eyed[:, :])
            sel_sb = p.sbuf("sel_sb", (64, 1), U32)
            g64 = p.sbuf("g64", (64, 128), F32)
            gT = p.sbuf("gT", (128, 64), F32)
            p.dma("sync", sel_sb[:, :], selidx[:, :])
            p.gather(g64[:, :], slotsT[:, :], sel_sb[:, 0:1])
            p.op("tensor", "transpose", psA[0:128, 0:64], g64[:, :],
                 eyedb2f[0:64, 0:64])
            p.op("vector", "tensor_copy", gT[:, :], psA[0:128, 0:64])
            p.op("vector", "tensor_copy", slots_sb[:, :], gT[:, :])
            p.gather(g64[:, :], wkT[:, :], sel_sb[:, 0:1])
            p.op("tensor", "transpose", psA[0:128, 0:64], g64[:, :],
                 eyedb2f[0:64, 0:64])
            p.op("vector", "tensor_copy", wk_sb[:, :], psA[0:128, 0:64])

        xtok = p.sbuf("xtok", (128, DM), BF16)
        zrow_b = p.sbuf("zrow_b", (1, DM), BF16)
        p.op("vector", "memset", zrow_b[:, :], 0.0)
        p.dma("gpsimd", y_s[TRASH0:TRASH0 + 1, :], zrow_b[:, :])
        for k in range(32):
            p.dma("sync", xtok[:, :], xn_tok[k * 128:(k + 1) * 128, :])
            for kk in range(2):
                p.scatter(disp[:, :], xtok[:, :],
                          slots_sb[:, kk * 32 + k: kk * 32 + k + 1])

        NCH = cdiv(CAPL0, 512)

        def nslice(nb):
            n0 = nb * 512
            return n0, min(CAPL0, n0 + 512)

        with p.scope():
            dT = p.sbuf("dT", (128, DC * CAPL0), BF16)
            hT = p.sbuf("hT", (128, 32 * CAPL0), BF16)
            wfull = p.sbuf("wfull", (128, 32768), BF16)
            b1sb = p.sbuf("b1sb", (128, 8 * 32), F32)
            b2sb = p.sbuf("b2sb", (128, 8 * 8), F32)
            ybf = p.sbuf("ybf", (128, 512), BF16)
            ytr = p.sbuf("ytr", (128, 128), BF16)
            psT = misc["psT"]
            p.dma("sync", b1sb[:, :], b1t[:, :])
            p.dma("sync", b2sb[:, :], b2t[:, :])
            for e in range(8):
                for kc in range(DC):
                    p.dma("sync", dT[:, kc * CAPL0:(kc + 1) * CAPL0],
                          disp[e * CAPL0:(e + 1) * CAPL0,
                               kc * 128:(kc + 1) * 128], transpose=True)
                for kc in range(DC):
                    p.dma("gpsimd", wfull[:, kc * 4096:(kc + 1) * 4096],
                          w1[e, kc * 128:(kc + 1) * 128, :])
                for oc in range(32):
                    for nb in range(NCH):
                        n0, n1 = nslice(nb)
                        for kc in range(DC):
                            p.op("tensor", "matmul", psA[:, 0:n1 - n0],
                                 wfull[:, kc * 4096 + oc * 128:
                                       kc * 4096 + (oc + 1) * 128],
                                 dT[:, kc * CAPL0 + n0: kc * CAPL0 + n1],
                                 start=(kc == 0), stop=(kc == DC - 1),
                                 group=(kc > 0))
                        p.op("scalar", "activation",
                             hT[:, oc * CAPL0 + n0: oc * CAPL0 + n1],
                             psA[:, 0:n1 - n0], AF.Gelu,
                             bias=b1sb[:, e * 32 + oc: e * 32 + oc + 1])
                for hc in range(32):
                    p.dma("gpsimd", wfull[:, hc * 1024:(hc + 1) * 1024],
                          w2[e, hc * 128:(hc + 1) * 128, :])
                for oc in range(DC):
                    for nb in range(NCH):
                        n0, n1 = nslice(nb)
                        for hc in range(32):
                            p.op("tensor", "matmul", psA[:, 0:n1 - n0],
                                 wfull[:, hc * 1024 + oc * 128:
                                       hc * 1024 + (oc + 1) * 128],
                                 hT[:, hc * CAPL0 + n0: hc * CAPL0 + n1],
                                 start=(hc == 0), stop=(hc == 31),
                                 group=(hc > 0))
                        p.op("scalar", "activation", ybf[:, 0:n1 - n0],
                             psA[:, 0:n1 - n0], AF.Identity,
                             bias=b2sb[:, e * 8 + oc: e * 8 + oc + 1])
                        for j in range(cdiv(n1 - n0, 128)):
                            p.op("tensor", "transpose", psT[:, 0:128],
                                 ybf[:, j * 128:(j + 1) * 128], eye[:, :])
                            p.op("vector", "tensor_copy", ytr[:, :],
                                 psT[:, 0:128])
                            p.dma("gpsimd",
                                  y_s[e * CAPL0 + n0 + j * 128:
                                      e * CAPL0 + n0 + (j + 1) * 128,
                                      oc * 128:(oc + 1) * 128], ytr[:, :])

        with p.scope():
            y0g = p.sbuf("y0g", (128, DM), BF16)
            y1g = p.sbuf("y1g", (128, DM), BF16)
            moe = p.sbuf("moe", (128, DM), F32)
            gsil = p.sbuf("gsil", (128, DM), BF16)
            gtr = p.sbuf("gtr", (128, 128), BF16)
            xf1 = p.sbuf("xf1", (128, 128), F32)
            sf1 = p.sbuf("sf1", (128, 128), BF16)
            psT = misc["psT"]
            p.op("vector", "memset", y0g[:, :], 0.0)
            p.op("vector", "memset", y1g[:, :], 0.0)
            for k in range(32):
                p.gather(y0g[:, :], y_s[:, :], slots_sb[:, k:k + 1])
                p.gather(y1g[:, :], y_s[:, :], slots_sb[:, 32 + k:32 + k + 1])
                p.op("vector", "tensor_scalar", moe[:, :], y0g[:, :],
                     wk_sb[:, k:k + 1], None, op0=ALU.mult)
                p.op("vector", "scalar_tensor_tensor", moe[:, :], y1g[:, :],
                     wk_sb[:, 32 + k:32 + k + 1], moe[:, :], op0=ALU.mult,
                     op1=ALU.add)
                p.op("scalar", "activation", gsil[:, :], moe[:, :], AF.Silu)
                for kc in range(DC):
                    p.op("tensor", "transpose", psT[:, 0:128],
                         gsil[:, kc * 128:(kc + 1) * 128], eye[:, :])
                    p.op("vector", "tensor_copy", gtr[:, :], psT[:, 0:128])
                    p.dma("sync", xf1[:, :],
                          xt[kc * 128:(kc + 1) * 128, k * 128:(k + 1) * 128])
                    p.dma("sync", sf1[:, :],
                          s0[kc * 128:(kc + 1) * 128, k * 128:(k + 1) * 128])
                    p.op("vector", "tensor_tensor", gtr[:, :], gtr[:, :],
                         sf1[:, :], op=ALU.mult)
                    p.op("vector", "tensor_tensor", xf1[:, :], xf1[:, :],
                         gtr[:, :], op=ALU.add)
                    p.dma("gpsimd",
                          x1_o[kc * 128:(kc + 1) * 128, k * 128:(k + 1) * 128],
                          xf1[:, :])

    if standalone:
        p.build(emit)
    else:
        emit(p)
    return p


def moe_host_consts(inputs):
    b1 = inputs["L0_ffn_b1"].astype(np.float32)
    b2 = inputs["L0_ffn_b2"].astype(np.float32)
    b1t = np.ascontiguousarray(
        b1.reshape(8, 32, 128).transpose(2, 0, 1).reshape(128, 256))
    b2t = np.ascontiguousarray(
        b2.reshape(8, 8, 128).transpose(2, 0, 1).reshape(128, 64))
    iota8 = np.ascontiguousarray(np.tile(np.arange(8, dtype=np.float32), (128, 1)))
    tril = np.ascontiguousarray(np.triu(np.ones((128, 128), np.float32)))
    return b1t, b2t, iota8, tril


def build_L1a_pre(p=None):
    standalone = p is None
    if standalone:
        p = Prog("l1a_pre")

    def emit(p):
        xt = p.dram("xt", (DM, TPC), F32, kind="ExternalInput").ap()
        wqT = p.dram("wqT", (DM, DM), F32, kind="ExternalInput").ap()
        wkT = p.dram("wkT", (DM, DM), F32, kind="ExternalInput").ap()
        wvT = p.dram("wvT", (DM, DM), F32, kind="ExternalInput").ap()
        wgT = p.dram("wgT", (DM, DM), F32, kind="ExternalInput").ap()
        wbT = p.dram("wbT", (DM, 16), F32, kind="ExternalInput").ap()
        wgkT = p.dram("wgkT", (DM, 16), F32, kind="ExternalInput").ap()
        cw = p.dram("cw", (3, 128, DC * 4), F32, kind="ExternalInput").ap()
        cb = p.dram("cb", (3, 128, DC), F32, kind="ExternalInput").ap()
        bbv = p.dram("bbv", (16, 1), F32, kind="ExternalInput").ap()
        bgkv = p.dram("bgkv", (16, 1), F32, kind="ExternalInput").ap()
        dtbv = p.dram("dtbv", (16, 1), F32, kind="ExternalInput").ap()
        alogv = p.dram("alogv", (16, 1), F32, kind="ExternalInput").ap()
        ropeC = p.dram("ropeC", (128, SEQ), F32, kind="ExternalInput").ap()
        ropeS = p.dram("ropeS", (128, SEQ), F32, kind="ExternalInput").ap()
        rotT = p.dram("rotT", (128, 128), F32, kind="ExternalInput").ap()
        bc2 = p.dram("bc2", (2, 128), F32, kind="ExternalInput").ap()
        eyed = p.dram("eye_d", (128, 128), F32, kind="ExternalInput").ap()

        xn_o = p.dram("xn_o", (DM, TPC), BF16, kind="ExternalOutput").ap()
        qf_o = p.dram("qf_o", (DM, TPC), BF16, kind="ExternalOutput").ap()
        kf_o = p.dram("kf_o", (DM, TPC), BF16, kind="ExternalOutput").ap()
        vf_o = p.dram("vf_o", (DM, TPC), BF16, kind="ExternalOutput").ap()
        g_o = p.dram("g_o", (DM, TPC), BF16, kind="ExternalOutput").ap()
        qt_o = p.dram("qt_o", (TPC, DM), BF16, kind="ExternalOutput").ap()
        kt_o = p.dram("kt_o", (TPC, DM), BF16, kind="ExternalOutput").ap()
        vt_o = p.dram("vt_o", (TPC, DM), BF16, kind="ExternalOutput").ap()
        la_o = p.dram("la_o", (16, TPC), F32, kind="ExternalOutput").ap()
        be_o = p.dram("be_o", (16, TPC), F32, kind="ExternalOutput").ap()

        misc = emit_misc(p)
        psA, psB, psC, psD = misc["psA"], misc["psB"], misc["psC"], misc["psD"]
        onef = misc["ones_b128_f"]

        xnbuf = p.sbuf("xnbuf", (128, DC * TPC), BF16)
        with p.scope():
            emit_ln(p, xt, xnbuf, misc)
        for kc in range(DC):
            p.dma("scalar", xn_o[kc * 128:(kc + 1) * 128, :],
                  xnbuf[:, kc * TPC:(kc + 1) * TPC])

        w_sb = p.sbuf("w_sb", (128, DC * DM), BF16)
        rc = p.sbuf("ropeC_sb", (128, SEQ), F32)
        rs = p.sbuf("ropeS_sb", (128, SEQ), F32)
        rt = p.sbuf("rotT_sb", (128, 128), F32)
        eye = p.sbuf("eye_sb", (128, 128), BF16)
        bc2s = p.sbuf("bc2s", (2, 128), F32)
        cws = p.sbuf("cws", (128, DC * 4), F32)
        cbs = p.sbuf("cbs", (128, DC), F32)
        p.dma("sync", rc[:, :], ropeC[:, :])
        p.dma("sync", rs[:, :], ropeS[:, :])
        p.dma("sync", rt[:, :], rotT[:, :])
        p.dma("gpsimd", eye[:, :], eyed[:, :])
        p.dma("sync", bc2s[:, :], bc2[:, :])

        cpad = p.sbuf("cpad", (128, SEQ + 3), F32)
        cv1 = p.sbuf("cv1", (128, SEQ), F32)
        cv2 = p.sbuf("cv2", (128, SEQ), F32)
        qrp = p.sbuf("qrp", (128, SEQ), F32)
        t1 = p.sbuf("t1b", (128, SEQ), F32)
        sqv = p.sbuf("sqv", (128, SEQ), BF16)
        nrm = p.sbuf("nrm", (2, SEQ), F32)
        nrmb = p.sbuf("nrmb", (128, SEQ), F32)
        outb = p.sbuf("outb", (128, SEQ), BF16)
        trb = p.sbuf("trb", (128, 128), BF16)
        onesb2 = p.sbuf("onesb2", (128, 2), BF16)
        p.op("vector", "memset", onesb2[:, :], 0.0)
        p.op("vector", "memset", onesb2[0:64, 0:1], 1.0)
        p.op("vector", "memset", onesb2[64:128, 1:2], 1.0)
        psT = misc["psT"]

        def proj_conv(wT_d, ci, f_out, t_out, do_rope_norm):
            load_w_bf16(p, w_sb, wT_d, DM)
            p.dma("sync", cws[:, :], cw[ci, :, :])
            p.dma("sync", cbs[:, :], cb[ci, :, :])
            for oc in range(DC):
                for b in range(NSEQ):
                    p.op("vector", "memset", cpad[:, 0:3], 0.0)
                    for tb in range(4):
                        bank = [psA, psB][tb % 2]
                        gtb = b * 4 + tb
                        for kc in range(DC):
                            p.op("tensor", "matmul", bank[:, :],
                                 w_sb[:, kc * DM + oc * 128: kc * DM + (oc + 1) * 128],
                                 xnbuf[:, kc * TPC + gtb * 512: kc * TPC + (gtb + 1) * 512],
                                 start=(kc == 0), stop=(kc == DC - 1),
                                 group=(kc > 0))
                        p.op("vector", "tensor_copy",
                             cpad[:, 3 + tb * 512: 3 + (tb + 1) * 512],
                             bank[:, :])
                    p.op("vector", "tensor_scalar", cv1[:, :], cpad[:, 0:SEQ],
                         cws[:, oc * 4 + 0: oc * 4 + 1], None, op0=ALU.mult)
                    for j in range(1, 4):
                        src, dst = (cv1, cv2) if j % 2 == 1 else (cv2, cv1)
                        p.op("vector", "scalar_tensor_tensor", dst[:, :],
                             cpad[:, j:j + SEQ], cws[:, oc * 4 + j: oc * 4 + j + 1],
                             src[:, :], op0=ALU.mult, op1=ALU.add)
                    p.op("scalar", "activation", qrp[:, :], cv2[:, :], AF.Silu,
                         bias=cbs[:, oc:oc + 1])
                    if do_rope_norm:
                        p.op("tensor", "matmul", psC[:, 0:512], rt[:, :],
                             qrp[:, 0:512], start=True, stop=True)
                        p.op("tensor", "matmul", psD[:, 0:512], rt[:, :],
                             qrp[:, 512:1024], start=True, stop=True)
                        p.op("vector", "tensor_tensor", t1[:, 0:512],
                             psC[:, 0:512], rs[:, 0:512], op=ALU.mult)
                        p.op("vector", "tensor_tensor", t1[:, 512:1024],
                             psD[:, 0:512], rs[:, 512:1024], op=ALU.mult)
                        p.op("tensor", "matmul", psC[:, 0:512], rt[:, :],
                             qrp[:, 1024:1536], start=True, stop=True)
                        p.op("tensor", "matmul", psD[:, 0:512], rt[:, :],
                             qrp[:, 1536:2048], start=True, stop=True)
                        p.op("vector", "tensor_tensor", t1[:, 1024:1536],
                             psC[:, 0:512], rs[:, 1024:1536], op=ALU.mult)
                        p.op("vector", "tensor_tensor", t1[:, 1536:2048],
                             psD[:, 0:512], rs[:, 1536:2048], op=ALU.mult)
                        p.op("vector", "tensor_tensor", qrp[:, :], qrp[:, :],
                             rc[:, :], op=ALU.mult)
                        p.op("vector", "tensor_tensor", qrp[:, :], qrp[:, :],
                             t1[:, :], op=ALU.add)
                        p.op("scalar", "activation", sqv[:, :], qrp[:, :],
                             AF.Square)
                        for ts in range(4):
                            p.op("tensor", "matmul", psC[0:2, 0:512],
                                 onesb2[:, :], sqv[:, ts * 512:(ts + 1) * 512],
                                 start=True, stop=True)
                            p.op("vector", "tensor_copy",
                                 nrm[:, ts * 512:(ts + 1) * 512], psC[0:2, 0:512])
                        p.op("scalar", "activation", nrm[:, :], nrm[:, :],
                             AF.Ln, bias=1e-12)
                        p.op("scalar", "activation", nrm[:, :], nrm[:, :],
                             AF.Exp, scale=-0.5)
                        p.op("tensor", "matmul", psC[:, 0:512], bc2s[:, :], nrm[:, 0:512], start=True, stop=True)
                        p.op("tensor", "matmul", psD[:, 0:512], bc2s[:, :],
                             nrm[:, 512:1024], start=True, stop=True)
                        p.op("vector", "tensor_tensor", qrp[:, 0:512],
                             qrp[:, 0:512], psC[:, 0:512], op=ALU.mult)
                        p.op("vector", "tensor_tensor", qrp[:, 512:1024],
                             qrp[:, 512:1024], psD[:, 0:512], op=ALU.mult)
                        p.op("tensor", "matmul", psC[:, 0:512], bc2s[:, :],
                             nrm[:, 1024:1536], start=True, stop=True)
                        p.op("tensor", "matmul", psD[:, 0:512], bc2s[:, :],
                             nrm[:, 1536:2048], start=True, stop=True)
                        p.op("vector", "tensor_tensor", qrp[:, 1024:1536],
                             qrp[:, 1024:1536], psC[:, 0:512], op=ALU.mult)
                        p.op("vector", "tensor_tensor", qrp[:, 1536:2048],
                             qrp[:, 1536:2048], psD[:, 0:512], op=ALU.mult)
                    p.op("vector", "tensor_copy", outb[:, :], qrp[:, :])
                    p.dma("gpsimd",
                          f_out[oc * 128:(oc + 1) * 128,
                                b * SEQ:(b + 1) * SEQ], outb[:, :])
                    for j in range(16):
                        p.op("tensor", "transpose", psT[:, 0:128],
                             outb[:, j * 128:(j + 1) * 128], eye[:, :])
                        p.op("vector", "tensor_copy", trb[:, :], psT[:, 0:128])
                        p.dma("gpsimd",
                              t_out[b * SEQ + j * 128: b * SEQ + (j + 1) * 128,
                                    oc * 128:(oc + 1) * 128], trb[:, :])

        proj_conv(wqT, 0, qf_o, qt_o, True)
        proj_conv(wkT, 1, kf_o, kt_o, True)
        proj_conv(wvT, 2, vf_o, vt_o, False)

        load_w_bf16(p, w_sb, wgT, DM)
        gb = p.sbuf("gb", (128, 512), BF16)

        def cb_g(oc, tb, bank):
            p.op("vector", "tensor_copy", gb[:, :], bank[:, :])
            p.dma("gpsimd", g_o[oc * 128:(oc + 1) * 128,
                               tb * 512:(tb + 1) * 512], gb[:, :])
        emit_proj(p, w_sb, xnbuf, cb_g, [psA, psB])

        w16 = p.sbuf("w16", (128, DC * 16), BF16)
        bbs = p.sbuf("bbs", (16, 1), F32)
        bgks = p.sbuf("bgks", (16, 1), F32)
        dtbs = p.sbuf("dtbs", (16, 1), F32)
        alogs = p.sbuf("alogs", (16, 1), F32)
        negc = p.sbuf("negc", (16, 1), F32)
        zrow = p.sbuf("zrow16", (16, 512), F32)
        sp1 = p.sbuf("sp1", (16, 512), F32)
        sp2 = p.sbuf("sp2", (16, 512), F32)
        brow = p.sbuf("brow", (16, 512), F32)
        p.dma("sync", bbs[:, :], bbv[:, :])
        p.dma("sync", bgks[:, :], bgkv[:, :])
        p.dma("sync", dtbs[:, :], dtbv[:, :])
        p.dma("sync", alogs[:, :], alogv[:, :])
        p.op("scalar", "activation", negc[:, :], alogs[:, :], AF.Exp)
        p.op("vector", "tensor_scalar", negc[:, :], negc[:, :], -1.0, None,
             op0=ALU.mult)
        load_w_bf16(p, w16, wbT, 16)
        w16b = p.sbuf("w16b", (128, DC * 16), BF16)
        load_w_bf16(p, w16b, wgkT, 16)
        for tb in range(8):
            for kc in range(DC):
                p.op("tensor", "matmul", psA[0:16, :],
                     w16[:, kc * 16:(kc + 1) * 16],
                     xnbuf[:, kc * TPC + tb * 512: kc * TPC + (tb + 1) * 512],
                     start=(kc == 0), stop=(kc == DC - 1), group=(kc > 0))
            p.op("scalar", "activation", brow[:, :], psA[0:16, :], AF.Sigmoid,
                 bias=bbs[:, 0:1])
            p.dma("gpsimd", be_o[:, tb * 512:(tb + 1) * 512], brow[:, :])
            for kc in range(DC):
                p.op("tensor", "matmul", psB[0:16, :],
                     w16b[:, kc * 16:(kc + 1) * 16],
                     xnbuf[:, kc * TPC + tb * 512: kc * TPC + (tb + 1) * 512],
                     start=(kc == 0), stop=(kc == DC - 1), group=(kc > 0))
            p.op("vector", "tensor_scalar", zrow[:, :], psB[0:16, :],
                 bgks[:, 0:1], dtbs[:, 0:1], op0=ALU.add, op1=ALU.add)
            p.op("scalar", "activation", sp1[:, :], zrow[:, :], AF.Abs)
            p.op("scalar", "activation", sp1[:, :], sp1[:, :], AF.Exp,
                 scale=-1.0)
            p.op("scalar", "activation", sp1[:, :], sp1[:, :], AF.Ln, bias=1.0)
            p.op("scalar", "activation", sp2[:, :], zrow[:, :], AF.Relu)
            p.op("vector", "tensor_tensor", sp1[:, :], sp1[:, :], sp2[:, :],
                 op=ALU.add)
            p.op("vector", "tensor_scalar", sp1[:, :], sp1[:, :],
                 negc[:, 0:1], None, op0=ALU.mult)
            p.dma("gpsimd", la_o[:, tb * 512:(tb + 1) * 512], sp1[:, :])

    if standalone:
        p.build(emit)
    else:
        emit(p)
    return p


CHK = 128
NCHK = SEQ // CHK


def build_L1a_scan(p=None):
    standalone = p is None
    if standalone:
        p = Prog("l1a_scan")

    def emit(p):
        qf = p.dram("qf", (DM, SEQ), BF16, kind="ExternalInput").ap()
        kf = p.dram("kf", (DM, SEQ), BF16, kind="ExternalInput").ap()
        qt = p.dram("qt", (SEQ, DM), BF16, kind="ExternalInput").ap()
        kt = p.dram("kt", (SEQ, DM), BF16, kind="ExternalInput").ap()
        vt = p.dram("vt", (SEQ, DM), BF16, kind="ExternalInput").ap()
        la = p.dram("la", (16, SEQ), F32, kind="ExternalInput").ap()
        be = p.dram("be", (16, SEQ), F32, kind="ExternalInput").ap()
        masksu = p.dram("masksu", (128, 128), F32, kind="ExternalInput").ap()
        masksl = p.dram("masksl", (128, 128), F32, kind="ExternalInput").ap()
        eyed = p.dram("eye_d", (128, 128), F32, kind="ExternalInput").ap()

        of_o = p.dram("of_o", (DM, SEQ), BF16, kind="ExternalOutput").ap()
        rows_s = p.dram("rows_s", (7, 16, SEQ), F32).ap()

        misc = emit_misc(p)
        psA, psB, psC, psD, psE, psF = (misc["psA"], misc["psB"], misc["psC"],
                                        misc["psD"], misc["psE"], misc["psF"])
        psT = misc["psT"]
        onef = misc["ones_b128_f"]

        su = p.sbuf("su", (128, 128), F32)
        sl = p.sbuf("sl", (128, 128), F32)
        eyeb = p.sbuf("eyeb", (128, 128), BF16)
        eyef = p.sbuf("eyef", (128, 128), F32)
        p.dma("sync", su[:, :], masksu[:, :])
        p.dma("sync", sl[:, :], masksl[:, :])
        p.dma("gpsimd", eyeb[:, :], eyed[:, :])
        p.dma("sync", eyef[:, :], eyed[:, :])

        with p.scope():
            lar = p.sbuf("lar", (16, SEQ), F32)
            ber = p.sbuf("ber", (16, SEQ), F32)
            Lr = p.sbuf("Lr", (16, SEQ), F32)
            tmp = p.sbuf("tmpr", (16, SEQ), F32)
            ones16 = p.sbuf("ones16", (16, SEQ), F32)
            p.dma("sync", lar[:, :], la[:, :])
            p.dma("sync", ber[:, :], be[:, :])
            p.op("vector", "memset", ones16[:, :], 1.0)
            for c in range(NCHK):
                s = slice(c * CHK, (c + 1) * CHK)
                p.op("vector", "tensor_tensor_scan", Lr[:, s], ones16[:, s],
                     lar[:, s], 0.0, op0=ALU.mult, op1=ALU.add)
            p.dma("gpsimd", rows_s[0, :, :], Lr[:, :])
            p.op("scalar", "activation", tmp[:, :], ber[:, :], AF.Ln)
            p.op("vector", "tensor_tensor", tmp[:, :], tmp[:, :], Lr[:, :],
                 op=ALU.add)
            p.dma("gpsimd", rows_s[1, :, :], tmp[:, :])
            p.op("scalar", "activation", tmp[:, :], tmp[:, :], AF.Exp)
            p.dma("gpsimd", rows_s[6, :, :], tmp[:, :])
            p.op("vector", "tensor_scalar", tmp[:, :], Lr[:, :], -1.0, None,
                 op0=ALU.mult)
            p.dma("gpsimd", rows_s[2, :, :], tmp[:, :])
            p.op("vector", "tensor_tensor", tmp[:, :], Lr[:, :], lar[:, :],
                 op=ALU.subtract)
            p.dma("gpsimd", rows_s[3, :, :], tmp[:, :])
            p.op("scalar", "activation", tmp[:, :], tmp[:, :], AF.Exp)
            p.dma("gpsimd", rows_s[4, :, :], tmp[:, :])
            p.op("scalar", "activation", tmp[:, :], Lr[:, :], AF.Exp)
            p.dma("gpsimd", rows_s[5, :, :], tmp[:, :])

        kfsl = p.sbuf("kfsl", (128, DC * CHK), BF16)
        qfsl = p.sbuf("qfsl", (128, DC * CHK), BF16)
        ktsl = p.sbuf("ktsl", (128, DM), BF16)
        qtsl = p.sbuf("qtsl", (128, DM), BF16)
        vtsl = p.sbuf("vtsl", (128, DM), BF16)
        colsL = p.sbuf("colsL", (128, 16), F32)
        colsLB = p.sbuf("colsLB", (128, 16), F32)
        colsBA = p.sbuf("colsBA", (128, 16), F32)
        colsBE = p.sbuf("colsBE", (128, 16), F32)
        colsRT = p.sbuf("colsRT", (128, 16), F32)
        r16 = p.sbuf("r16", (16, CHK), F32)
        lastc = p.sbuf("lastc", (16, 1), F32)
        rowLb = p.sbuf("rowLb", (1, CHK), F32)
        rowNL = p.sbuf("rowNL", (1, CHK), F32)
        rowLp = p.sbuf("rowLp", (1, CHK), F32)
        rowAb = p.sbuf("rowAb", (1, CHK), F32)
        ac11 = p.sbuf("ac11", (1, 1), F32)
        accol = p.sbuf("accol", (64, 1), F32)
        Ebuild = p.sbuf("Ebuild", (128, 128), F32)
        Wm = p.sbuf("Wm", (128, 128), F32)
        Wmt = p.sbuf("Wmt", (128, 128), F32)
        Htm = p.sbuf("Htm", (128, 128), F32)
        PWa = p.sbuf("PWa", (128, 128), F32)
        PWb = p.sbuf("PWb", (128, 128), F32)
        WT = [p.sbuf("WTl%d" % i, (128, 128), F32) for i in range(6)]
        Yst = p.sbuf("Yst", (128, 128), F32)
        Yst2 = p.sbuf("Yst2", (128, 128), F32)
        Wbart = p.sbuf("Wbart", (64, 128), F32)
        Ubf = p.sbuf("Ubf", (128, 64), F32)
        AQf = p.sbuf("AQf", (64, 128), F32)
        Ktil = p.sbuf("Ktil", (128, 64), F32)
        Obf = p.sbuf("Obf", (128, 64), F32)
        Otr = p.sbuf("Otr", (64, 128), F32)
        S0f = [p.sbuf("S0f%d" % h, (64, 64), F32) for h in range(16)]
        S0b = p.sbuf("S0b", (64, 64), F32)
        stmp = p.sbuf("stmp", (64, 64), F32)

        for h in range(16):
            p.op("vector", "memset", S0f[h][:, :], 0.0)

        def col_tile(row_idx, dst, do_exp=False):
            p.dma("sync", r16[:, :], rows_s[row_idx, :, cs])
            if do_exp:
                p.op("scalar", "activation", r16[:, :], r16[:, :], AF.Exp)
            p.op("tensor", "transpose", psF[0:128, 0:16], r16[:, :],
                 eyef[0:16, 0:16])
            p.op("vector", "tensor_copy", dst[:, :], psF[0:128, 0:16])

        for c in range(NCHK):
            cs = slice(c * CHK, (c + 1) * CHK)
            for kc2 in range(DC):
                p.dma("sync", kfsl[:, kc2 * CHK:(kc2 + 1) * CHK],
                      kf[kc2 * 128:(kc2 + 1) * 128, cs])
                p.dma("sync", qfsl[:, kc2 * CHK:(kc2 + 1) * CHK],
                      qf[kc2 * 128:(kc2 + 1) * 128, cs])
            p.dma("sync", ktsl[:, :], kt[cs, :])
            p.dma("sync", qtsl[:, :], qt[cs, :])
            p.dma("sync", vtsl[:, :], vt[cs, :])
            col_tile(0, colsL)
            col_tile(1, colsLB)
            col_tile(6, colsBA)
            p.dma("sync", r16[:, :], be[:, cs])
            p.op("tensor", "transpose", psF[0:128, 0:16], r16[:, :],
                 eyef[0:16, 0:16])
            p.op("vector", "tensor_copy", colsBE[:, :], psF[0:128, 0:16])
            p.dma("sync", r16[:, :], rows_s[2, :, cs])
            p.dma("sync", lastc[:, :],
                  rows_s[0, :, c * CHK + CHK - 1: c * CHK + CHK],
                  allow_slow_non_contiguous=True)
            p.op("vector", "tensor_scalar", r16[:, :], r16[:, :],
                 lastc[:, 0:1], None, op0=ALU.add)
            p.op("scalar", "activation", r16[:, :], r16[:, :], AF.Exp)
            p.op("tensor", "transpose", psF[0:128, 0:16], r16[:, :],
                 eyef[0:16, 0:16])
            p.op("vector", "tensor_copy", colsRT[:, :], psF[0:128, 0:16])

            for h in range(16):
                kc_, rr = h // 2, (h % 2) * 64
                Kf = kfsl[rr:rr + 64, kc_ * CHK:(kc_ + 1) * CHK]
                Qf = qfsl[rr:rr + 64, kc_ * CHK:(kc_ + 1) * CHK]
                Kt_ = ktsl[:, h * 64:(h + 1) * 64]
                Qt_ = qtsl[:, h * 64:(h + 1) * 64]
                Vt_ = vtsl[:, h * 64:(h + 1) * 64]
                p.dma("sync", rowLb[:, :], rows_s[1, h:h + 1, cs])
                p.dma("sync", rowNL[:, :], rows_s[2, h:h + 1, cs])
                p.dma("sync", rowLp[:, :], rows_s[3, h:h + 1, cs])
                p.dma("sync", rowAb[:, :], rows_s[4, h:h + 1, cs])
                p.dma("sync", ac11[:, :],
                      rows_s[5, h:h + 1, c * CHK + CHK - 1: c * CHK + CHK])
                p.op("vector", "tensor_copy", S0b[:, :], S0f[h][:, :])
                p.op("tensor", "matmul", psA[:, 0:128], Kf, Kf,
                     start=True, stop=True)
                p.op("tensor", "matmul", psB[:, 0:128], Kf, Qf,
                     start=True, stop=True)
                p.op("tensor", "matmul", psC[:, 0:128], onef[:, :],
                     rowLb[:, :], start=True, stop=True)
                p.op("vector", "tensor_scalar", Ebuild[:, :], psC[:, 0:128],
                     colsL[:, h:h + 1], 0.0, op0=ALU.subtract, op1=ALU.min)
                p.op("scalar", "activation", Ebuild[:, :], Ebuild[:, :], AF.Exp)
                p.op("vector", "tensor_tensor", Ebuild[:, :], Ebuild[:, :],
                     su[:, :], op=ALU.mult)
                p.op("vector", "tensor_tensor", Wmt[:, :], Ebuild[:, :],
                     psA[:, 0:128], op=ALU.mult)
                p.op("tensor", "matmul", psC[:, 0:128], onef[:, :],
                     rowNL[:, :], start=True, stop=True)
                p.op("vector", "tensor_scalar", Ebuild[:, :], psC[:, 0:128],
                     colsLB[:, h:h + 1], 0.0, op0=ALU.add, op1=ALU.min)
                p.op("scalar", "activation", Ebuild[:, :], Ebuild[:, :], AF.Exp)
                p.op("vector", "tensor_tensor", Ebuild[:, :], Ebuild[:, :],
                     sl[:, :], op=ALU.mult)
                p.op("vector", "tensor_tensor", Wm[:, :], Ebuild[:, :],
                     psA[:, 0:128], op=ALU.mult)
                p.op("tensor", "matmul", psC[:, 0:128], onef[:, :],
                     rowLp[:, :], start=True, stop=True)
                p.op("vector", "tensor_scalar", Ebuild[:, :], psC[:, 0:128],
                     colsL[:, h:h + 1], 0.0, op0=ALU.subtract, op1=ALU.min)
                p.op("scalar", "activation", Ebuild[:, :], Ebuild[:, :], AF.Exp)
                p.op("vector", "tensor_tensor", Ebuild[:, :], Ebuild[:, :],
                     su[:, :], op=ALU.mult)
                p.op("vector", "tensor_tensor", Htm[:, :], Ebuild[:, :],
                     psB[:, 0:128], op=ALU.mult)
                srcW, srcWt = Wm, Wmt
                for lev in range(6):
                    dstW = PWa if lev % 2 == 0 else PWb
                    p.op("tensor", "matmul", psC[:, 0:128], srcWt[:, :],
                         srcW[:, :], start=True, stop=True)
                    p.op("vector", "tensor_copy", dstW[:, :], psC[:, 0:128])
                    p.op("tensor", "matmul", psC[:, 0:128], srcW[:, :],
                         srcWt[:, :], start=True, stop=True)
                    p.op("vector", "tensor_copy", WT[lev][:, :], psC[:, 0:128])
                    srcW, srcWt = dstW, WT[lev]
                p.op("vector", "tensor_scalar", Yst[:, 0:64], Vt_,
                     colsBE[:, h:h + 1], None, op0=ALU.mult)
                p.op("vector", "tensor_scalar", Yst[:, 64:128], Kt_,
                     colsBA[:, h:h + 1], None, op0=ALU.mult)
                cur, oth = Yst, Yst2
                for lev in range(5, -1, -1):
                    p.op("tensor", "matmul", psC[:, 0:128], WT[lev][:, :],
                         cur[:, :], start=True, stop=True)
                    p.op("vector", "tensor_tensor", oth[:, :], cur[:, :],
                         psC[:, 0:128], op=ALU.add)
                    cur, oth = oth, cur
                p.op("tensor", "matmul", psC[:, 0:128], Wmt[:, :], cur[:, :],
                     start=True, stop=True)
                p.op("vector", "tensor_tensor", oth[:, :], cur[:, :],
                     psC[:, 0:128], op=ALU.subtract)
                cur = oth
                p.op("tensor", "transpose", psF[0:64, 0:128], cur[:, 64:128],
                     eyef[:, :])
                p.op("vector", "tensor_copy", Wbart[:, :], psF[0:64, 0:128])
                p.op("tensor", "matmul", psD[:, 0:64], Wbart[:, :], S0b[:, :],
                     start=True, stop=True)
                p.op("vector", "tensor_tensor", Ubf[:, :], cur[:, 0:64],
                     psD[:, 0:64], op=ALU.subtract)
                p.op("tensor", "matmul", psF[0:64, 0:128], onef[:, 0:64],
                     rowAb[:, :], start=True, stop=True)
                p.op("vector", "tensor_tensor", AQf[:, :], Qf, psF[0:64, 0:128],
                     op=ALU.mult)
                p.op("tensor", "matmul", psE[:, 0:64], Htm[:, :], Ubf[:, :],
                     start=True, stop=False)
                p.op("tensor", "matmul", psE[:, 0:64], AQf[:, :], S0b[:, :],
                     start=False, stop=True, group=True, skip_group_check=True)
                p.op("vector", "tensor_copy", Obf[:, :], psE[:, 0:64])
                p.op("tensor", "transpose", psF[0:64, 0:128], Obf[:, :],
                     eyef[:, :])
                p.op("vector", "tensor_copy", Otr[:, :], psF[0:64, 0:128])
                p.dma("gpsimd", of_o[h * 64:(h + 1) * 64, cs], Otr[:, :])
                p.op("vector", "tensor_scalar", Ktil[:, :], Kt_,
                     colsRT[:, h:h + 1], None, op0=ALU.mult)
                p.op("tensor", "matmul", psD[0:64, 0:64], Ktil[:, :],
                     Ubf[:, :], start=True, stop=True)
                p.op("tensor", "matmul", psF[0:64, 0:1], onef[:, 0:64],
                     ac11[:, :], start=True, stop=True)
                p.op("vector", "tensor_copy", accol[:, :], psF[0:64, 0:1])
                p.op("vector", "tensor_scalar", stmp[:, :], S0f[h][:, :],
                     accol[:, 0:1], None, op0=ALU.mult)
                p.op("vector", "tensor_tensor", S0f[h][:, :], stmp[:, :],
                     psD[0:64, 0:64], op=ALU.add)

    if standalone:
        p.build(emit)
    else:
        emit(p)
    return p


def build_L1a_out(p=None):
    standalone = p is None
    if standalone:
        p = Prog("l1a_out")

    def emit(p):
        qf = p.dram("qf", (DM, TPC), BF16, kind="ExternalInput").ap()
        kf = p.dram("kf", (DM, TPC), BF16, kind="ExternalInput").ap()
        vf = p.dram("vf", (DM, TPC), BF16, kind="ExternalInput").ap()
        gf = p.dram("gf", (DM, TPC), BF16, kind="ExternalInput").ap()
        off = p.dram("off", (DM, TPC), BF16, kind="ExternalInput").ap()
        woT = p.dram("woT", (DM, DM), F32, kind="ExternalInput").ap()
        dp2 = p.dram("dp2", (2, DC), F32, kind="ExternalInput").ap()
        onw = p.dram("onw", (128, 1), F32, kind="ExternalInput").ap()
        bc2 = p.dram("bc2", (2, 128), F32, kind="ExternalInput").ap()

        s1_o = p.dram("s1_o", (DM, TPC), BF16, kind="ExternalOutput").ap()

        misc = emit_misc(p)
        psA, psB, psC, psD = misc["psA"], misc["psB"], misc["psC"], misc["psD"]
        onesb2 = p.sbuf("onesb2", (128, 2), BF16)
        p.op("vector", "memset", onesb2[:, :], 0.0)
        p.op("vector", "memset", onesb2[0:64, 0:1], 1.0)
        p.op("vector", "memset", onesb2[64:128, 1:2], 1.0)
        bc2s = p.sbuf("bc2s", (2, 128), F32)
        dp2s = p.sbuf("dp2s", (2, DC), F32)
        onws = p.sbuf("onws", (128, 1), F32)
        p.dma("sync", bc2s[:, :], bc2[:, :])
        p.dma("sync", dp2s[:, :], dp2[:, :])
        p.dma("sync", onws[:, :], onw[:, :])

        qt_ = p.sbuf("qt_", (128, 512), BF16)
        kt_ = p.sbuf("kt_", (128, 512), BF16)
        vt_ = p.sbuf("vt_", (128, 512), BF16)
        ot_ = p.sbuf("ot_", (128, 512), F32)
        gt_ = p.sbuf("gt_", (128, 512), BF16)
        prod = p.sbuf("prod", (128, 512), BF16)
        dqk = p.sbuf("dqk", (2, 512), F32)
        ms = p.sbuf("ms", (2, 512), F32)
        sqb = p.sbuf("sqb", (128, 512), BF16)
        gated = p.sbuf("gated", (128, DC * TPC), BF16)
        sbf = p.sbuf("sbf", (128, 512), BF16)

        for oc in range(DC):
            for tb in range(8):
                cs = slice(tb * 512, (tb + 1) * 512)
                rowsl = slice(oc * 128, (oc + 1) * 128)
                p.dma("sync", qt_[:, :], qf[rowsl, cs])
                p.dma("sync", kt_[:, :], kf[rowsl, cs])
                p.dma("sync", vt_[:, :], vf[rowsl, cs])
                p.dma("gpsimd", ot_[:, :], off[rowsl, cs])
                p.dma("sync", gt_[:, :], gf[rowsl, cs])
                p.op("vector", "tensor_tensor", prod[:, :], qt_[:, :],
                     kt_[:, :], op=ALU.mult)
                p.op("tensor", "matmul", psA[0:2, :], onesb2[:, :],
                     prod[:, :], start=True, stop=True)
                p.op("vector", "tensor_scalar", dqk[:, :], psA[0:2, :],
                     dp2s[:, oc:oc + 1], None, op0=ALU.mult)
                p.op("tensor", "matmul", psB[:, :], bc2s[:, :], dqk[:, :],
                     start=True, stop=True)
                p.op("vector", "tensor_tensor", prod[:, :], vt_[:, :],
                     psB[:, :], op=ALU.mult)
                p.op("vector", "tensor_tensor", ot_[:, :], ot_[:, :],
                     prod[:, :], op=ALU.add)
                p.op("scalar", "activation", sqb[:, :], ot_[:, :], AF.Square)
                p.op("tensor", "matmul", psA[0:2, :], onesb2[:, :],
                     sqb[:, :], start=True, stop=True)
                p.op("vector", "tensor_scalar", ms[:, :], psA[0:2, :],
                     1.0 / 64.0, None, op0=ALU.mult)
                p.op("scalar", "activation", ms[:, :], ms[:, :], AF.Ln,
                     bias=1e-6)
                p.op("scalar", "activation", ms[:, :], ms[:, :], AF.Exp,
                     scale=-0.5)
                p.op("tensor", "matmul", psB[:, :], bc2s[:, :], ms[:, :],
                     start=True, stop=True)
                p.op("vector", "tensor_tensor", ot_[:, :], ot_[:, :],
                     psB[:, :], op=ALU.mult)
                p.op("vector", "tensor_scalar", ot_[:, :], ot_[:, :],
                     onws[:, 0:1], None, op0=ALU.mult)
                p.op("scalar", "activation", ot_[:, :], ot_[:, :], AF.Silu)
                p.op("vector", "tensor_tensor",
                     gated[:, oc * TPC + tb * 512: oc * TPC + (tb + 1) * 512],
                     ot_[:, :], gt_[:, :], op=ALU.mult)

        w_sb = p.sbuf("w_sb", (128, DC * DM), BF16)
        load_w_bf16(p, w_sb, woT, DM)

        def cb(oc, tb, bank):
            p.op("vector", "tensor_copy", sbf[:, :], bank[:, :])
            p.dma("gpsimd", s1_o[oc * 128:(oc + 1) * 128,
                                 tb * 512:(tb + 1) * 512], sbf[:, :])
        emit_proj(p, w_sb, gated, cb, [psC, psD])

    if standalone:
        p.build(emit)
    else:
        emit(p)
    return p


def l1_host_consts():
    c = host_consts()
    if "l1ropeC" in c:
        return c
    D = HD
    inv_freq = 1.0 / (10000.0 ** (np.arange(0, D, 2, dtype=np.float32) / D))
    t = np.arange(SEQ, dtype=np.float32)
    freqs = np.outer(t, inv_freq)
    emb = np.concatenate([freqs, freqs], 1)
    cose = np.cos(emb).astype(np.float32)
    sine = np.sin(emb).astype(np.float32)
    cp = cose[:, ::2].T
    sp_ = sine[:, ::2].T
    c64 = np.zeros((64, SEQ), np.float32)
    s64 = np.zeros((64, SEQ), np.float32)
    c64[0::2] = cp; c64[1::2] = cp
    s64[0::2] = sp_; s64[1::2] = sp_
    c["l1ropeC"] = np.ascontiguousarray(np.concatenate([c64, c64], 0))
    c["l1ropeS"] = np.ascontiguousarray(np.concatenate([s64, s64], 0))
    bc2 = np.zeros((2, 128), np.float32)
    bc2[0, :64] = 1.0; bc2[1, 64:] = 1.0
    c["bc2"] = bc2
    tt = np.arange(128)
    c["masksu"] = np.ascontiguousarray((tt[:, None] < tt[None, :]).astype(np.float32))
    c["masksl"] = np.ascontiguousarray((tt[None, :] < tt[:, None]).astype(np.float32))
    return c


def l1a_pre_inmaps(inputs, x1_shards):
    c = l1_host_consts()
    cw = np.stack([inputs["L1_attn_cqw"][:, 0, :], inputs["L1_attn_ckw"][:, 0, :],
                   inputs["L1_attn_cvw"][:, 0, :]]).astype(np.float32)
    cwl = np.ascontiguousarray(
        cw.reshape(3, 8, 128, 4).transpose(0, 2, 1, 3).reshape(3, 128, 32))
    cbv = np.stack([inputs["L1_attn_cqb"], inputs["L1_attn_ckb"],
                    inputs["L1_attn_cvb"]]).astype(np.float32)
    cbl = np.ascontiguousarray(
        cbv.reshape(3, 8, 128).transpose(0, 2, 1))
    base = {
        "wqT": np.ascontiguousarray(inputs["L1_attn_Wq"].T.astype(np.float32)),
        "wkT": np.ascontiguousarray(inputs["L1_attn_Wk"].T.astype(np.float32)),
        "wvT": np.ascontiguousarray(inputs["L1_attn_Wv"].T.astype(np.float32)),
        "wgT": np.ascontiguousarray(inputs["L1_attn_Wg"].T.astype(np.float32)),
        "wbT": np.ascontiguousarray(inputs["L1_attn_Wb"].T.astype(np.float32)),
        "wgkT": np.ascontiguousarray(inputs["L1_attn_Wgk"].T.astype(np.float32)),
        "cw": cwl, "cb": cbl,
        "bbv": np.ascontiguousarray(inputs["L1_attn_bb"].reshape(16, 1).astype(np.float32)),
        "bgkv": np.ascontiguousarray(inputs["L1_attn_bgk"].reshape(16, 1).astype(np.float32)),
        "dtbv": np.ascontiguousarray(inputs["L1_attn_dt_bias"].reshape(16, 1).astype(np.float32)),
        "alogv": np.ascontiguousarray(inputs["L1_attn_A_log"].reshape(16, 1).astype(np.float32)),
        "ropeC": c["l1ropeC"], "ropeS": c["l1ropeS"], "rotT": c["ropeRotT"],
        "bc2": c["bc2"], "eye_d": c["eye128"],
    }
    maps = []
    for ci in range(NCORE):
        m = dict(base)
        m["xt"] = x1_shards[ci]
        maps.append(m)
    return maps


def l1a_out_inmaps(inputs, pre_res, of_all):
    c = l1_host_consts()
    dp = inputs["L1_attn_Dp"].astype(np.float32)
    dp2 = np.ascontiguousarray(dp.reshape(8, 2).T)
    onw = np.ascontiguousarray(
        np.tile(inputs["L1_attn_onorm_w"].astype(np.float32), 2).reshape(128, 1))
    base = {
        "woT": np.ascontiguousarray(inputs["L1_attn_Wo"].T.astype(np.float32)),
        "dp2": dp2, "onw": onw, "bc2": c["bc2"],
    }
    maps = []
    for ci in range(NCORE):
        m = dict(base)
        r = pre_res[ci]
        m["qf"] = r["qf_o"]
        m["kf"] = r["kf_o"]
        m["vf"] = r["vf_o"]
        m["gf"] = r["g_o"]
        m["off"] = of_all[ci]
        maps.append(m)
    return maps


def build_L1b(p=None):
    standalone = p is None
    if standalone:
        p = Prog("l1b")

    def emit(p):
        xn1 = p.dram("xn1", (DM, TPC), BF16, kind="ExternalInput").ap()
        x1 = p.dram("x1", (DM, TPC), F32, kind="ExternalInput").ap()
        s1 = p.dram("s1", (DM, TPC), BF16, kind="ExternalInput").ap()
        winT = p.dram("winT", (DM, DM), F32, kind="ExternalInput").ap()
        woutT = p.dram("woutT", (DM, DM), F32, kind="ExternalInput").ap()
        keysa = p.dram("keysa", (8, 128, 512), F32, kind="ExternalInput").ap()
        keysb = p.dram("keysb", (8, 128, 512), F32, kind="ExternalInput").ap()
        ushd = p.dram("ushd", (128, 1), F32, kind="ExternalInput").ap()
        evtab = p.dram("evtab", (262144, 128), F32, kind="ExternalInput").ap()
        selS = p.dram("selS", (32, 256), F32, kind="ExternalInput").ap()
        selID = p.dram("selID", (32, 256), F32, kind="ExternalInput").ap()
        lnwt = p.dram("lnwt", (128, DC), F32, kind="ExternalInput").ap()
        lnbt = p.dram("lnbt", (128, DC), F32, kind="ExternalInput").ap()
        eyed = p.dram("eye_d", (128, 128), F32, kind="ExternalInput").ap()

        x2_o = p.dram("x2_o", (DM, TPC), F32, kind="ExternalOutput").ap()

        mT_s = p.dram("mT_s", (DM, TPC), BF16).ap()
        act_s = p.dram("act_s", (8, TPC), F32).ap()

        misc = emit_misc(p)
        psA, psB, psC, psD = misc["psA"], misc["psB"], misc["psC"], misc["psD"]
        psT = misc["psT"]
        onef = misc["ones_b128_f"]

        eye = p.sbuf("eye_sb", (128, 128), BF16)
        p.dma("gpsimd", eye[:, :], eyed[:, :])
        eyef = p.sbuf("eyef_sb", (128, 128), F32)
        p.dma("sync", eyef[:, :], eyed[:, :])

        hT = p.sbuf("hT", (128, DC * TPC), BF16)
        with p.scope():
            xbuf = p.sbuf("xbuf", (128, DC * TPC), BF16)
            w_sb = p.sbuf("w_sb", (128, DC * DM), BF16)
            for kc in range(DC):
                p.dma("sync", xbuf[:, kc * TPC:(kc + 1) * TPC],
                      xn1[kc * 128:(kc + 1) * 128, :])
            load_w_bf16(p, w_sb, winT, DM)

            def cb_h(oc, tb, bank):
                p.op("vector", "tensor_copy",
                     hT[:, oc * TPC + tb * 512: oc * TPC + (tb + 1) * 512],
                     bank[:, :])
            emit_proj(p, w_sb, xbuf, cb_h, [psA, psB])

        with p.scope():
            ushb = p.sbuf("ushb", (128, 1), BF16)
            p.dma("gpsimd", ushb[:, :], ushd[:, :])
            arow = p.sbuf("arow", (1, 512), F32)
            for tb in range(8):
                for hh in range(8):
                    p.op("tensor", "matmul", psA[0:1, :], ushb[:, :],
                         hT[:, hh * TPC + tb * 512: hh * TPC + (tb + 1) * 512],
                         start=True, stop=True)
                    p.op("scalar", "activation", arow[0:1, :], psA[0:1, :],
                         AF.Sigmoid)
                    p.dma("gpsimd", act_s[hh, tb * 512:(tb + 1) * 512],
                          arow[0:1, :])

        ka_sb = p.sbuf("ka_sb", (128, 8 * 512), BF16)
        kb_sb = p.sbuf("kb_sb", (128, 8 * 512), BF16)
        for hh in range(8):
            p.dma("gpsimd", ka_sb[:, hh * 512:(hh + 1) * 512], keysa[hh, :, :])
            p.dma("gpsimd", kb_sb[:, hh * 512:(hh + 1) * 512], keysb[hh, :, :])
        selSs = p.sbuf("selSs", (32, 256), F32)
        selIDs = p.sbuf("selIDs", (64, 256), F32)
        p.dma("sync", selSs[:, :], selS[:, :])
        p.dma("sync", selIDs[32:64, :], selID[:, :])

        sc_t = p.sbuf("sc_t", (128, 512), F32)
        v8a = p.sbuf("v8a", (128, 8), F32)
        i8a = p.sbuf("i8a", (128, 8), U32)
        mr_t = p.sbuf("mr_t", (128, 512), F32)
        stg = p.sbuf("stg", (128, 64), F32)
        stgT = p.sbuf("stgT", (64, 128), F32)
        grid_s = p.sbuf("grid_s", (128, 256), F32)
        grid_idf = p.sbuf("grid_idf", (128, 256), F32)
        gid_i = p.sbuf("gid_i", (128, 256), I32)
        keyA = p.sbuf("keyA", (128, 256), I32)
        keyB = p.sbuf("keyB", (128, 256), I32)
        tmpi = p.sbuf("tmpi", (128, 256), I32)
        kA16 = p.sbuf("kA16", (128, 16), F32)
        kB16 = p.sbuf("kB16", (128, 16), F32)
        mr2 = p.sbuf("mr2", (128, 256), F32)
        rAi = p.sbuf("rAi", (128, 16), I32)
        rBi = p.sbuf("rBi", (128, 16), I32)
        rAf = p.sbuf("rAf", (128, 16), F32)
        rBf = p.sbuf("rBf", (128, 16), F32)
        rowf = p.sbuf("rowf", (128, 16), F32)
        rows_u = p.sbuf("rows_u", (128, 16), U32)
        negm = p.sbuf("negm", (128, 1), F32)
        w16 = p.sbuf("w16", (128, 16), F32)
        zc = p.sbuf("zc", (128, 1), F32)
        actc = p.sbuf("actc", (128, 8), F32)
        vg = [p.sbuf("vg%d" % i, (128, 128), BF16) for i in range(2)]
        acc = [p.sbuf("acc%d" % i, (128, 128), F32) for i in range(2)]
        mtr = p.sbuf("mtr", (128, 128), BF16)

        a8 = p.sbuf("a8", (8, 128), F32)
        for k in range(32):
            p.dma("sync", a8[:, :], act_s[:, k * 128:(k + 1) * 128])
            p.op("tensor", "transpose", psD[0:128, 0:8], a8[:, :],
                 eyef[0:8, 0:8])
            p.op("vector", "tensor_copy", actc[:, :], psD[0:128, 0:8])
            for hh in range(8):
                lhs_h = hT[:, hh * TPC + k * 128: hh * TPC + (k + 1) * 128]
                for (tbl, keys, vo, io_, col0) in (
                        (0, ka_sb, v8a, i8a, 0), (1, kb_sb, v8a, i8a, 16)):
                    p.op("tensor", "matmul", psA[:, :], lhs_h,
                         keys[:, hh * 512:(hh + 1) * 512], start=True, stop=True)
                    p.op("vector", "tensor_copy", sc_t[:, :], psA[:, :])
                    p.op("vector", "max", v8a[:, :], sc_t[:, :])
                    p.op("vector", "max_index", i8a[:, :], v8a[:, :], sc_t[:, :])
                    p.op("vector", "tensor_copy", stg[:, col0:col0 + 8], v8a[:, :])
                    p.op("vector", "tensor_copy", stg[:, 32 + col0:32 + col0 + 8],
                         i8a[:, :])
                    p.op("vector", "match_replace", mr_t[:, :], v8a[:, :],
                         sc_t[:, :], -1e30)
                    p.op("vector", "max", v8a[:, :], mr_t[:, :])
                    p.op("vector", "max_index", i8a[:, :], v8a[:, :], mr_t[:, :])
                    p.op("vector", "tensor_copy", stg[:, col0 + 8:col0 + 16],
                         v8a[:, :])
                    p.op("vector", "tensor_copy",
                         stg[:, 32 + col0 + 8:32 + col0 + 16], i8a[:, :])
                p.op("tensor", "transpose", psC[0:64, 0:128], stg[:, :],
                     eyef[:, :])
                p.op("vector", "tensor_copy", stgT[:, :], psC[0:64, 0:128])
                p.op("tensor", "matmul", psA[:, 0:256], stgT[0:32, :],
                     selSs[:, :], start=True, stop=True)
                p.op("vector", "tensor_copy", grid_s[:, :], psA[:, 0:256])
                p.op("tensor", "matmul", psB[:, 0:256], stgT[32:64, :],
                     selIDs[32:64, :], start=True, stop=True)
                p.op("vector", "tensor_copy", grid_idf[:, :], psB[:, 0:256])
                p.op("vector", "tensor_copy", gid_i[:, :], grid_idf[:, :])
                p.op("vector", "tensor_scalar", keyA[:, :],
                     grid_s[:, :].bitcast(I32), ~0x1FF, None,
                     op0=ALU.bitwise_and)
                p.op("vector", "tensor_scalar", tmpi[:, :], gid_i[:, :], 9,
                     None, op0=ALU.logical_shift_right)
                p.op("vector", "tensor_tensor", keyB[:, :], keyA[:, :],
                     tmpi[:, :], op=ALU.bitwise_or)
                p.op("vector", "tensor_scalar", tmpi[:, :], gid_i[:, :], 511,
                     None, op0=ALU.bitwise_and)
                p.op("vector", "tensor_tensor", keyA[:, :], keyA[:, :],
                     tmpi[:, :], op=ALU.bitwise_or)
                p.op("vector", "max", kA16[:, 0:8], keyB[:, :].bitcast(F32))
                p.op("vector", "match_replace", mr2[:, :], kA16[:, 0:8],
                     keyB[:, :].bitcast(F32), -1e30)
                p.op("vector", "max", kA16[:, 8:16], mr2[:, :])
                p.op("vector", "max", kB16[:, 0:8], keyA[:, :].bitcast(F32))
                p.op("vector", "match_replace", mr2[:, :], kB16[:, 0:8],
                     keyA[:, :].bitcast(F32), -1e30)
                p.op("vector", "max", kB16[:, 8:16], mr2[:, :])
                p.op("vector", "tensor_scalar", rAi[:, :],
                     kA16[:, :].bitcast(I32), 0x1FF, None, op0=ALU.bitwise_and)
                p.op("vector", "tensor_scalar", rBi[:, :],
                     kB16[:, :].bitcast(I32), 0x1FF, None, op0=ALU.bitwise_and)
                p.op("vector", "tensor_copy", rAf[:, :], rAi[:, :])
                p.op("vector", "tensor_copy", rBf[:, :], rBi[:, :])
                p.op("vector", "tensor_scalar", rowf[:, :], rAf[:, :], 512.0,
                     None, op0=ALU.mult)
                p.op("vector", "tensor_tensor", rowf[:, :], rowf[:, :],
                     rBf[:, :], op=ALU.add)
                p.op("vector", "tensor_copy", rows_u[:, :], rowf[:, :])
                p.op("vector", "tensor_scalar", negm[:, :], kA16[:, 0:1],
                     -1.0, None, op0=ALU.mult)
                p.op("scalar", "activation", w16[:, :], kA16[:, :], AF.Exp,
                     bias=negm[:, 0:1], accum_out=zc[:, 0:1])
                p.op("vector", "reciprocal", zc[:, :], zc[:, :])
                p.op("vector", "tensor_scalar", w16[:, :], w16[:, :],
                     zc[:, 0:1], None, op0=ALU.mult)
                p.op("vector", "tensor_scalar", w16[:, :], w16[:, :],
                     actc[:, hh:hh + 1], None, op0=ALU.mult)
                for s_ in range(16):
                    p.gather(vg[s_ % 2][:, :], evtab[:, :],
                             rows_u[:, s_:s_ + 1])
                    if s_ == 0:
                        p.op("vector", "tensor_scalar", acc[0][:, :],
                             vg[0][:, :], w16[:, 0:1], None, op0=ALU.mult)
                    else:
                        p.op("vector", "scalar_tensor_tensor",
                             acc[s_ % 2][:, :], vg[s_ % 2][:, :],
                             w16[:, s_:s_ + 1], acc[(s_ + 1) % 2][:, :],
                             op0=ALU.mult, op1=ALU.add)
                p.op("vector", "tensor_copy", mtr[:, :], acc[15 % 2][:, :])
                p.op("tensor", "transpose", psT[:, 0:128], mtr[:, :], eye[:, :])
                p.op("vector", "tensor_copy", mtr[:, :], psT[:, 0:128])
                p.dma("gpsimd", mT_s[hh * 128:(hh + 1) * 128,
                                     k * 128:(k + 1) * 128], mtr[:, :])

        with p.scope():
            w_sb = p.sbuf("w_sb2", (128, DC * DM), BF16)
            load_w_bf16(p, w_sb, woutT, DM)
            mbuf = p.sbuf("mbuf", (128, DC * 512), BF16)
            lnws = p.sbuf("lnws", (128, DC), F32)
            lnbs = p.sbuf("lnbs", (128, DC), F32)
            p.dma("sync", lnws[:, :], lnwt[:, :])
            p.dma("sync", lnbs[:, :], lnbt[:, :])
            onecol = misc["ones_col_bf"]
            sq = p.sbuf("sq2", (128, 512), BF16)
            st_mu = p.sbuf("st_mu2", (1, 512), F32)
            st_v = p.sbuf("st_v2", (1, 512), F32)
            st_r = p.sbuf("st_r2", (1, 512), F32)
            st_m2 = p.sbuf("st_m22", (1, 512), F32)
            rb = p.sbuf("rb2", (128, 512), F32)
            m2b = p.sbuf("m2b2", (128, 512), F32)
            pj = [p.sbuf("pj%d" % i, (128, DC * 512), F32) for i in range(1)]
            gb = p.sbuf("gb2", (128, 512), F32)
            xf1 = p.sbuf("xf12", (128, 512), F32)
            sf1 = p.sbuf("sf12", (128, 512), BF16)
            for tb in range(8):
                for kc in range(DC):
                    p.dma("sync", mbuf[:, kc * 512:(kc + 1) * 512],
                          mT_s[kc * 128:(kc + 1) * 128, tb * 512:(tb + 1) * 512])
                for oc in range(DC):
                    bank = [psA, psB][oc % 2]
                    for kc in range(DC):
                        p.op("tensor", "matmul", bank[:, :],
                             w_sb[:, kc * DM + oc * 128: kc * DM + (oc + 1) * 128],
                             mbuf[:, kc * 512:(kc + 1) * 512],
                             start=(kc == 0), stop=(kc == DC - 1), group=(kc > 0))
                    p.op("vector", "tensor_copy",
                         pj[0][:, oc * 512:(oc + 1) * 512], bank[:, :])
                onecf = p.constf(1.0)
                for oc in range(DC):
                    p.op("tensor", "matmul", psC[0:1, :], onecf[:, :],
                         pj[0][:, oc * 512:(oc + 1) * 512],
                         start=(oc == 0), stop=(oc == DC - 1), group=(oc > 0))
                for oc in range(DC):
                    p.op("scalar", "activation", sq[:, :],
                         pj[0][:, oc * 512:(oc + 1) * 512], AF.Square)
                    p.op("tensor", "matmul", psD[0:1, :], onecol[:, :], sq[:, :],
                         start=(oc == 0), stop=(oc == DC - 1), group=(oc > 0))
                p.op("vector", "tensor_scalar", st_mu[:, :], psC[0:1, :],
                     1.0 / DM, None, op0=ALU.mult)
                p.op("vector", "tensor_scalar", st_v[:, :], psD[0:1, :],
                     1.0 / DM, None, op0=ALU.mult)
                p.op("vector", "tensor_tensor", st_m2[:, :], st_mu[:, :],
                     st_mu[:, :], op=ALU.mult)
                p.op("vector", "tensor_tensor", st_v[:, :], st_v[:, :],
                     st_m2[:, :], op=ALU.subtract)
                p.op("scalar", "activation", st_v[:, :], st_v[:, :], AF.Ln,
                     bias=1e-5)
                p.op("scalar", "activation", st_r[:, :], st_v[:, :], AF.Exp,
                     scale=-0.5)
                p.op("vector", "tensor_tensor", st_m2[:, :], st_mu[:, :],
                     st_r[:, :], op=ALU.mult)
                p.op("tensor", "matmul", psC[:, :], misc["ones_b128_f"][:, :],
                     st_r[:, :], start=True, stop=True)
                p.op("vector", "tensor_copy", rb[:, :], psC[:, :])
                p.op("tensor", "matmul", psC[:, :], misc["ones_b128_f"][:, :],
                     st_m2[:, :], start=True, stop=True)
                p.op("vector", "tensor_copy", m2b[:, :], psC[:, :])
                for oc in range(DC):
                    sl_ = slice(oc * 512, (oc + 1) * 512)
                    p.op("vector", "tensor_tensor", gb[:, :], pj[0][:, sl_],
                         rb[:, :], op=ALU.mult)
                    p.op("vector", "tensor_tensor", gb[:, :], gb[:, :],
                         m2b[:, :], op=ALU.subtract)
                    p.op("vector", "tensor_scalar", gb[:, :], gb[:, :],
                         lnws[:, oc:oc + 1], lnbs[:, oc:oc + 1],
                         op0=ALU.mult, op1=ALU.add)
                    p.op("scalar", "activation", gb[:, :], gb[:, :], AF.Silu)
                    p.dma("sync", xf1[:, :],
                          x1[oc * 128:(oc + 1) * 128, tb * 512:(tb + 1) * 512])
                    p.dma("gpsimd", sf1[:, :],
                          s1[oc * 128:(oc + 1) * 128, tb * 512:(tb + 1) * 512])
                    p.op("vector", "tensor_tensor", gb[:, :], gb[:, :],
                         sf1[:, :], op=ALU.mult)
                    p.op("vector", "tensor_tensor", xf1[:, :], xf1[:, :],
                         gb[:, :], op=ALU.add)
                    p.dma("gpsimd",
                          x2_o[oc * 128:(oc + 1) * 128, tb * 512:(tb + 1) * 512],
                          xf1[:, :])

    if standalone:
        p.build(emit)
    else:
        emit(p)
    return p


def l1b_inmaps(inputs, xn1_all, x1_shards, s1_all):
    c = host_consts()
    selS = np.zeros((32, 256), np.float32)
    selID = np.zeros((32, 256), np.float32)
    for cc in range(256):
        i, j = cc >> 4, cc & 15
        selS[i, cc] = 1.0
        selS[16 + j, cc] = 1.0
        selID[i, cc] = 512.0
        selID[16 + j, cc] = 1.0
    lnw = inputs["L1_ffn_ln_w"].astype(np.float32)
    lnb = inputs["L1_ffn_ln_b"].astype(np.float32)
    base = {
        "winT": np.ascontiguousarray(inputs["L1_ffn_W_in"].T.astype(np.float32)),
        "woutT": np.ascontiguousarray(inputs["L1_ffn_W_out"].T.astype(np.float32)),
        "keysa": np.ascontiguousarray(
            inputs["L1_ffn_keys_a"].transpose(0, 2, 1).astype(np.float32)),
        "keysb": np.ascontiguousarray(
            inputs["L1_ffn_keys_b"].transpose(0, 2, 1).astype(np.float32)),
        "ushd": np.ascontiguousarray(
            inputs["L1_ffn_u_shared"].reshape(128, 1).astype(np.float32)),
        "evtab": np.ascontiguousarray(inputs["L1_ffn_expert_v"].astype(np.float32)),
        "selS": selS, "selID": selID,
        "lnwt": np.ascontiguousarray(lnw.reshape(8, 128).T),
        "lnbt": np.ascontiguousarray(lnb.reshape(8, 128).T),
        "eye_d": c["eye128"],
    }
    maps = []
    for ci in range(NCORE):
        m = dict(base)
        m["xn1"] = xn1_all[ci]
        m["x1"] = x1_shards[ci]
        m["s1"] = s1_all[ci]
        maps.append(m)
    return maps


def build_L2a(p=None):
    standalone = p is None
    if standalone:
        p = Prog("l2a")

    def emit(p):
        xt = p.dram("xt", (DM, TPC), F32, kind="ExternalInput").ap()
        wqT = p.dram("wqT", (DM, DM), F32, kind="ExternalInput").ap()
        wkT = p.dram("wkT", (DM, DM), F32, kind="ExternalInput").ap()
        wvT = p.dram("wvT", (DM, DM), F32, kind="ExternalInput").ap()
        wgT = p.dram("wgT", (DM, 8), F32, kind="ExternalInput").ap()
        eyed = p.dram("eye_d", (128, 128), F32, kind="ExternalInput").ap()

        xn_o = p.dram("xn_o", (DM, TPC), BF16, kind="ExternalOutput").ap()
        s2_o = p.dram("s2_o", (DM, TPC), BF16, kind="ExternalOutput").ap()
        sc_o = p.dram("sc_o", (8, TPC), F32, kind="ExternalOutput").ap()

        q_s = p.dram("q_s", (NSEQ * NH, 64, SEQ), BF16).ap()
        k_s = p.dram("k_s2", (NSEQ * NH, 64, SEQ), BF16).ap()
        vtok_s = p.dram("vtok_s2", (NSEQ * NH, SEQ, 65), BF16).ap()

        misc = emit_misc(p)
        psA, psB, psC, psD = misc["psA"], misc["psB"], misc["psC"], misc["psD"]
        psT = misc["psT"]
        orow = misc["ones_row_bf"]

        xnbuf = p.sbuf("xnbuf", (128, DC * TPC), BF16)
        with p.scope():
            emit_ln(p, xt, xnbuf, misc)
        for kc in range(DC):
            p.dma("scalar", xn_o[kc * 128:(kc + 1) * 128, :],
                  xnbuf[:, kc * TPC:(kc + 1) * TPC])

        w_sb = p.sbuf("w_sb", (128, DC * DM), BF16)
        eye = p.sbuf("eye_sb", (128, 128), BF16)
        p.dma("gpsimd", eye[:, :], eyed[:, :])
        qb = p.sbuf("qb", (128, 512), BF16)

        def cb_qk(dst):
            def cb(oc, tb, bank):
                b = tb // 4
                tloc = (tb % 4) * 512
                bh0 = b * 16 + 2 * oc
                p.op("vector", "tensor_copy", qb[:, :], bank[:, :])
                p.dma("gpsimd", dst[bh0, :, tloc:tloc + 512], qb[0:64, :])
                p.dma("gpsimd", dst[bh0 + 1, :, tloc:tloc + 512], qb[64:128, :])
            return cb
        load_w_bf16(p, w_sb, wqT, DM)
        emit_proj(p, w_sb, xnbuf, cb_qk(q_s), [psA, psB])
        load_w_bf16(p, w_sb, wkT, DM)
        emit_proj(p, w_sb, xnbuf, cb_qk(k_s), [psA, psB])

        vtstage = p.sbuf("vtstage", (128, 65), BF16)
        p.op("vector", "memset", vtstage[:, :], 1.0)
        vbf = p.sbuf("vbf", (128, 512), BF16)

        def cb_v(oc, tb, bank):
            b = tb // 4
            tloc = (tb % 4) * 512
            p.op("vector", "tensor_copy", vbf[:, :], bank[:, :])
            for hh in range(2):
                bh = b * 16 + 2 * oc + hh
                for j in range(4):
                    p.op("tensor", "transpose", psT[:, 0:64],
                         vbf[hh * 64:(hh + 1) * 64, j * 128:(j + 1) * 128],
                         eye[hh * 64:(hh + 1) * 64, hh * 64:(hh + 1) * 64])
                    p.op("vector", "tensor_copy", vtstage[:, 0:64], psT[:, 0:64])
                    p.dma("gpsimd",
                          vtok_s[bh, tloc + j * 128: tloc + (j + 1) * 128, :],
                          vtstage[:, :])
        load_w_bf16(p, w_sb, wvT, DM)
        emit_proj(p, w_sb, xnbuf, cb_v, [psA, psB])

        wg_sb = p.sbuf("wg_sb", (128, DC * 8), BF16)
        load_w_bf16(p, wg_sb, wgT, 8)
        lgf = p.sbuf("lgf", (8, 512), F32)
        for tb in range(8):
            for kc in range(DC):
                p.op("tensor", "matmul", psA[0:8, :],
                     wg_sb[:, kc * 8:(kc + 1) * 8],
                     xnbuf[:, kc * TPC + tb * 512: kc * TPC + (tb + 1) * 512],
                     start=(kc == 0), stop=(kc == DC - 1), group=(kc > 0))
            p.op("vector", "tensor_copy", lgf[:, :], psA[0:8, :])
            p.dma("sync", sc_o[:, tb * 512:(tb + 1) * 512], lgf[:, :])

        if p.iomap is not None:
            xntok2 = p.dram("xntok2", (TPC, DM), BF16).ap()
            xtr2 = p.sbuf("xtr2", (128, 128), BF16)
            for k in range(32):
                for kc in range(DC):
                    p.op("tensor", "transpose", psT[:, 0:128],
                         xnbuf[:, kc * TPC + k * 128: kc * TPC + (k + 1) * 128],
                         eye[:, :])
                    p.op("vector", "tensor_copy", xtr2[:, :], psT[:, 0:128])
                    p.dma("gpsimd",
                          xntok2[k * 128:(k + 1) * 128, kc * 128:(kc + 1) * 128],
                          xtr2[:, :])
            scg = p.dram("scg", (NCORE * 8, TPC), F32).ap()
            p.collective("AllGather", ALU.bypass,
                         [sc_o.tensor.ap().opt()], [scg.tensor.ap().opt()])
        q_t = p.sbuf("q_t", (64, SEQ), BF16)
        k_t = p.sbuf("k_t", (64, SEQ), BF16)
        vt_t = p.sbuf("vt_t", (128, 16 * 65), BF16)
        ptile = p.sbuf("ptile", (128, 512), BF16)
        zrow = p.sbuf("zrow", (1, 512), F32)
        zr = p.sbuf("zr", (1, 512), F32)
        zrb = p.sbuf("zrb", (128, 512), BF16)
        otile = p.sbuf("otile", (128, 512), BF16)
        for b in range(NSEQ):
            for h in range(NH):
                bh = b * 16 + h
                p.dma("sync", q_t[:, :], q_s[bh, :, :])
                p.dma("sync", k_t[:, :], k_s[bh, :, :])
                for j in range(16):
                    p.dma("scalar", vt_t[:, j * 65:(j + 1) * 65],
                          vtok_s[bh, j * 128:(j + 1) * 128, :])
                for tq in range(4):
                    for tkj in range(16):
                        p.op("tensor", "matmul", psA[:, :],
                             k_t[:, tkj * 128:(tkj + 1) * 128],
                             q_t[:, tq * 512:(tq + 1) * 512],
                             start=True, stop=True)
                        p.op("scalar", "activation", ptile[:, :], psA[:, :],
                             AF.Exp, scale=0.125, bias=-L2_SHIFT)
                        p.op("tensor", "matmul", psC[0:65, :],
                             vt_t[:, tkj * 65:(tkj + 1) * 65], ptile[:, :],
                             start=(tkj == 0), stop=(tkj == 15))
                    p.op("vector", "tensor_copy", zrow[:, :], psC[64:65, :])
                    p.op("vector", "reciprocal", zr[:, :], zrow[:, :])
                    p.op("tensor", "matmul", psD[:, :],
                         misc["ones_b128_f"][:, :], zr[:, :],
                         start=True, stop=True)
                    p.op("vector", "tensor_copy", zrb[:, :], psD[:, :])
                    p.op("vector", "tensor_copy", otile[:, :], psC[:, :])
                    p.op("vector", "tensor_tensor", otile[:, :], otile[:, :],
                         zrb[:, :], op=ALU.mult)
                    p.dma("gpsimd",
                          s2_o[h * 64:(h + 1) * 64,
                               b * 2048 + tq * 512: b * 2048 + (tq + 1) * 512],
                          otile[0:64, :])

    if standalone:
        p.build(emit)
    else:
        emit(p)
    return p


CAPL2 = 768
TRASH2 = 8 * CAPL2
CAP2 = 4096


def build_L2b(p=None):
    standalone = p is None
    if standalone:
        p = Prog("l2b")

    def emit(p):
        scf = p.dram("scf", (8, 256 * 128), F32, kind="ExternalInput").ap()
        scmy = p.dram("scmy", (8, TPC), F32, kind="ExternalInput").ap()
        xn_tok = p.dram("xn_tok", (TPC, DM), BF16, kind="ExternalInput").ap()
        xt = p.dram("xt", (DM, TPC), F32, kind="ExternalInput").ap()
        s2 = p.dram("s2", (DM, TPC), BF16, kind="ExternalInput").ap()
        w1T = p.dram("w1T", (8, DM, 2048), F32, kind="ExternalInput").ap()
        w3T = p.dram("w3T", (8, DM, 2048), F32, kind="ExternalInput").ap()
        w2T = p.dram("w2T", (8, 2048, DM), F32, kind="ExternalInput").ap()
        grp8 = p.dram("grp8", (128, 8), F32, kind="ExternalInput").ap()
        bc8 = p.dram("bc8", (8, 128), F32, kind="ExternalInput").ap()
        ecap = p.dram("ecap", (8, 1), F32, kind="ExternalInput").ap()
        eyed = p.dram("eye_d", (128, 128), F32, kind="ExternalInput").ap()

        x3_o = p.dram("x3_o", (DM, TPC), F32, kind="ExternalOutput").ap()

        disp = p.dram("disp2", (TRASH2 + 1, DM), BF16).ap()
        y_s = p.dram("y_s2", (TRASH2 + 1, DM), BF16).ap()

        misc = emit_misc(p)
        psA, psB, psC, psD = misc["psA"], misc["psB"], misc["psC"], misc["psD"]
        psT = misc["psT"]
        eye = p.sbuf("eye_sb", (128, 128), BF16)
        eyef = p.sbuf("eyef_sb", (128, 128), F32)
        p.dma("gpsimd", eye[:, :], eyed[:, :])
        p.dma("sync", eyef[:, :], eyed[:, :])

        slots_u = p.sbuf("slots_u", (128, 8 * 32), U32)
        wtok = p.sbuf("wtok", (128, 8 * 32), F32)
        xtok = p.sbuf("xtok", (128, DM), BF16)
        zrow_b = p.sbuf("zrow_b", (1, DM), BF16)
        rscope = p.scope()
        rscope.__enter__()
        SC = p.sbuf("SC", (128, 2048), F32)
        if p.iomap is None:
            p.dma("sync", SC[:, :], scf.rearrange("e (c t) -> (e c) t", c=16))
        else:
            for e_ in range(8):
                for r_ in range(NCORE):
                    p.dma("sync", SC[e_ * 16 + 2 * r_: e_ * 16 + 2 * r_ + 2, :],
                          scf[r_ * 8 + e_, :].rearrange("(u t) -> u t", u=2))
        grp = p.sbuf("grp", (128, 8), F32)
        bcg = p.sbuf("bcg", (8, 128), F32)
        p.dma("sync", grp[:, :], grp8[:, :])
        p.dma("sync", bcg[:, :], bc8[:, :])
        lo = p.sbuf("lo", (128, 1), F32)
        hi = p.sbuf("hi", (128, 1), F32)
        mid = p.sbuf("mid", (128, 1), F32)
        cntp = p.sbuf("cntp", (128, 1), F32)
        cnt8 = p.sbuf("cnt8", (8, 1), F32)
        dec8 = p.sbuf("dec8", (8, 1), F32)
        decb = p.sbuf("decb", (128, 1), F32)
        tmpb = p.sbuf("tmpb", (128, 1), F32)
        junk = p.sbuf("junk2", (128, 2048), F32)
        p.op("vector", "memset", lo[:, :], -60.0)
        p.op("vector", "memset", hi[:, :], 60.0)
        for it in range(46):
            p.op("vector", "tensor_tensor", mid[:, :], lo[:, :], hi[:, :],
                 op=ALU.add)
            p.op("vector", "tensor_scalar", mid[:, :], mid[:, :], 0.5, None,
                 op0=ALU.mult)
            p.op("vector", "tensor_scalar", junk[:, :], SC[:, :],
                 mid[:, 0:1], None, op0=ALU.is_ge, op1=ALU.add,
                 accum_out=cntp[:, 0:1])
            p.op("tensor", "matmul", psA[0:8, 0:1], grp[:, :], cntp[:, :],
                 start=True, stop=True)
            p.op("vector", "tensor_scalar", cnt8[:, :], psA[0:8, 0:1],
                 float(CAP2), None, op0=ALU.is_ge)
            p.op("tensor", "matmul", psB[:, 0:1], bcg[:, :], cnt8[:, :],
                 start=True, stop=True)
            p.op("vector", "tensor_copy", decb[:, :], psB[:, 0:1])
            p.op("vector", "tensor_tensor", tmpb[:, :], mid[:, :], lo[:, :],
                 op=ALU.subtract)
            p.op("vector", "scalar_tensor_tensor", tmpb[:, :], tmpb[:, :],
                 1.0, decb[:, :], op0=ALU.mult, op1=ALU.mult)
            p.op("vector", "tensor_tensor", lo[:, :], lo[:, :], tmpb[:, :],
                 op=ALU.add)
            p.op("vector", "tensor_tensor", tmpb[:, :], hi[:, :], mid[:, :],
                 op=ALU.subtract)
            p.op("vector", "scalar_tensor_tensor", tmpb[:, :], tmpb[:, :],
                 1.0, decb[:, :], op0=ALU.mult, op1=ALU.mult)
            p.op("vector", "tensor_tensor", hi[:, :], mid[:, :], tmpb[:, :],
                 op=ALU.add)
        mrow = p.sbuf("mrow", (128, 1), F32)
        p.op("vector", "tensor_scalar", junk[:, :], SC[:, :], 1.0,
             -1e30, op0=ALU.mult, op1=ALU.max, accum_out=mrow[:, 0:1])
        sh = p.sbuf("sh", (128, 1), F32)
        cur = mrow
        for off in (8, 4, 2, 1):
            mask = [(i ^ off) % 32 for i in range(32)]
            p.op("vector", "stream_shuffle", sh[:, :], cur[:, :], mask)
            p.op("vector", "tensor_tensor", mrow[:, :], cur[:, :], sh[:, :],
                 op=ALU.max)
            cur = mrow
        ex = p.sbuf("ex", (128, 2048), F32)
        msk = p.sbuf("msk", (128, 2048), F32)
        zp = p.sbuf("zp", (128, 1), F32)
        negm = p.sbuf("negm2", (128, 1), F32)
        p.op("vector", "tensor_scalar", negm[:, :], mrow[:, :], -1.0, None,
             op0=ALU.mult)
        p.op("scalar", "activation", ex[:, :], SC[:, :], AF.Exp,
             bias=negm[:, 0:1])
        p.op("vector", "tensor_scalar", msk[:, :], SC[:, :], lo[:, 0:1],
             None, op0=ALU.is_ge)
        p.op("vector", "scalar_tensor_tensor", junk[:, :], ex[:, :], 1.0,
             msk[:, :], op0=ALU.mult, op1=ALU.mult, accum_out=zp[:, 0:1])
        z8 = p.sbuf("z8", (8, 1), F32)
        p.op("tensor", "matmul", psA[0:8, 0:1], grp[:, :], zp[:, :],
             start=True, stop=True)
        p.op("vector", "tensor_copy", z8[:, :], psA[0:8, 0:1])
        th_s = p.dram("th_s", (128, 2), F32).ap()
        thm = p.sbuf("thm", (128, 2), F32)
        p.op("vector", "tensor_copy", thm[:, 0:1], lo[:, :])
        p.op("vector", "tensor_copy", thm[:, 1:2], mrow[:, :])
        p.dma("gpsimd", th_s[:, :], thm[:, :])
        th8 = p.sbuf("th8", (8, 2), F32)
        p.dma("sync", th8[:, :], th_s[:, :].rearrange("(e c) x -> e (c x)", c=16)[:, 0:2])

        SCm = p.sbuf("SCm", (8, TPC), F32)
        p.dma("sync", SCm[:, :], scmy[:, :])
        wrow = p.sbuf("wrow", (8, TPC), F32)
        mskm = p.sbuf("mskm", (8, TPC), F32)
        lposr = p.sbuf("lposr", (8, TPC), F32)
        slotr = p.sbuf("slotr", (8, TPC), F32)
        ones8 = p.sbuf("ones8", (8, TPC), F32)
        zi8 = p.sbuf("zi8", (8, 1), F32)
        nm8 = p.sbuf("nm8", (8, 1), F32)
        ecaps = p.sbuf("ecaps", (8, 1), F32)
        p.dma("sync", ecaps[:, :], ecap[:, :])
        p.op("vector", "memset", ones8[:, :], 1.0)
        p.op("vector", "reciprocal", zi8[:, :], z8[:, :])
        p.op("vector", "tensor_scalar", nm8[:, :], th8[:, 1:2], -1.0, None,
             op0=ALU.mult)
        p.op("scalar", "activation", wrow[:, :], SCm[:, :], AF.Exp,
             bias=nm8[:, 0:1])
        p.op("vector", "tensor_scalar", mskm[:, :], SCm[:, :], th8[:, 0:1],
             None, op0=ALU.is_ge)
        p.op("vector", "tensor_scalar", wrow[:, :], wrow[:, :], zi8[:, 0:1],
             None, op0=ALU.mult)
        p.op("vector", "tensor_tensor", wrow[:, :], wrow[:, :], mskm[:, :],
             op=ALU.mult)
        p.op("vector", "tensor_tensor_scan", lposr[:, :], ones8[:, :],
             mskm[:, :], 0.0, op0=ALU.mult, op1=ALU.add)
        p.op("vector", "tensor_scalar", slotr[:, :], lposr[:, :],
             float(CAPL2), None, op0=ALU.is_le)
        p.op("vector", "tensor_tensor", mskm[:, :], mskm[:, :], slotr[:, :],
             op=ALU.mult)
        p.op("vector", "tensor_scalar", slotr[:, :], lposr[:, :],
             ecaps[:, 0:1], None, op0=ALU.add)
        p.op("vector", "scalar_tensor_tensor", slotr[:, :], slotr[:, :],
             float(-TRASH2), mskm[:, :], op0=ALU.add, op1=ALU.mult)
        p.op("vector", "tensor_scalar", slotr[:, :], slotr[:, :],
             float(TRASH2), float(TRASH2), op0=ALU.add, op1=ALU.min)
        sw = p.sbuf("sw", (16, 128), F32)
        swT = p.sbuf("swT", (128, 16), F32)
        p.op("vector", "memset", zrow_b[:, :], 0.0)
        p.dma("gpsimd", y_s[TRASH2:TRASH2 + 1, :], zrow_b[:, :])
        for k in range(32):
            cs = slice(k * 128, (k + 1) * 128)
            p.op("tensor", "transpose", psD[0:128, 0:8], slotr[:, cs],
                 eyef[0:8, 0:8])
            p.op("vector", "tensor_copy", swT[:, 0:8], psD[0:128, 0:8])
            p.op("tensor", "transpose", psD[0:128, 0:8], wrow[:, cs],
                 eyef[0:8, 0:8])
            p.op("vector", "tensor_copy", swT[:, 8:16], psD[0:128, 0:8])
            for e in range(8):
                p.op("vector", "tensor_copy",
                     slots_u[:, e * 32 + k: e * 32 + k + 1], swT[:, e:e + 1])
                p.op("vector", "tensor_copy",
                     wtok[:, e * 32 + k: e * 32 + k + 1], swT[:, 8 + e:9 + e])
            p.dma("sync", xtok[:, :], xn_tok[cs, :])
            for e in range(8):
                p.scatter(disp[:, :], xtok[:, :],
                          slots_u[:, e * 32 + k: e * 32 + k + 1])

        rscope.__exit__(None, None, None)
        with p.scope():
            dT = p.sbuf("dT", (128, DC * CAPL2), BF16)
            h1 = p.sbuf("h1", (128, 16 * CAPL2), BF16)
            wfull = p.sbuf("wfull", (128, DC * 2048), BF16)
            ybf = p.sbuf("ybf", (128, 512), BF16)
            ytr = p.sbuf("ytr", (128, 128), BF16)
            NB2 = cdiv(CAPL2, 512)

            def nsl(nb):
                n0 = nb * 512
                return n0, min(CAPL2, n0 + 512)

            for e in range(8):
                for kc in range(DC):
                    p.dma("sync", dT[:, kc * CAPL2:(kc + 1) * CAPL2],
                          disp[e * CAPL2:(e + 1) * CAPL2,
                               kc * 128:(kc + 1) * 128], transpose=True)
                for kc in range(DC):
                    p.dma("gpsimd", wfull[:, kc * 2048:(kc + 1) * 2048],
                          w1T[e, kc * 128:(kc + 1) * 128, :])
                for oc in range(16):
                    for nb in range(NB2):
                        n0, n1 = nsl(nb)
                        for kc in range(DC):
                            p.op("tensor", "matmul", psA[:, 0:n1 - n0],
                                 wfull[:, kc * 2048 + oc * 128:
                                       kc * 2048 + (oc + 1) * 128],
                                 dT[:, kc * CAPL2 + n0: kc * CAPL2 + n1],
                                 start=(kc == 0), stop=(kc == DC - 1),
                                 group=(kc > 0))
                        p.op("scalar", "activation",
                             h1[:, oc * CAPL2 + n0: oc * CAPL2 + n1],
                             psA[:, 0:n1 - n0], AF.Silu)
                for kc in range(DC):
                    p.dma("gpsimd", wfull[:, kc * 2048:(kc + 1) * 2048],
                          w3T[e, kc * 128:(kc + 1) * 128, :])
                for oc in range(16):
                    for nb in range(NB2):
                        n0, n1 = nsl(nb)
                        for kc in range(DC):
                            p.op("tensor", "matmul", psA[:, 0:n1 - n0],
                                 wfull[:, kc * 2048 + oc * 128:
                                       kc * 2048 + (oc + 1) * 128],
                                 dT[:, kc * CAPL2 + n0: kc * CAPL2 + n1],
                                 start=(kc == 0), stop=(kc == DC - 1),
                                 group=(kc > 0))
                        p.op("vector", "tensor_tensor",
                             h1[:, oc * CAPL2 + n0: oc * CAPL2 + n1],
                             h1[:, oc * CAPL2 + n0: oc * CAPL2 + n1],
                             psA[:, 0:n1 - n0], op=ALU.mult)
                for hc in range(16):
                    p.dma("gpsimd", wfull[:, hc * 1024:(hc + 1) * 1024],
                          w2T[e, hc * 128:(hc + 1) * 128, :])
                for oc in range(DC):
                    for nb in range(NB2):
                        n0, n1 = nsl(nb)
                        for hc in range(16):
                            p.op("tensor", "matmul", psA[:, 0:n1 - n0],
                                 wfull[:, hc * 1024 + oc * 128:
                                       hc * 1024 + (oc + 1) * 128],
                                 h1[:, hc * CAPL2 + n0: hc * CAPL2 + n1],
                                 start=(hc == 0), stop=(hc == 15),
                                 group=(hc > 0))
                        p.op("vector", "tensor_copy", ybf[:, 0:n1 - n0],
                             psA[:, 0:n1 - n0])
                        for j in range(cdiv(n1 - n0, 128)):
                            p.op("tensor", "transpose", psT[:, 0:128],
                                 ybf[:, j * 128:(j + 1) * 128], eye[:, :])
                            p.op("vector", "tensor_copy", ytr[:, :],
                                 psT[:, 0:128])
                            p.dma("gpsimd",
                                  y_s[e * CAPL2 + n0 + j * 128:
                                      e * CAPL2 + n0 + (j + 1) * 128,
                                      oc * 128:(oc + 1) * 128], ytr[:, :])

        with p.scope():
            yg = [p.sbuf("yg%d" % i, (128, DM), BF16) for i in range(2)]
            acc = [p.sbuf("gacc%d" % i, (128, DM), F32) for i in range(2)]
            gsil = p.sbuf("gsil", (128, DM), BF16)
            gtr = p.sbuf("gtr", (128, 128), BF16)
            xf1 = p.sbuf("xf1", (128, 128), F32)
            sf1 = p.sbuf("sf1", (128, 128), BF16)
            p.op("vector", "memset", yg[0][:, :], 0.0)
            p.op("vector", "memset", yg[1][:, :], 0.0)
            for k in range(32):
                for e in range(8):
                    p.gather(yg[e % 2][:, :], y_s[:, :],
                             slots_u[:, e * 32 + k: e * 32 + k + 1])
                    if e == 0:
                        p.op("vector", "tensor_scalar", acc[0][:, :],
                             yg[0][:, :], wtok[:, k:k + 1], None, op0=ALU.mult)
                    else:
                        p.op("vector", "scalar_tensor_tensor",
                             acc[e % 2][:, :], yg[e % 2][:, :],
                             wtok[:, e * 32 + k: e * 32 + k + 1],
                             acc[(e + 1) % 2][:, :], op0=ALU.mult, op1=ALU.add)
                p.op("scalar", "activation", gsil[:, :], acc[7 % 2][:, :],
                     AF.Silu)
                for kc in range(DC):
                    p.op("tensor", "transpose", psT[:, 0:128],
                         gsil[:, kc * 128:(kc + 1) * 128], eye[:, :])
                    p.op("vector", "tensor_copy", gtr[:, :], psT[:, 0:128])
                    p.dma("sync", xf1[:, :],
                          xt[kc * 128:(kc + 1) * 128, k * 128:(k + 1) * 128])
                    p.dma("gpsimd", sf1[:, :],
                          s2[kc * 128:(kc + 1) * 128, k * 128:(k + 1) * 128])
                    p.op("vector", "tensor_tensor", gtr[:, :], gtr[:, :],
                         sf1[:, :], op=ALU.mult)
                    p.op("vector", "tensor_tensor", xf1[:, :], xf1[:, :],
                         gtr[:, :], op=ALU.add)
                    p.dma("gpsimd",
                          x3_o[kc * 128:(kc + 1) * 128, k * 128:(k + 1) * 128],
                          xf1[:, :])

    if standalone:
        p.build(emit)
    else:
        emit(p)
    return p


def l2_inmaps(inputs, x2_shards):
    c = host_consts()
    base = {
        "wqT": np.ascontiguousarray(inputs["L2_attn_Wq"].T.astype(np.float32)),
        "wkT": np.ascontiguousarray(inputs["L2_attn_Wk"].T.astype(np.float32)),
        "wvT": np.ascontiguousarray(inputs["L2_attn_Wv"].T.astype(np.float32)),
        "wgT": np.ascontiguousarray(inputs["L2_ffn_Wg"].T.astype(np.float32)),
        "eye_d": c["eye128"],
    }
    maps = []
    for ci in range(NCORE):
        m = dict(base)
        m["xt"] = x2_shards[ci]
        maps.append(m)
    return maps


def l2b_inmaps(inputs, scores_full, res2a, x2_shards):
    c = host_consts()
    grp8 = np.zeros((128, 8), np.float32)
    for pp in range(128):
        grp8[pp, pp // 16] = 1.0
    bc8 = np.ascontiguousarray(grp8.T)
    ecap = np.ascontiguousarray(
        (np.arange(8, dtype=np.float32) * CAPL2 - 1.0).reshape(8, 1))
    base = {
        "scf": np.ascontiguousarray(scores_full.astype(np.float32)),
        "w1T": np.ascontiguousarray(
            inputs["L2_ffn_W1"].transpose(0, 2, 1).astype(np.float32)),
        "w3T": np.ascontiguousarray(
            inputs["L2_ffn_W3"].transpose(0, 2, 1).astype(np.float32)),
        "w2T": np.ascontiguousarray(
            inputs["L2_ffn_W2"].transpose(0, 2, 1).astype(np.float32)),
        "grp8": grp8, "bc8": bc8, "ecap": ecap, "eye_d": c["eye128"],
    }
    maps = []
    for ci in range(NCORE):
        m = dict(base)
        m["scmy"] = np.ascontiguousarray(
            scores_full[:, ci * TPC:(ci + 1) * TPC].astype(np.float32))
        m["xn_tok"] = np.ascontiguousarray(np.asarray(res2a[ci]["xn_o"]).T)
        m["xt"] = x2_shards[ci]
        m["s2"] = res2a[ci]["s2_o"]
        maps.append(m)
    return maps


RND_MAGIC = 1.5 * (2.0 ** 23)


def build_L3(p=None):
    standalone = p is None
    if standalone:
        p = Prog("l3")

    def emit(p):
        xt = p.dram("xt", (DM, TPC), F32, kind="ExternalInput").ap()
        wqkvT = p.dram("wqkvT", (DM, 3 * DM), F32, kind="ExternalInput").ap()
        woT = p.dram("woT", (DM, DM), F32, kind="ExternalInput").ap()
        wff = p.dram("wff", (DM, DM), F32, kind="ExternalInput").ap()
        bff = p.dram("bff", (128, DC), F32, kind="ExternalInput").ap()
        bc2d = p.dram("bc2d", (2, 128), F32, kind="ExternalInput").ap()
        eyed = p.dram("eye_d", (128, 128), F32, kind="ExternalInput").ap()

        x4_o = p.dram("x4_o", (DM, TPC), F32, kind="ExternalOutput").ap()

        qf_s = p.dram("qf_s", (DM, TPC), BF16).ap()
        ktok_s = p.dram("ktok_s", (TPC, DM), BF16).ap()
        vtok_s = p.dram("vtok_s3", (TPC, DM), BF16).ap()
        kf_s = p.dram("kf_s3", (DM, TPC), BF16).ap()
        s3_s = p.dram("s3_s", (DM, TPC), BF16).ap()
        wq_s = p.dram("wq_s", (DM, DM), BF16).ap()

        misc = emit_misc(p)
        psA, psB, psC, psD = misc["psA"], misc["psB"], misc["psC"], misc["psD"]
        psT = misc["psT"]
        onesb2 = p.sbuf("onesb2", (128, 2), BF16)
        p.op("vector", "memset", onesb2[:, :], 0.0)
        p.op("vector", "memset", onesb2[0:64, 0:1], 1.0)
        p.op("vector", "memset", onesb2[64:128, 1:2], 1.0)
        eye = p.sbuf("eye_sb", (128, 128), BF16)
        p.dma("gpsimd", eye[:, :], eyed[:, :])
        bc2s = p.sbuf("bc2s", (2, 128), F32)
        p.dma("sync", bc2s[:, :], bc2d[:, :])
        for v in (1e-30, 1.0 / 7.5, RND_MAGIC, -RND_MAGIC):
            p.constf(v)

        xnbuf = p.sbuf("xnbuf", (128, DC * TPC), BF16)
        with p.scope():
            emit_ln(p, xt, xnbuf, misc)

        with p.scope():
            w_sb = p.sbuf("w_sb", (128, DC * DM), BF16)
            qb = p.sbuf("qb3", (128, 512), BF16)
            eqb = p.sbuf("eqb", (128, 512), BF16)
            nrm = p.sbuf("nrm3", (2, 512), F32)
            vtstage = p.sbuf("vtstage3", (128, 128), BF16)
            for part in range(3):
                for kc in range(DC):
                    p.dma("gpsimd", w_sb[:, kc * DM:(kc + 1) * DM],
                          wqkvT[kc * 128:(kc + 1) * 128,
                                part * DM:(part + 1) * DM])

                def cb3(oc, tb, bank, part=part):
                    if part == 0:
                        p.op("scalar", "activation", eqb[:, :], bank[:, :],
                             AF.Exp)
                        p.op("tensor", "matmul", psC[0:2, 0:512], onesb2[:, :],
                             eqb[:, :], start=True, stop=True)
                        p.op("vector", "tensor_copy", nrm[:, :], psC[0:2, 0:512])
                        p.op("vector", "reciprocal", nrm[:, :], nrm[:, :])
                        p.op("vector", "tensor_scalar", nrm[:, :], nrm[:, :],
                             0.125, None, op0=ALU.mult)
                        p.op("tensor", "matmul", psD[:, 0:512], bc2s[:, :],
                             nrm[:, :], start=True, stop=True)
                        p.op("vector", "tensor_tensor", qb[:, :], eqb[:, :],
                             psD[:, 0:512], op=ALU.mult)
                        p.dma("gpsimd", qf_s[oc * 128:(oc + 1) * 128,
                                             tb * 512:(tb + 1) * 512], qb[:, :])
                    else:
                        if part == 1:
                            p.op("scalar", "activation", qb[:, :], bank[:, :],
                                 AF.Exp)
                            p.dma("gpsimd", kf_s[oc * 128:(oc + 1) * 128,
                                                 tb * 512:(tb + 1) * 512],
                                  qb[:, :])
                        else:
                            p.op("vector", "tensor_copy", qb[:, :], bank[:, :])
                        dst = ktok_s if part == 1 else vtok_s
                        for j in range(4):
                            for hh in range(2):
                                p.op("tensor", "transpose", psT[:, 0:64],
                                     qb[hh * 64:(hh + 1) * 64,
                                        j * 128:(j + 1) * 128],
                                     eye[hh * 64:(hh + 1) * 64,
                                         hh * 64:(hh + 1) * 64])
                                p.op("vector", "tensor_copy",
                                     vtstage[:, hh * 64:(hh + 1) * 64],
                                     psT[:, 0:64])
                            p.dma("gpsimd",
                                  dst[tb * 512 + j * 128: tb * 512 + (j + 1) * 128,
                                      oc * 128:(oc + 1) * 128], vtstage[:, :])
                emit_proj(p, w_sb, xnbuf, cb3, [psA, psB])

        BKT = 64
        NBK = SEQ // BKT
        with p.scope():
            qfh = p.sbuf("qfh", (64, SEQ), BF16)
            kfh = p.sbuf("kfh", (64, SEQ), BF16)
            kth = p.sbuf("kth", (128, 16 * 64), BF16)
            vth = p.sbuf("vth", (128, 16 * 64), BF16)
            bks = p.sbuf("bks", (64, NBK), F32)
            bkc = p.sbuf("bkc", (64, NBK), F32)
            onesr = p.sbuf("onesr", (64, NBK), F32)
            denr = p.sbuf("denr", (1, SEQ), F32)
            dtmp = p.sbuf("dtmp", (64, 64), F32)
            ctxa = p.sbuf("ctxa", (64, 64), F32)
            ctxb = p.sbuf("ctxb", (64, 64), BF16)
            qsc = p.sbuf("qsc", (64, SEQ), BF16)
            obuf3 = p.sbuf("obuf3", (128, DC * SEQ), BF16)
            sbf3 = p.sbuf("sbf3", (128, 512), BF16)
            onecf = p.constf(1.0)
            p.op("vector", "memset", onesr[:, :], 1.0)
            w_sb2 = p.sbuf("w_sb2", (128, DC * DM), BF16)
            load_w_bf16(p, w_sb2, woT, DM)
            for b in range(NSEQ):
                for h in range(NH):
                    kc_, rr = h // 2, (h % 2) * 64
                    p.dma("sync", qfh[:, :],
                          qf_s[h * 64:(h + 1) * 64, b * SEQ:(b + 1) * SEQ])
                    p.dma("sync", kfh[:, :],
                          kf_s[h * 64:(h + 1) * 64, b * SEQ:(b + 1) * SEQ])
                    for j in range(16):
                        p.dma("sync", kth[:, j * 64:(j + 1) * 64],
                              ktok_s[b * SEQ + j * 128: b * SEQ + (j + 1) * 128,
                                     h * 64:(h + 1) * 64])
                        p.dma("sync", vth[:, j * 64:(j + 1) * 64],
                              vtok_s[b * SEQ + j * 128: b * SEQ + (j + 1) * 128,
                                     h * 64:(h + 1) * 64])
                    p.op("vector", "tensor_reduce", bks[:, :],
                         kfh[:, :].rearrange("p (n s) -> p n s", s=BKT),
                         axis=AX.X, op=ALU.add)
                    p.op("vector", "tensor_tensor_scan", bkc[:, :],
                         onesr[:, :], bks[:, :], 0.0, op0=ALU.mult, op1=ALU.add)
                    p.op("vector", "memset", denr[:, 0:BKT], 1.0)
                    for n in range(1, NBK):
                        p.op("vector", "tensor_scalar", dtmp[:, :],
                             qfh[:, n * BKT:(n + 1) * BKT], bkc[:, n - 1:n],
                             None, op0=ALU.mult)
                        p.op("tensor", "matmul", psA[0:1, 0:64],
                             onecf[0:64, :], dtmp[:, :], start=True, stop=True)
                        p.op("vector", "tensor_copy",
                             denr[:, n * BKT:(n + 1) * BKT], psA[0:1, 0:64])
                    p.op("vector", "tensor_scalar", denr[:, :], denr[:, :],
                         1e-6, None, op0=ALU.max)
                    p.op("scalar", "activation", denr[:, :], denr[:, :], AF.Ln)
                    p.op("scalar", "activation", denr[:, :], denr[:, :],
                         AF.Exp, scale=-1.0)
                    for ts in range(4):
                        p.op("tensor", "matmul", psD[0:64, 0:512],
                             misc["ones_b128_f"][:, 0:64],
                             denr[:, ts * 512:(ts + 1) * 512],
                             start=True, stop=True)
                        p.op("vector", "tensor_tensor",
                             qsc[:, ts * 512:(ts + 1) * 512],
                             qfh[:, ts * 512:(ts + 1) * 512], psD[0:64, 0:512],
                             op=ALU.mult)
                    p.op("vector", "memset", ctxa[:, :], 0.0)
                    for n in range(NBK):
                        p.op("vector", "tensor_copy", ctxb[:, :], ctxa[:, :])
                        p.op("tensor", "matmul", psB[0:64, 0:64], ctxb[:, :],
                             qsc[:, n * BKT:(n + 1) * BKT],
                             start=True, stop=True)
                        p.op("vector", "tensor_copy",
                             obuf3[rr:rr + 64,
                                   kc_ * SEQ + n * BKT: kc_ * SEQ + (n + 1) * BKT],
                             psB[0:64, 0:64])
                        jj, roff = n // 2, (n % 2) * 64
                        p.op("tensor", "matmul", psC[0:64, 0:64],
                             kth[roff:roff + 64, jj * 64:(jj + 1) * 64],
                             vth[roff:roff + 64, jj * 64:(jj + 1) * 64],
                             start=True, stop=True)
                        p.op("vector", "tensor_tensor", ctxa[:, :], ctxa[:, :],
                             psC[0:64, 0:64], op=ALU.add)
                for oc in range(DC):
                    for tb in range(4):
                        bank = [psA, psB][tb % 2]
                        for kc in range(DC):
                            p.op("tensor", "matmul", bank[:, :],
                                 w_sb2[:, kc * DM + oc * 128: kc * DM + (oc + 1) * 128],
                                 obuf3[:, kc * SEQ + tb * 512: kc * SEQ + (tb + 1) * 512],
                                 start=(kc == 0), stop=(kc == DC - 1),
                                 group=(kc > 0))
                        p.op("vector", "tensor_copy", sbf3[:, :], bank[:, :])
                        p.dma("gpsimd",
                              s3_s[oc * 128:(oc + 1) * 128,
                                   b * SEQ + tb * 512: b * SEQ + (tb + 1) * 512],
                              sbf3[:, :])

        def emit_quant(src_ap_fn, dst_fn, ntiles, width, partition_blocks):
            wt = p.sbuf("qn_wt", (128, width), F32)
            ab = p.sbuf("qn_ab", (128, width), F32)
            sc = p.sbuf("qn_sc", (128, width // 32), F32)
            scb = p.sbuf("qn_scb", (128, width), F32)
            yq = p.sbuf("qn_yq", (128, width), F32)
            ei = p.sbuf("qn_ei", (128, width), I32)
            e2 = p.sbuf("qn_e2", (128, width), I32)
            stp = p.sbuf("qn_stp", (128, width), F32)
            sti = p.sbuf("qn_sti", (128, width), F32)
            ob = p.sbuf("qn_ob", (128, width), BF16)
            for ti in range(ntiles):
                p.dma("sync", wt[:, :], src_ap_fn(ti))
                if partition_blocks:
                    p.op("scalar", "activation", ab[:, :], wt[:, :], AF.Abs)
                    for off in (16, 8, 4, 2, 1):
                        mask = [(i ^ off) % 32 for i in range(32)]
                        p.op("vector", "stream_shuffle", scb[:, :], ab[:, :],
                             mask)
                        p.op("vector", "tensor_tensor", ab[:, :], ab[:, :],
                             scb[:, :], op=ALU.max)
                    p.op("vector", "tensor_scalar", scb[:, :], ab[:, :],
                         1.0 / 7.5, 1e-30, op0=ALU.mult, op1=ALU.max)
                else:
                    p.op("vector", "tensor_reduce", sc[:, :],
                         wt[:, :].rearrange("p (b i) -> p b i", i=32),
                         axis=AX.X, op=ALU.max, apply_absolute_value=True)
                    p.op("vector", "tensor_scalar", sc[:, :], sc[:, :],
                         1.0 / 7.5, 1e-30, op0=ALU.mult, op1=ALU.max)
                    p.op("vector", "tensor_copy",
                         scb[:, :].rearrange("p (b i) -> p b i", i=32),
                         sc[:, :].rearrange("p (b o) -> p b o", o=1)
                         .broadcast_to((128, width // 32, 32)))
                p.op("scalar", "activation", sti[:, :], scb[:, :], AF.Ln)
                p.op("scalar", "activation", sti[:, :], sti[:, :], AF.Exp,
                     scale=-1.0)
                p.op("vector", "tensor_tensor", yq[:, :], wt[:, :], sti[:, :],
                     op=ALU.mult)
                p.op("vector", "tensor_scalar", ei[:, :],
                     yq[:, :].bitcast(I32), 0x7FFFFFFF, None,
                     op0=ALU.bitwise_and)
                p.op("vector", "tensor_scalar", ei[:, :], ei[:, :], 23, None,
                     op0=ALU.logical_shift_right)
                p.op("vector", "tensor_scalar", ei[:, :], ei[:, :], -3.0,
                     124.0, op0=ALU.add, op1=ALU.max)
                p.op("vector", "tensor_scalar", e2[:, :], ei[:, :], -1.0,
                     254.0, op0=ALU.mult, op1=ALU.add)
                p.op("vector", "tensor_scalar", ei[:, :], ei[:, :], 23, None,
                     op0=ALU.logical_shift_left)
                p.op("vector", "tensor_scalar", e2[:, :], e2[:, :], 23, None,
                     op0=ALU.logical_shift_left)
                p.op("vector", "tensor_copy", stp[:, :], ei[:, :].bitcast(F32))
                p.op("vector", "tensor_copy", sti[:, :], e2[:, :].bitcast(F32))
                p.op("vector", "tensor_tensor", yq[:, :], yq[:, :], sti[:, :],
                     op=ALU.mult)
                p.op("vector", "tensor_scalar", yq[:, :], yq[:, :],
                     RND_MAGIC, -RND_MAGIC, op0=ALU.add, op1=ALU.add)
                p.op("vector", "tensor_tensor", yq[:, :], yq[:, :], stp[:, :],
                     op=ALU.mult)
                p.op("vector", "tensor_tensor", yq[:, :], yq[:, :], scb[:, :],
                     op=ALU.mult)
                p.op("vector", "tensor_copy", ob[:, :], yq[:, :])
                dst_fn(ti, ob)

        with p.scope():
            def wsrc(ti):
                return wff[ti * 128:(ti + 1) * 128, :]

            def wdst(ti, ob):
                p.dma("gpsimd", wq_s[ti * 128:(ti + 1) * 128, :], ob[:, :])
            emit_quant(wsrc, wdst, DC, DM, False)

        xqbuf = p.sbuf("xqbuf", (128, DC * TPC), BF16)
        with p.scope():
            def xsrc(ti):
                kc, tb = ti // 8, ti % 8
                return None

            xw = p.sbuf("xw_t", (128, 512), F32)
            qn_tiles = []
            def xsrc2(ti):
                return None
            wt32 = p.sbuf("qn2_wt", (128, 512), F32)
            ab = p.sbuf("qn2_ab", (128, 512), F32)
            scb = p.sbuf("qn2_scb", (128, 512), F32)
            yq = p.sbuf("qn2_yq", (128, 512), F32)
            ei = p.sbuf("qn2_ei", (128, 512), I32)
            e2 = p.sbuf("qn2_e2", (128, 512), I32)
            stp = p.sbuf("qn2_stp", (128, 512), F32)
            sti = p.sbuf("qn2_sti", (128, 512), F32)
            for kc in range(DC):
                for tb in range(8):
                    sl_ = slice(kc * TPC + tb * 512, kc * TPC + (tb + 1) * 512)
                    p.op("vector", "tensor_copy", wt32[:, :], xnbuf[:, sl_])
                    p.op("scalar", "activation", ab[:, :], wt32[:, :], AF.Abs)
                    for off in (16, 8, 4, 2, 1):
                        mask = [(i ^ off) % 32 for i in range(32)]
                        p.op("vector", "stream_shuffle", scb[:, :], ab[:, :],
                             mask)
                        p.op("vector", "tensor_tensor", ab[:, :], ab[:, :],
                             scb[:, :], op=ALU.max)
                    p.op("vector", "tensor_scalar", scb[:, :], ab[:, :],
                         1.0 / 7.5, 1e-30, op0=ALU.mult, op1=ALU.max)
                    p.op("scalar", "activation", sti[:, :], scb[:, :], AF.Ln)
                    p.op("scalar", "activation", sti[:, :], sti[:, :], AF.Exp,
                         scale=-1.0)
                    p.op("vector", "tensor_tensor", yq[:, :], wt32[:, :],
                         sti[:, :], op=ALU.mult)
                    p.op("vector", "tensor_scalar", ei[:, :],
                         yq[:, :].bitcast(I32), 0x7FFFFFFF, None,
                         op0=ALU.bitwise_and)
                    p.op("vector", "tensor_scalar", ei[:, :], ei[:, :], 23,
                         None, op0=ALU.logical_shift_right)
                    p.op("vector", "tensor_scalar", ei[:, :], ei[:, :], -3.0,
                         124.0, op0=ALU.add, op1=ALU.max)
                    p.op("vector", "tensor_scalar", e2[:, :], ei[:, :], -1.0,
                         254.0, op0=ALU.mult, op1=ALU.add)
                    p.op("vector", "tensor_scalar", ei[:, :], ei[:, :], 23,
                         None, op0=ALU.logical_shift_left)
                    p.op("vector", "tensor_scalar", e2[:, :], e2[:, :], 23,
                         None, op0=ALU.logical_shift_left)
                    p.op("vector", "tensor_copy", stp[:, :],
                         ei[:, :].bitcast(F32))
                    p.op("vector", "tensor_copy", sti[:, :],
                         e2[:, :].bitcast(F32))
                    p.op("vector", "tensor_tensor", yq[:, :], yq[:, :],
                         sti[:, :], op=ALU.mult)
                    p.op("vector", "tensor_scalar", yq[:, :], yq[:, :],
                         RND_MAGIC, -RND_MAGIC, op0=ALU.add, op1=ALU.add)
                    p.op("vector", "tensor_tensor", yq[:, :], yq[:, :],
                         stp[:, :], op=ALU.mult)
                    p.op("vector", "tensor_tensor", yq[:, :], yq[:, :],
                         scb[:, :], op=ALU.mult)
                    p.op("vector", "tensor_copy", xqbuf[:, sl_], yq[:, :])

        with p.scope():
            w_sb3 = p.sbuf("w_sb3", (128, DC * DM), BF16)
            for kc in range(DC):
                p.dma("sync", w_sb3[:, kc * DM:(kc + 1) * DM],
                      wq_s[:, kc * 128:(kc + 1) * 128], transpose=True)
            bffs = p.sbuf("bffs", (128, DC), F32)
            p.dma("sync", bffs[:, :], bff[:, :])
            gsil = p.sbuf("gsil3", (128, 512), BF16)
            xf4 = p.sbuf("xf4", (128, 512), F32)
            sf4 = p.sbuf("sf4", (128, 512), BF16)

            def cb4(oc, tb, bank):
                p.op("scalar", "activation", gsil[:, :], bank[:, :], AF.Silu,
                     bias=bffs[:, oc:oc + 1])
                p.dma("sync", xf4[:, :],
                      xt[oc * 128:(oc + 1) * 128, tb * 512:(tb + 1) * 512])
                p.dma("gpsimd", sf4[:, :],
                      s3_s[oc * 128:(oc + 1) * 128, tb * 512:(tb + 1) * 512])
                p.op("vector", "tensor_tensor", gsil[:, :], gsil[:, :],
                     sf4[:, :], op=ALU.mult)
                p.op("vector", "tensor_tensor", xf4[:, :], xf4[:, :],
                     gsil[:, :], op=ALU.add)
                p.dma("gpsimd",
                      x4_o[oc * 128:(oc + 1) * 128, tb * 512:(tb + 1) * 512],
                      xf4[:, :])
            emit_proj(p, w_sb3, xqbuf, cb4, [psA, psB])

    if standalone:
        p.build(emit)
    else:
        emit(p)
    return p


def l3_inmaps(inputs, x3_shards):
    c = l1_host_consts()
    base = {
        "wqkvT": np.ascontiguousarray(inputs["L3_attn_W_qkv"].T.astype(np.float32)),
        "woT": np.ascontiguousarray(inputs["L3_attn_W_out"].T.astype(np.float32)),
        "wff": np.ascontiguousarray(inputs["L3_ffn_weight"].astype(np.float32)),
        "bff": np.ascontiguousarray(
            inputs["L3_ffn_bias"].astype(np.float32).reshape(8, 128).T),
        "bc2": None, "bc2d": c["bc2"], "eye_d": c["eye128"],
    }
    del base["bc2"]
    maps = []
    for ci in range(NCORE):
        m = dict(base)
        m["xt"] = x3_shards[ci]
        maps.append(m)
    return maps


_PROGS = {}


def _prog(name, builder):
    if name not in _PROGS:
        _PROGS[name] = builder()
    return _PROGS[name]


FUSED = True


def kernel(**inputs):
    if FUSED:
        return kernel_fused(**inputs)
    return kernel_multilaunch(**inputs)


def kernel_multilaunch(**inputs):
    import ml_dtypes
    cores = list(range(NCORE))
    x = np.asarray(inputs["L0_attn_x"], np.float32)
    shards = shard_x(x)

    pa = _prog("l0a", build_L0a)
    r0a = run_bass_kernel_spmd(pa.nc, l0a_inmaps(inputs, shards), core_ids=cores)
    logits_full = np.concatenate(
        [np.asarray(r0a.results[ci]["lg_o"]).T for ci in range(NCORE)], axis=0)

    b1t, b2t, iota8, tril = moe_host_consts(inputs)
    pb1 = _prog("l0b1", build_L0b1)
    m1 = {"lg": np.ascontiguousarray(logits_full), "iota8": iota8, "tril": tril}
    rb1 = run_bass_kernel_spmd(pb1.nc, [m1] * NCORE, core_ids=cores)
    slots = np.asarray(rb1.results[0]["slots_o"])
    wk = np.asarray(rb1.results[0]["wk_o"])

    c = host_consts()
    pb2 = _prog("l0b2", build_L0b2)
    maps = []
    for ci in range(NCORE):
        cols = [kk * 256 + ci * 32 + k for kk in range(2) for k in range(32)]
        maps.append({
            "xn_tok": np.ascontiguousarray(np.asarray(r0a.results[ci]["xn_o"]).T),
            "xt": shards[ci],
            "s0": np.asarray(r0a.results[ci]["s0_o"]),
            "w1": np.ascontiguousarray(inputs["L0_ffn_w1"].astype(np.float32)),
            "w2": np.ascontiguousarray(inputs["L0_ffn_w2"].astype(np.float32)),
            "b1t": b1t, "b2t": b2t, "eye_d2": c["eye128"],
            "myslots": np.ascontiguousarray(slots[:, cols]),
            "myw": np.ascontiguousarray(wk[:, cols]),
        })
    rb2 = run_bass_kernel_spmd(pb2.nc, maps, core_ids=cores)
    x1_shards = [np.ascontiguousarray(np.asarray(rb2.results[ci]["x1_o"]))
                 for ci in range(NCORE)]

    ppre = _prog("l1a_pre", build_L1a_pre)
    rpre = run_bass_kernel_spmd(ppre.nc, l1a_pre_inmaps(inputs, x1_shards),
                                core_ids=cores)
    cst = l1_host_consts()
    pscan = _prog("l1a_scan", build_L1a_scan)
    of_all = [np.zeros((DM, TPC), ml_dtypes.bfloat16) for _ in range(NCORE)]
    for b in range(NSEQ):
        smaps = []
        for ci in range(NCORE):
            r = rpre.results[ci]
            sl = slice(b * SEQ, (b + 1) * SEQ)
            smaps.append({
                "qf": np.ascontiguousarray(np.asarray(r["qf_o"])[:, sl]),
                "kf": np.ascontiguousarray(np.asarray(r["kf_o"])[:, sl]),
                "qt": np.ascontiguousarray(np.asarray(r["qt_o"])[sl, :]),
                "kt": np.ascontiguousarray(np.asarray(r["kt_o"])[sl, :]),
                "vt": np.ascontiguousarray(np.asarray(r["vt_o"])[sl, :]),
                "la": np.ascontiguousarray(np.asarray(r["la_o"])[:, sl]),
                "be": np.ascontiguousarray(np.asarray(r["be_o"])[:, sl]),
                "masksu": cst["masksu"], "masksl": cst["masksl"],
                "eye_d": cst["eye128"],
            })
        rs = run_bass_kernel_spmd(pscan.nc, smaps, core_ids=cores)
        for ci in range(NCORE):
            of_all[ci][:, b * SEQ:(b + 1) * SEQ] = np.asarray(
                rs.results[ci]["of_o"])
    pout = _prog("l1a_out", build_L1a_out)
    rout = run_bass_kernel_spmd(
        pout.nc, l1a_out_inmaps(inputs, [rpre.results[i] for i in range(NCORE)],
                                of_all), core_ids=cores)

    plb = _prog("l1b", build_L1b)
    xn1_all = [np.asarray(rpre.results[ci]["xn_o"]) for ci in range(NCORE)]
    s1_all = [np.asarray(rout.results[ci]["s1_o"]) for ci in range(NCORE)]
    rlb = run_bass_kernel_spmd(plb.nc, l1b_inmaps(inputs, xn1_all, x1_shards,
                                                  s1_all), core_ids=cores)
    x2_shards = [np.ascontiguousarray(np.asarray(rlb.results[ci]["x2_o"]))
                 for ci in range(NCORE)]

    p2a = _prog("l2a", build_L2a)
    r2a = run_bass_kernel_spmd(p2a.nc, l2_inmaps(inputs, x2_shards),
                               core_ids=cores)
    scores_full = np.concatenate(
        [np.asarray(r2a.results[ci]["sc_o"]) for ci in range(NCORE)], axis=1)
    p2b = _prog("l2b", build_L2b)
    r2b = run_bass_kernel_spmd(
        p2b.nc, l2b_inmaps(inputs, scores_full,
                           [r2a.results[i] for i in range(NCORE)], x2_shards),
        core_ids=cores)
    x3_shards = [np.ascontiguousarray(np.asarray(r2b.results[ci]["x3_o"]))
                 for ci in range(NCORE)]

    p3 = _prog("l3", build_L3)
    r3 = run_bass_kernel_spmd(p3.nc, l3_inmaps(inputs, x3_shards),
                              core_ids=cores)

    out = np.empty((16, SEQ, DM), np.float32)
    for ci in range(NCORE):
        x4 = np.asarray(r3.results[ci]["x4_o"])
        out[ci * NSEQ:(ci + 1) * NSEQ] = x4.T.reshape(NSEQ, SEQ, DM)
    return out


def _np2dt(a):
    import ml_dtypes
    if a.dtype == np.float32:
        return F32
    if a.dtype == ml_dtypes.bfloat16:
        return BF16
    if a.dtype == np.uint32:
        return U32
    raise ValueError(str(a.dtype))


PARANOID_BLOCKS = {"l1b"}
FUSED_BLOCKS = [
    ("l0a", "build_L0a"), ("l0b1", "build_L0b1"), ("l0b2", "build_L0b2"),
    ("l1a_pre", "build_L1a_pre"), ("scan0", "build_L1a_scan"),
    ("scan1", "build_L1a_scan"), ("l1a_out", "build_L1a_out"),
    ("l1b", "build_L1b"), ("l2a", "build_L2a"), ("l2b", "build_L2b"),
    ("l3", "build_L3"),
]


def build_fused(shapes):
    p = Prog("fused")

    def emit(p):
        nc = p.nc
        misc = emit_misc(p)
        x4out = nc.dram_tensor("x4_o", [DM, TPC], F32, kind="ExternalOutput")

        def run(prefix, fn, wired):
            p.prefix = prefix + "_"
            io = {}
            for name, (shape, dt) in shapes[prefix].items():
                t = nc.dram_tensor(prefix + "_" + name, list(shape), dt,
                                   kind="ExternalInput")
                io[name] = t.ap()
            io.update(wired)
            p.iomap = io
            p.paranoid = prefix in PARANOID_BLOCKS
            with p.scope():
                fn(p)
            p.paranoid = False
            p.prefix = ""
            p.iomap = None
            return io

        io0 = run("l0a", build_L0a, {})
        io1 = run("l0b1", build_L0b1, {"lg": io0["lgfull"]})
        io2 = run("l0b2", build_L0b2,
                  {"xn_tok": io0["xntok"], "s0": io0["s0_o"],
                   "slotsT_s": io1["slotsT_s"], "wkT_s": io1["wkT_s"]})
        iop = run("l1a_pre", build_L1a_pre, {"xt": io2["x1_o"]})
        ofint = nc.dram_tensor("of_int", [DM, TPC], BF16)
        for b in range(NSEQ):
            sl = slice(b * SEQ, (b + 1) * SEQ)
            run("scan%d" % b, build_L1a_scan,
                {"qf": iop["qf_o"][:, sl], "kf": iop["kf_o"][:, sl],
                 "qt": iop["qt_o"][sl, :], "kt": iop["kt_o"][sl, :],
                 "vt": iop["vt_o"][sl, :], "la": iop["la_o"][:, sl],
                 "be": iop["be_o"][:, sl], "of_o": ofint.ap()[:, sl]})
        ioout = run("l1a_out", build_L1a_out,
                    {"qf": iop["qf_o"], "kf": iop["kf_o"], "vf": iop["vf_o"],
                     "gf": iop["g_o"], "off": ofint.ap()})
        iolb = run("l1b", build_L1b,
                   {"xn1": iop["xn_o"], "x1": io2["x1_o"],
                    "s1": ioout["s1_o"]})
        io2a = run("l2a", build_L2a, {"xt": iolb["x2_o"]})
        io2b = run("l2b", build_L2b,
                   {"scf": io2a["scg"], "scmy": io2a["sc_o"],
                    "xn_tok": io2a["xntok2"], "xt": iolb["x2_o"],
                    "s2": io2a["s2_o"]})
        run("l3", build_L3, {"xt": io2b["x3_o"], "x4_o": x4out.ap()})

    p.build(emit, tracked=True)
    return p


def fused_inmaps(inputs):
    import ml_dtypes
    c = host_consts()
    cl = l1_host_consts()
    x = np.asarray(inputs["L0_attn_x"], np.float32)
    shards = shard_x(x)
    b1t, b2t, iota8, tril = moe_host_consts(inputs)

    per_core = [dict() for _ in range(NCORE)]
    shapes = {}

    def add(prefix, name, arrs):
        if not isinstance(arrs, list):
            arrs = [arrs] * NCORE
        shapes.setdefault(prefix, {})[name] = (arrs[0].shape, _np2dt(arrs[0]))
        for ci in range(NCORE):
            per_core[ci][prefix + "_" + name] = arrs[ci]

    m = l0a_inmaps(inputs, shards)
    for k in m[0]:
        add("l0a", k, [m[ci][k] for ci in range(NCORE)])
    add("l0a", "eyedl0", c["eye128"])
    add("l0b1", "iota8", iota8)
    add("l0b1", "tril", tril)
    add("l0b1", "eyedb1", c["eye128"])
    add("l0b2", "xt", shards)
    add("l0b2", "w1", np.ascontiguousarray(inputs["L0_ffn_w1"].astype(np.float32)))
    add("l0b2", "w2", np.ascontiguousarray(inputs["L0_ffn_w2"].astype(np.float32)))
    add("l0b2", "b1t", b1t)
    add("l0b2", "b2t", b2t)
    add("l0b2", "eye_d2", c["eye128"])
    sel = []
    for ci in range(NCORE):
        v = np.array([kk * 256 + ci * 32 + k for kk in range(2)
                      for k in range(32)], np.uint32).reshape(64, 1)
        sel.append(np.ascontiguousarray(v))
    add("l0b2", "selidx", sel)
    mp = l1a_pre_inmaps(inputs, shards)
    for k in mp[0]:
        if k == "xt":
            continue
        add("l1a_pre", k, [mp[ci][k] for ci in range(NCORE)])
    for b in range(NSEQ):
        pref = "scan%d" % b
        add(pref, "masksu", cl["masksu"])
        add(pref, "masksl", cl["masksl"])
        add(pref, "eye_d", cl["eye128"])
    dp = inputs["L1_attn_Dp"].astype(np.float32)
    add("l1a_out", "woT", np.ascontiguousarray(inputs["L1_attn_Wo"].T.astype(np.float32)))
    add("l1a_out", "dp2", np.ascontiguousarray(dp.reshape(8, 2).T))
    add("l1a_out", "onw", np.ascontiguousarray(
        np.tile(inputs["L1_attn_onorm_w"].astype(np.float32), 2).reshape(128, 1)))
    add("l1a_out", "bc2", cl["bc2"])
    mlb = l1b_inmaps(inputs, [None] * NCORE, [None] * NCORE, [None] * NCORE)
    for k in mlb[0]:
        if k in ("xn1", "x1", "s1"):
            continue
        add("l1b", k, [mlb[ci][k] for ci in range(NCORE)])
    m2 = l2_inmaps(inputs, shards)
    for k in m2[0]:
        if k == "xt":
            continue
        add("l2a", k, [m2[ci][k] for ci in range(NCORE)])
    grp8 = np.zeros((128, 8), np.float32)
    for pp in range(128):
        grp8[pp, pp // 16] = 1.0
    add("l2b", "w1T", np.ascontiguousarray(
        inputs["L2_ffn_W1"].transpose(0, 2, 1).astype(np.float32)))
    add("l2b", "w3T", np.ascontiguousarray(
        inputs["L2_ffn_W3"].transpose(0, 2, 1).astype(np.float32)))
    add("l2b", "w2T", np.ascontiguousarray(
        inputs["L2_ffn_W2"].transpose(0, 2, 1).astype(np.float32)))
    add("l2b", "grp8", grp8)
    add("l2b", "bc8", np.ascontiguousarray(grp8.T))
    add("l2b", "ecap", np.ascontiguousarray(
        (np.arange(8, dtype=np.float32) * CAPL2 - 1.0).reshape(8, 1)))
    add("l2b", "eye_d", c["eye128"])
    m3 = l3_inmaps(inputs, shards)
    for k in m3[0]:
        if k == "xt":
            continue
        add("l3", k, [m3[ci][k] for ci in range(NCORE)])
    return per_core, shapes


def kernel_fused(**inputs):
    per_core, shapes = fused_inmaps(inputs)
    p = build_fused(shapes)
    res = run_bass_kernel_spmd(p.nc, per_core, core_ids=list(range(NCORE)))
    out = np.empty((16, SEQ, DM), np.float32)
    for ci in range(NCORE):
        x4 = np.asarray(res.results[ci]["x4_o"])
        out[ci * NSEQ:(ci + 1) * NSEQ] = x4.T.reshape(NSEQ, SEQ, DM)
    return out
